# version 24
# baseline (speedup 1.0000x reference)
import numpy as np
import ml_dtypes
from contextlib import ExitStack

import concourse.bass as bass
import concourse.mybir as mybir
from concourse.bass import IndirectOffsetOnAxis
from concourse.bass_utils import run_bass_kernel_spmd

F32 = mybir.dt.float32
BF16 = mybir.dt.bfloat16
I32 = mybir.dt.int32
AF = mybir.ActivationFunctionType
ALU = mybir.AluOpType
BF16NP = ml_dtypes.bfloat16

NCORES = 8
B, T, D, FF = 2, 8192, 1024, 4096
TOK = 2048
NH, HD = 16, 64
EPS_LN = 1e-5
MS = 34.0
DIAG_OFF = 45.0
TWO_PI = 6.283185307179586

SLOPES = [2 ** (-0.5 * (h + 1)) for h in range(NH)]


def core_chunks(c):
    s = c % 4
    return c // 4, 1024 * s, 8192 - 1024 * (s + 1)


class Sem:

    _ctr = [0]
    _all = []

    def __init__(self, nc, es, name):
        Sem._ctr[0] += 1
        self.h = es.enter_context(nc.semaphore(f"{name}_{Sem._ctr[0]}"))
        self.n = 0
        Sem._all.append(self)

    def inc(self, instr, step=1):
        instr.then_inc(self.h, step)
        self.n += step
        return self.n


class FakeIns:

    __slots__ = ("incs",)

    def __init__(self):
        self.incs = []

    def then_inc(self, sem, n=1):
        self.incs.append((sem, n))
        return self


class EngQ:

    def __init__(self):
        self.calls = []

    def __getattr__(self, name):
        def m(*a, **k):
            fi = FakeIns()
            self.calls.append((name, a, k, fi))
            return fi
        return m

    def wait_ge(self, sem, val):
        self.calls.append(("wait_ge", (sem, val), {}, None))

    def replay(self, e):
        for name, a, k, fi in self.calls:
            ins = getattr(e, name)(*a, **k)
            if fi is not None:
                for sem, n in fi.incs:
                    ins.then_inc(sem, n)



A_INPUTS = [("xT", [D, 2050], "F32"), ("tmk", [128, 8], "F32"), ("tmv", [128, 8], "F32"),
            ("tmr", [128, 8], "F32"), ("WkT", [D, D], "F32"), ("WvT", [D, D], "F32"),
            ("WrT", [D, D], "F32")]


def emit_A(nc, io):
    es = ExitStack()
    xT, tmk, tmv, tmr, WkT, WvT, WrT = (io[n] for n, _, _ in A_INPUTS)
    kv_dst = io["kv_dst"]

    x_sb = es.enter_context(nc.sbuf_tensor("a_x_sb", [128, 2 * 2050], F32, side="right"))
    mix = {m: es.enter_context(nc.sbuf_tensor(f"a_mix_{m}", [128, 8 * TOK], BF16, side="right"))
           for m in "kvr"}
    tm_sb = es.enter_context(nc.sbuf_tensor("a_tm_sb", [128, 3 * 8], F32, side="right"))
    dscr = es.enter_context(nc.sbuf_tensor("a_dscr", [128, 1024], F32, side="right"))
    w_sb = es.enter_context(nc.sbuf_tensor("a_w_sb", [128, 2 * 8 * 1024], BF16, side="right"))
    stage = es.enter_context(nc.sbuf_tensor("a_stage", [128, 2 * TOK], F32, side="right"))
    ps = es.enter_context(nc.psum_tensor("a_ps", [128, 4 * 512], F32))

    ld = Sem(nc, es, "a_ld")
    ldx = [Sem(nc, es, "a_ldx0"), Sem(nc, es, "a_ldx1")]
    lw = [Sem(nc, es, "a_lw0"), Sem(nc, es, "a_lw1")]
    vm = Sem(nc, es, "a_vm")
    tsm = Sem(nc, es, "a_tsm")
    ssm = Sem(nc, es, "a_ssm")
    st = Sem(nc, es, "a_st")

    EQ = {n_: EngQ() for n_ in ("sync", "gpsimd", "vector", "scalar", "tensor")}
    SY = lambda f: f(EQ["sync"])
    GP = lambda f: f(EQ["gpsimd"])
    VE = lambda f: f(EQ["vector"])
    SC = lambda f: f(EQ["scalar"])
    TE = lambda f: f(EQ["tensor"])

    def _(sync):
        for i, tmx in enumerate((tmk, tmv, tmr)):
            ld.inc(sync.dma_start(out=tm_sb[:, i * 8:(i + 1) * 8], in_=tmx), 16)
            sync.wait_ge(ld.h, ld.n)
        for ft in range(8):
            if ft >= 2:
                sync.wait_ge(vm.h, 8 * (ft - 1))
            p = ft % 2
            ldx[p].inc(sync.dma_start(out=x_sb[:, p * 2050:(p + 1) * 2050],
                                      in_=xT[ft * 128:(ft + 1) * 128, :]), 16)
    SY(_)

    def _(g):
        for pi, W in enumerate((WkT, WvT, WrT)):
            if pi == 2:
                g.wait_ge(tsm.h, 32)
            p = pi % 2
            g.wait_ge(lw[p].h, lw[p].n)
            base = p * 8192
            for dt in range(8):
                lw[p].inc(g.dma_start(out=w_sb[:, base + dt * 1024: base + (dt + 1) * 1024],
                                      in_=W[dt * 128:(dt + 1) * 128, :]), 16)
    GP(_)

    def _(v):
        v.wait_ge(ld.h, 48)
        for ft in range(8):
            p = ft % 2
            v.wait_ge(ldx[p].h, 16 * (ft // 2 + 1))
            xb = x_sb[:, p * 2050:(p + 1) * 2050]
            for ch in range(2):
                cur = xb[:, 1 + ch * 1025: 1 + ch * 1025 + 1024]
                last = xb[:, ch * 1025: ch * 1025 + 1024]
                vm.inc(v.tensor_sub(dscr[:], cur, last))
                v.wait_ge(vm.h, vm.n)
                for i in range(3):
                    m = "kvr"[i]
                    vm.inc(v.scalar_tensor_tensor(
                        mix[m][:, ft * TOK + ch * 1024: ft * TOK + ch * 1024 + 1024],
                        dscr[:], tm_sb[:, i * 8 + ft: i * 8 + ft + 1], last,
                        ALU.mult, ALU.add))
                v.wait_ge(vm.h, vm.n)
    VE(_)

    def _(t):
        t.wait_ge(vm.h, 64)
        for pi in range(3):
            p = pi % 2
            t.wait_ge(lw[p].h, 128 if pi < 2 else 256)
            wbase = p * 8192
            for et in range(8):
                for nc4 in range(4):
                    g_ = pi * 32 + et * 4 + nc4
                    if g_ >= 4:
                        t.wait_ge(ssm.h, g_ - 3)
                    ins = None
                    for dt in range(8):
                        ins = t.matmul(
                            ps[:, (g_ % 4) * 512:(g_ % 4 + 1) * 512],
                            w_sb[:, wbase + dt * 1024 + et * 128: wbase + dt * 1024 + (et + 1) * 128],
                            mix["kvr"[pi]][:, dt * TOK + nc4 * 512: dt * TOK + (nc4 + 1) * 512],
                            start=(dt == 0), stop=(dt == 7))
                    tsm.inc(ins)
    TE(_)

    def _(s):
        def stcnt(m):
            return 32 * min(m + 1, 16) + 16 * max(0, m - 15)
        for pi in range(3):
            for et in range(8):
                ei = pi * 8 + et
                for nc4 in range(4):
                    g_ = pi * 32 + et * 4 + nc4
                    s.wait_ge(tsm.h, g_ + 1)
                    if nc4 == 0 and ei >= 2:
                        s.wait_ge(st.h, stcnt(ei - 2))
                    ssm.inc(s.activation(
                        stage[:, (ei % 2) * TOK + nc4 * 512: (ei % 2) * TOK + (nc4 + 1) * 512],
                        ps[:, (g_ % 4) * 512:(g_ % 4 + 1) * 512],
                        AF.Sigmoid if pi == 2 else AF.Copy))
    SC(_)

    def _(sync):
        for pi in range(3):
            for et in range(8):
                ei = pi * 8 + et
                sync.wait_ge(ssm.h, (ei + 1) * 4)
                if pi == 2:
                    st.inc(sync.dma_start(out=kv_dst(pi, et),
                                          in_=stage[:, (ei % 2) * TOK:(ei % 2) * TOK + TOK]), 16)
                else:
                    for half in range(2):
                        st.inc(sync.dma_start(out=io["kv_sth"](pi, et, half),
                                              in_=stage[:, (ei % 2) * TOK + half * 1024:
                                                        (ei % 2) * TOK + (half + 1) * 1024]), 16)
                sync.wait_ge(st.h, st.n)
    SY(_)

    es.close()
    return EQ, st


def build_launch_A():
    nc = bass.Bass("TRN2", target_bir_lowering=False)
    io = {}
    _declare_io(nc, io, A_INPUTS,
                [("kv_out", [2048, TOK], "F32"), ("sr_out", [D, TOK], "F32")])
    def kv_dst(pi, et):
        if pi == 2:
            return io["sr_out"][et * 128:(et + 1) * 128, :]
        j, sub = et // 2, (et % 2) * 128
        base = 512 * j + (0 if pi == 0 else 256) + sub
        return io["kv_out"][base:base + 128, :]
    io["kv_dst"] = kv_dst
    io["kv_sth"] = lambda pi, et, half: kv_dst(pi, et)[:, half * 1024:(half + 1) * 1024]
    EQ, _st = emit_A(nc, io)
    _flush(nc, EQ)
    return nc


def emit_B(nc, io):
    es = ExitStack()
    CH = 2048
    NCH = T // CH
    OPS = 10
    kv_load = io["kv_load"]
    wkv_dst = io["wkv_dst"]
    td2, tf2 = io["td2"], io["tf2"]

    kc = es.enter_context(nc.sbuf_tensor("b_kc", [128, 2 * CH], F32, side="right"))
    vc = es.enter_context(nc.sbuf_tensor("b_vc", [128, 2 * CH], F32, side="right"))
    ek = es.enter_context(nc.sbuf_tensor("b_ek", [128, CH], F32, side="right"))
    euk = es.enter_context(nc.sbuf_tensor("b_euk", [128, CH], F32, side="right"))
    da = es.enter_context(nc.sbuf_tensor("b_da", [128, CH], F32, side="right"))
    abuf = es.enter_context(nc.sbuf_tensor("b_abuf", [128, 2 * (CH + 1)], F32, side="right"))
    bbuf = es.enter_context(nc.sbuf_tensor("b_bbuf", [128, 2 * (CH + 1)], F32, side="right"))
    num = es.enter_context(nc.sbuf_tensor("b_num", [128, CH], F32, side="right"))
    den = es.enter_context(nc.sbuf_tensor("b_den", [128, CH], F32, side="right"))
    lam = es.enter_context(nc.sbuf_tensor("b_lam", [128, 2 * CH], F32, side="right"))
    wkv_sb = es.enter_context(nc.sbuf_tensor("b_wkv", [128, 2 * CH], F32, side="right"))
    vec = es.enter_context(nc.sbuf_tensor("b_vec", [128, 8], F32, side="right"))

    ldv = Sem(nc, es, "b_ldv")
    ldc = [Sem(nc, es, "b_ldc0"), Sem(nc, es, "b_ldc1")]
    vm = Sem(nc, es, "b_vm")
    sm = Sem(nc, es, "b_sm")
    st = Sem(nc, es, "b_st")

    EQ = {n_: EngQ() for n_ in ("sync", "gpsimd", "vector", "scalar", "tensor")}
    ORDER = [(i % 2, i // 2) for i in range(2 * NCH)]

    def _(sync):
        ldv.inc(sync.dma_start(out=vec[:, 0:2], in_=td2), 16)
        ldv.inc(sync.dma_start(out=vec[:, 2:4], in_=tf2), 16)
    _(EQ["sync"])

    def _loads(g, i):
        dt2, c = ORDER[i]
        p = i % 2
        if i >= 2:
            g.wait_ge(sm.h, 2 + (i - 1) * 2)
            g.wait_ge(ldc[p].h, ldc[p].n)
        for half in range(2):
            ldc[p].inc(kv_load(g, kc[:, p * CH + half * 1024: p * CH + (half + 1) * 1024],
                               0, dt2, c, half), 16)
            ldc[p].inc(kv_load(g, vc[:, p * CH + half * 1024: p * CH + (half + 1) * 1024],
                               1, dt2, c, half), 16)

    def _(s):
        s.wait_ge(ldv.h, 32)
        s.activation(vec[:, 4:6], vec[:, 0:2], AF.Exp).then_inc(sm.h, 1)
        s.wait_ge(sm.h, 1)
        s.activation(vec[:, 6:8], vec[:, 4:6], AF.Exp, scale=-1.0).then_inc(sm.h, 1)
        sm.n = 2
        for i, (dt2, c) in enumerate(ORDER):
                p = i % 2
                s.wait_ge(ldc[p].h, 64 * (i // 2 + 1))
                if i >= 1:
                    s.wait_ge(vm.h, 4 + (i - 1) * OPS + 8)
                sm.inc(s.activation(ek[:], kc[:, p * CH:(p + 1) * CH], AF.Exp))
                sm.inc(s.activation(euk[:], kc[:, p * CH:(p + 1) * CH], AF.Exp,
                                    bias=vec[:, 2 + dt2:3 + dt2]))
    _(EQ["scalar"])

    def _(v):
        v.wait_ge(sm.h, 2)
        vm.inc(v.memset(lam[:], 1.0))
        v.wait_ge(vm.h, vm.n)
        for dt2 in range(2):
            vm.inc(v.tensor_scalar(lam[:, dt2 * CH:(dt2 + 1) * CH], lam[:, dt2 * CH:(dt2 + 1) * CH],
                                   vec[:, 6 + dt2:7 + dt2], None, ALU.mult))
        v.wait_ge(vm.h, vm.n)
        vm.inc(v.memset(da[:], 0.0))
        for i, (dt2, c) in enumerate(ORDER):
                bf = (i % 2) * CH
                ab = abuf[:, dt2 * (CH + 1):(dt2 + 1) * (CH + 1)]
                bb = bbuf[:, dt2 * (CH + 1):(dt2 + 1) * (CH + 1)]
                if c == 0:
                    vm.inc(v.memset(ab[:, 0:1], 0.0))
                    vm.inc(v.memset(bb[:, 0:1], 0.0))
                else:
                    vm.inc(v.tensor_copy(ab[:, 0:1], ab[:, CH:CH + 1]))
                    vm.inc(v.tensor_copy(bb[:, 0:1], bb[:, CH:CH + 1]))
                v.wait_ge(sm.h, 2 + (i + 1) * 2)
                v.wait_ge(vm.h, vm.n)
                vm.inc(v.tensor_mul(da[:], ek[:], vc[:, bf:bf + CH]))
                v.wait_ge(vm.h, vm.n)
                vm.inc(v.tensor_tensor_scan(ab[:, 1:CH + 1], lam[:, dt2 * CH:(dt2 + 1) * CH], da[:],
                                            ab[:, 0:1], ALU.mult, ALU.add))
                vm.inc(v.tensor_tensor_scan(bb[:, 1:CH + 1], lam[:, dt2 * CH:(dt2 + 1) * CH], ek[:],
                                            bb[:, 0:1], ALU.mult, ALU.add))
                v.wait_ge(vm.h, vm.n)
                vm.inc(v.tensor_mul(num[:], euk[:], vc[:, bf:bf + CH]))
                v.wait_ge(vm.h, vm.n)
                vm.inc(v.tensor_add(num[:], num[:], ab[:, 0:CH]))
                vm.inc(v.tensor_add(den[:], euk[:], bb[:, 0:CH]))
                v.wait_ge(vm.h, vm.n)
                vm.inc(v.reciprocal(den[:], den[:]))
                v.wait_ge(vm.h, vm.n)
                if i >= 2:
                    v.wait_ge(st.h, (i - 1) * 32)
                vm.inc(v.tensor_mul(wkv_sb[:, bf:bf + CH], num[:], den[:]))
    _(EQ["vector"])

    def _(g):
        _loads(g, 0)
        _loads(g, 1)
        for i, (dt2, c) in enumerate(ORDER):
                g.wait_ge(vm.h, 4 + (i + 1) * OPS)
                for half in range(2):
                    st.inc(g.dma_start(out=wkv_dst(dt2, c, half),
                                       in_=wkv_sb[:, (i % 2) * CH + half * 1024:(i % 2) * CH + (half + 1) * 1024]), 16)
                g.wait_ge(st.h, st.n)
                if i == 3 and "wkv_ag_hook" in io:
                    io["wkv_ag_hook"](g)
                if i + 2 < 2 * NCH:
                    _loads(g, i + 2)
    _(EQ["gpsimd"])

    es.close()
    return EQ, st


def _g2src(g_):
    if g_ < 4:
        return g_, 0
    return 7 - g_, 1024


def build_launch_B():
    nc = bass.Bass("TRN2", target_bir_lowering=False)
    io = {}
    _declare_io(nc, io, [("kv_all", [2048, 2048], "F32"), ("td2", [128, 2], "F32"),
                         ("tf2", [128, 2], "F32")],
                [("wkv_out", [1024, 2048], "F32")])
    def kv_load(g, dst, kv, dt2, c, half):
        g_ = 2 * c + half
        s, lc = _g2src(g_)
        base = 512 * s + (0 if kv == 0 else 256) + dt2 * 128
        return g.dma_start(out=dst, in_=io["kv_all"][base:base + 128, lc:lc + 1024])
    def wkv_dst(dt2, c, half):
        g_ = 2 * c + half
        s, lc = _g2src(g_)
        return io["wkv_out"][256 * s + dt2 * 128: 256 * s + dt2 * 128 + 128, lc:lc + 1024]
    io["kv_load"] = kv_load
    io["wkv_dst"] = wkv_dst
    EQ, _st = emit_B(nc, io)
    _flush(nc, EQ)
    return nc


C_INPUTS = [("xT", [D, TOK], "F32"), ("sr_in", [D, TOK], "F32"), ("wkv_in", [D, TOK], "F32"),
            ("WoT", [D, D], "F32"), ("W1", [D, FF], "F32"), ("b1", [128, 32], "F32"),
            ("th", [128, 32], "F32"), ("qwr", [128, 32], "F32"), ("qwi", [128, 32], "F32"),
            ("W2", [FF, D], "F32"), ("b2", [128, 8], "F32"), ("Wq", [D, D], "F32"),
            ("Wk1", [D, D], "F32"), ("Wv1", [D, D], "F32"), ("ident_in", [128, 128], "F32")]
C_OUTPUTS = [("x2T", [D, TOK], "F32"), ("q_fm", [D, TOK], "BF16"),
             ("k_fm", [D, TOK], "BF16"), ("v_tm", [TOK, NH * 65], "BF16")]


def emit_C(nc, io):
    es = ExitStack()
    (xT, sr_in, wkv_in, WoT, W1, b1, th, qwr, qwi, W2, b2, Wq, Wk1, Wv1, ident_in) = (
        io[n] for n, _, _ in C_INPUTS)
    x2T, q_fm, k_fm, v_tm = (io[n] for n, _, _ in C_OUTPUTS)

    y_dram = nc.dram_tensor("y_dram", [D, TOK], F32, kind="Internal").ap()
    x1_dram = nc.dram_tensor("x1_dram", [D, TOK], F32, kind="Internal").ap()
    y2_dram = nc.dram_tensor("y2_dram", [D, TOK], F32, kind="Internal").ap()

    xbf = es.enter_context(nc.sbuf_tensor("c_xbf", [128, 8 * TOK], BF16))
    B1 = es.enter_context(nc.sbuf_tensor("c_B1", [128, 16 * 1040], BF16))
    B2 = es.enter_context(nc.sbuf_tensor("c_B2", [128, 16 * 1024], BF16))
    wsq = es.enter_context(nc.sbuf_tensor("c_wsq", [128, 8 * 1024], BF16))
    w1b = es.enter_context(nc.sbuf_tensor("c_w1b", [128, 2 * 8 * 512], BF16))
    w2b = es.enter_context(nc.sbuf_tensor("c_w2b", [128, 16 * 512], BF16))
    fscr = es.enter_context(nc.sbuf_tensor("c_fscr", [128, 6 * 512], F32))
    ldscr = es.enter_context(nc.sbuf_tensor("c_ldscr", [128, 4096], F32))
    iscr = es.enter_context(nc.sbuf_tensor("c_iscr", [128, 512], I32))
    sqscr = es.enter_context(nc.sbuf_tensor("c_sqscr", [128, 512], BF16))
    ybf2 = es.enter_context(nc.sbuf_tensor("c_ybf2", [128, 512], BF16))
    vcols = es.enter_context(nc.sbuf_tensor("c_vcols", [128, 320], F32))
    vscr = es.enter_context(nc.sbuf_tensor("c_vscr", [128, 192], F32))
    rows = es.enter_context(nc.sbuf_tensor("c_rows", [1, 4096], F32))
    rscr = es.enter_context(nc.sbuf_tensor("c_rscr", [1, 2 * 512], F32))
    onesb = es.enter_context(nc.sbuf_tensor("c_onesb", [128, 1], BF16))
    onesf = es.enter_context(nc.sbuf_tensor("c_onesf", [1, 128], F32))
    epsc = es.enter_context(nc.sbuf_tensor("c_epsc", [128, 1], F32))
    ident = es.enter_context(nc.sbuf_tensor("c_ident", [128, 128], BF16))

    ps = es.enter_context(nc.psum_tensor("c_ps", [128, 7 * 512], F32))
    ps_tr = es.enter_context(nc.psum_tensor("c_ps_tr", [128, 128], BF16))

    ld = Sem(nc, es, "ld")
    lw = Sem(nc, es, "lw")
    lwb = Sem(nc, es, "lwb")
    lg = Sem(nc, es, "lg")
    lw1 = [Sem(nc, es, "lw1a"), Sem(nc, es, "lw1b")]
    vm = Sem(nc, es, "vm")
    sm = Sem(nc, es, "sm")
    tm = Sem(nc, es, "tm")
    st = Sem(nc, es, "st")
    sg = Sem(nc, es, "sg")
    sg2 = Sem(nc, es, "sg2")

    ZARG, MARG, SSIN, YT, YW, SP5 = (fscr[:, i * 512:(i + 1) * 512] for i in range(6))
    B1C, THC, QWRC, QWIC = (vcols[:, i * 32:(i + 1) * 32] for i in range(4))
    B2C = vcols[:, 128:136]
    ZCC = vcols[:, 160:192]
    CPC = vcols[:, 192:224]
    CpCc = vcols[:, 256:288]
    CpSc = vcols[:, 288:320]

    def bank(i):
        return ps[:, (i % 4) * 512:(i % 4 + 1) * 512]

    SB = ps[:, 4 * 512:5 * 512]
    QB = ps[:, 5 * 512:6 * 512]
    UB = ps[:, 6 * 512:7 * 512]

    bank_free_c = {0: 0, 1: 0, 2: 0, 3: 0}
    w1_used = {0: 0, 1: 0}
    w2_used = {0: 0}
    lw1_tot = {0: 0, 1: 0}

    EQ = {n_: EngQ() for n_ in ("sync", "gpsimd", "vector", "scalar", "tensor")}
    SY = lambda f: f(EQ["sync"])
    GP = lambda f: f(EQ["gpsimd"])
    VE = lambda f: f(EQ["vector"])
    SC = lambda f: f(EQ["scalar"])
    TE = lambda f: f(EQ["tensor"])
    if True:
        def vchain(v, ins):
            vm.inc(ins)
            v.wait_ge(vm.h, vm.n)

        def schain(s, ins):
            sm.inc(ins)
            s.wait_ge(sm.h, sm.n)

        def _(sync):
            for t_, dst in ((b1, B1C), (th, THC), (qwr, QWRC), (qwi, QWIC), (b2, B2C)):
                ld.inc(sync.dma_start(out=dst, in_=t_), 16)
                sync.wait_ge(ld.h, ld.n)
        SY(_)

        def _(g):
            lg.inc(g.dma_start(out=ident[:], in_=ident_in), 16)
            g.wait_ge(lg.h, lg.n)
            for dt in range(8):
                lg.inc(g.dma_start(out=wsq[:, dt * 1024:(dt + 1) * 1024],
                                   in_=WoT[dt * 128:(dt + 1) * 128, :]), 16)
            g.wait_ge(lg.h, lg.n)
        GP(_)
        lg_wo = lg.n

        def _(v):
            vchain(v, v.memset(onesb[:], 1.0))
            vchain(v, v.memset(onesf[:], 1.0))
            vchain(v, v.memset(epsc[:], EPS_LN))
            v.wait_ge(ld.h, ld.n)
            vchain(v, v.tensor_scalar(iscr[:, 0:32], THC, 1.0 / TWO_PI, None, ALU.mult))
            vchain(v, v.tensor_copy(vscr[:, 0:32], iscr[:, 0:32]))
            vchain(v, v.scalar_tensor_tensor(vscr[:, 0:32], vscr[:, 0:32], -TWO_PI, THC,
                                             ALU.mult, ALU.add))
            vchain(v, v.tensor_scalar(vscr[:, 96:128], vscr[:, 0:32], 3.141592653589793, -TWO_PI,
                                      ALU.is_gt, ALU.mult))
            vchain(v, v.tensor_add(vscr[:, 0:32], vscr[:, 0:32], vscr[:, 96:128]))
            vchain(v, v.tensor_scalar(vscr[:, 96:128], vscr[:, 0:32], -3.141592653589793, TWO_PI,
                                      ALU.is_lt, ALU.mult))
            vchain(v, v.tensor_add(vscr[:, 0:32], vscr[:, 0:32], vscr[:, 96:128]))
            vchain(v, v.scalar_tensor_tensor(ZCC, B1C, 0.1, vscr[:, 0:32], ALU.mult, ALU.add))
        VE(_)

        def _(s):
            s.wait_ge(vm.h, vm.n)
            schain(s, s.activation(vscr[:, 32:64], vscr[:, 0:32], AF.Sin))
            schain(s, s.activation(vscr[:, 64:96], vscr[:, 0:32], AF.Sin, scale=0.5))
            schain(s, s.activation(vscr[:, 96:128], vscr[:, 64:96], AF.Square))
        SC(_)

        def _(v):
            v.wait_ge(sm.h, sm.n)
            vchain(v, v.tensor_scalar(vscr[:, 128:160], vscr[:, 96:128], -2.0, 1.0,
                                      ALU.mult, ALU.add))
            vchain(v, v.tensor_mul(vscr[:, 160:192], vscr[:, 128:160], QWRC))
            vchain(v, v.tensor_mul(vscr[:, 96:128], vscr[:, 32:64], QWIC))
            vchain(v, v.tensor_add(vscr[:, 160:192], vscr[:, 160:192], vscr[:, 96:128]))
            vchain(v, v.tensor_mul(vscr[:, 160:192], vscr[:, 160:192], QWIC))
            vchain(v, v.tensor_scalar(CPC, vscr[:, 160:192], 0.1, None, ALU.mult))
            vchain(v, v.tensor_scalar(vscr[:, 0:32], ZCC, 3.141592653589793, -TWO_PI,
                                      ALU.is_gt, ALU.mult))
            vchain(v, v.tensor_add(vscr[:, 32:64], ZCC, vscr[:, 0:32]))
            vchain(v, v.tensor_scalar(vscr[:, 0:32], vscr[:, 32:64], -3.141592653589793, TWO_PI,
                                      ALU.is_lt, ALU.mult))
            vchain(v, v.tensor_add(vscr[:, 64:96], vscr[:, 32:64], vscr[:, 0:32]))
        VE(_)

        def _(s):
            s.wait_ge(vm.h, vm.n)
            schain(s, s.activation(vscr[:, 96:128], vscr[:, 64:96], AF.Sin))
            schain(s, s.activation(vscr[:, 128:160], vscr[:, 64:96], AF.Sin, scale=0.5))
            schain(s, s.activation(vscr[:, 160:192], vscr[:, 128:160], AF.Square))
        SC(_)

        def _(v):
            v.wait_ge(sm.h, sm.n)
            vchain(v, v.tensor_scalar(vscr[:, 0:32], vscr[:, 160:192], -2.0, 1.0,
                                      ALU.mult, ALU.add))
            vchain(v, v.tensor_mul(CpCc, CPC, vscr[:, 0:32]))
            vchain(v, v.tensor_mul(CpSc, CPC, vscr[:, 96:128]))
        VE(_)

        wkv_load = io["wkv_load"]
        lwp = [lw, lwb]
        vm_rw = {}
        ldn_rw = {}
        lwn_rw = {}
        for u, (ft, th) in enumerate([(f_, t_) for t_ in range(2) for f_ in range(8)]):
                sl = u % 2

                def _(sync, ft=ft, th=th, u=u, sl=sl):
                    sync.wait_ge(vm.h, vm_rw.get(u - 2, 0))
                    sync.wait_ge(ld.h, ld.n)
                    ld.inc(sync.dma_start(out=ldscr[:, sl * 1024:(sl + 1) * 1024],
                                          in_=sr_in[ft * 128:(ft + 1) * 128,
                                                    th * 1024:(th + 1) * 1024]), 16)
                SY(_)
                ldn_rw[u] = ld.n

                def _(g, ft=ft, th=th, u=u, sl=sl):
                    g.wait_ge(vm.h, vm_rw.get(u - 2, 0))
                    g.wait_ge(lwp[sl].h, lwp[sl].n)
                    lwp[sl].inc(wkv_load(g, ldscr[:, 2048 + sl * 1024:2048 + (sl + 1) * 1024],
                                         ft, th), 16)
                GP(_)
                lwn_rw[u] = lwp[sl].n

                def _(v, ft=ft, th=th, u=u, sl=sl):
                    v.wait_ge(ld.h, ldn_rw[u])
                    v.wait_ge(lwp[sl].h, lwn_rw[u])
                    vm.inc(v.tensor_mul(B1[:, ft * 2048 + th * 1024: ft * 2048 + (th + 1) * 1024],
                                        ldscr[:, sl * 1024:(sl + 1) * 1024],
                                        ldscr[:, 2048 + sl * 1024:2048 + (sl + 1) * 1024]))
                VE(_)
                vm_rw[u] = vm.n

        tmP = {}
        ldn1 = {}
        smEv = {}
        vmAd = {}
        smSq = {}
        ldfree = {0: vm.n, 1: vm.n}
        evfree = {0: 0, 1: 0}
        sgp = [sg, sg2]
        stY = {}
        rows_done = 0
        EVAC = [YT, SP5]
        YWP = [YW, ZARG]
        S1P = [ZARG, SSIN]
        S2P = [MARG, SP5]
        s1free = {0: 0, 1: 0}
        for nc4 in range(4):
            for et in range(8):
                gi = nc4 * 8 + et
                lp = gi % 2

                def _(t, nc4=nc4, et=et, gi=gi):
                    if gi == 0:
                        t.wait_ge(vm.h, vm.n)
                        t.wait_ge(lg.h, lg_wo)
                    if gi >= 4:
                        t.wait_ge(vm.h, bank_free_c[gi % 4])
                    ins = None
                    for dt in range(8):
                        ins = t.matmul(bank(gi),
                                       wsq[:, dt * 1024 + et * 128: dt * 1024 + (et + 1) * 128],
                                       B1[:, dt * 2048 + nc4 * 512: dt * 2048 + (nc4 + 1) * 512],
                                       start=(dt == 0), stop=(dt == 7))
                    tm.inc(ins)
                TE(_)
                tmP[gi] = tm.n

                def _(sync, nc4=nc4, et=et, gi=gi, lp=lp):
                    sync.wait_ge(vm.h, ldfree[lp])
                    sync.wait_ge(ld.h, ld.n)
                    ld.inc(sync.dma_start(out=ldscr[:, lp * 512:(lp + 1) * 512],
                                          in_=xT[et * 128:(et + 1) * 128, nc4 * 512:(nc4 + 1) * 512]), 16)
                SY(_)
                ldn1[gi] = ld.n

                def _(s, gi=gi, lp=lp):
                    s.wait_ge(tm.h, tmP[gi])
                    s.wait_ge(vm.h, evfree[lp])
                    sm.inc(s.activation(EVAC[lp], bank(gi), AF.Copy))
                SC(_)
                smEv[gi] = sm.n

                def _(v, nc4=nc4, et=et, gi=gi, lp=lp):
                    v.wait_ge(sm.h, smEv[gi])
                    v.wait_ge(ld.h, ldn1[gi])
                    v.wait_ge(*stY.get(gi - 2, (sg.h, 0)))
                    vchain(v, v.tensor_add(YWP[lp], EVAC[lp], ldscr[:, lp * 512:(lp + 1) * 512]))
                    vchain(v, v.tensor_copy(xbf[:, et * TOK + nc4 * 512: et * TOK + (nc4 + 1) * 512],
                                            YWP[lp]))
                    bank_free_c[gi % 4] = vm.n
                    ldfree[lp] = vm.n
                    evfree[lp] = vm.n
                VE(_)
                vmAd[gi] = vm.n

                def _(s, nc4=nc4, et=et, gi=gi):
                    s.wait_ge(vm.h, vmAd[gi])
                    s.wait_ge(tm.h, tm.n)
                    sm.inc(s.activation(sqscr[:], xbf[:, et * TOK + nc4 * 512: et * TOK + (nc4 + 1) * 512],
                                        AF.Square))
                SC(_)
                smSq[gi] = sm.n

                def _(t, nc4=nc4, et=et, gi=gi, rd=rows_done):
                    t.wait_ge(sm.h, smSq[gi])
                    if et == 0:
                        t.wait_ge(vm.h, rd)
                    t.matmul(SB[0:1, 0:512], onesb[:],
                             xbf[:, et * TOK + nc4 * 512: et * TOK + (nc4 + 1) * 512],
                             start=(et == 0), stop=(et == 7))
                    ins = t.matmul(QB[0:1, 0:512], onesb[:], sqscr[:],
                                   start=(et == 0), stop=(et == 7))
                    tm.inc(ins)
                TE(_)

                def _(g, nc4=nc4, et=et, gi=gi, lp=lp):
                    g.wait_ge(vm.h, vmAd[gi])
                    g.wait_ge(sgp[lp].h, sgp[lp].n)
                    sgp[lp].inc(g.dma_start(out=y_dram[et * 128:(et + 1) * 128, nc4 * 512:(nc4 + 1) * 512],
                                            in_=YWP[lp]), 16)
                GP(_)
                stY[gi] = (sgp[lp].h, sgp[lp].n)

            def _(v, nc4=nc4):
                v.wait_ge(tm.h, tm.n)
                mu = rows[0:1, nc4 * 512:(nc4 + 1) * 512]
                vchain(v, v.tensor_scalar(mu, SB[0:1, 0:512], 1.0 / 1024.0, None, ALU.mult))
                vchain(v, v.tensor_scalar(rscr[0:1, 0:512], QB[0:1, 0:512], 1.0 / 1024.0, None, ALU.mult))
                vchain(v, v.tensor_mul(rscr[0:1, 512:1024], mu, mu))
                vchain(v, v.tensor_sub(rscr[0:1, 0:512], rscr[0:1, 0:512], rscr[0:1, 512:1024]))
            VE(_)

            def _(s, nc4=nc4):
                s.wait_ge(vm.h, vm.n)
                schain(s, s.activation(rscr[0:1, 512:1024], rscr[0:1, 0:512], AF.Sqrt, bias=epsc[0:1, 0:1]))
            SC(_)

            def _(v, nc4=nc4):
                v.wait_ge(sm.h, sm.n)
                vchain(v, v.reciprocal(rows[0:1, 2048 + nc4 * 512: 2048 + (nc4 + 1) * 512],
                                       rscr[0:1, 512:1024]))
            VE(_)
            rows_done = vm.n

        vmAp = {}
        ldnB = {}
        stX = {}
        p1_vm = vm.n
        p1_st = [(sg.h, sg.n), (sg2.h, sg2.n)]
        for nc4 in range(4):
            def _(t, nc4=nc4):
                t.wait_ge(vm.h, vm.n)
                t.matmul(UB[:, 0:512], onesf[:], rows[0:1, nc4 * 512:(nc4 + 1) * 512],
                         start=True, stop=True)
                ins = t.matmul(QB[:, 0:512], onesf[:], rows[0:1, 2048 + nc4 * 512: 2048 + (nc4 + 1) * 512],
                               start=True, stop=True)
                tm.inc(ins)
            TE(_)
            tmUB = tm.n

            for et in range(8):
                gi = nc4 * 8 + et
                lp = gi % 2

                def _(sync, nc4=nc4, et=et, gi=gi, lp=lp):
                    sync.wait_ge(vm.h, vmAp.get(gi - 2, p1_vm))
                    sync.wait_ge(*stY[gi])
                    sync.wait_ge(ld.h, ld.n)
                    ld.inc(sync.dma_start(out=ldscr[:, lp * 512:(lp + 1) * 512],
                                          in_=y_dram[et * 128:(et + 1) * 128, nc4 * 512:(nc4 + 1) * 512]), 16)
                SY(_)
                ldnB[gi] = ld.n

                def _(v, nc4=nc4, et=et, gi=gi, lp=lp, tmUB=tmUB):
                    v.wait_ge(ld.h, ldnB[gi])
                    v.wait_ge(tm.h, tmUB)
                    for _hh, _nn in ([stX[gi - 2]] if gi >= 2 else p1_st):
                        v.wait_ge(_hh, _nn)
                    vchain(v, v.tensor_sub(EVAC[lp], ldscr[:, lp * 512:(lp + 1) * 512], UB[:, 0:512]))
                    vchain(v, v.tensor_mul(YWP[lp], EVAC[lp], QB[:, 0:512]))
                    vchain(v, v.tensor_copy(xbf[:, et * TOK + nc4 * 512: et * TOK + (nc4 + 1) * 512],
                                            YWP[lp]))
                VE(_)
                vmAp[gi] = vm.n

                def _(g, nc4=nc4, et=et, gi=gi, lp=lp):
                    g.wait_ge(vm.h, vmAp[gi])
                    g.wait_ge(sgp[lp].h, sgp[lp].n)
                    sgp[lp].inc(g.dma_start(out=x1_dram[et * 128:(et + 1) * 128, nc4 * 512:(nc4 + 1) * 512],
                                            in_=YWP[lp]), 16)
                GP(_)
                stX[gi] = (sgp[lp].h, sgp[lp].n)
        p1b_st = [(sg.h, sg.n), (sg2.h, sg2.n)]

        for t512 in range(4):
            for fog in range(8):
                p = fog % 2

                def _(g, t512=t512, fog=fog, p=p):
                    if t512 * 8 + fog >= 2:
                        g.wait_ge(tm.h, w1_used[p])
                    lw1[p].inc(g.dma_start(
                        out=w1b[:, p * 4096:(p + 1) * 4096].rearrange("q (a n) -> q a n", a=8),
                        in_=W1.rearrange("(a q) n -> q a n", q=128)[:, :, fog * 512:(fog + 1) * 512]), 16)
                GP(_)
                lw1_tot[p] = lw1[p].n

                for fo4 in range(4):
                    fo = fog * 4 + fo4

                    def _(t, t512=t512, fo=fo, fo4=fo4, p=p, lw1_w=lw1_tot[p]):
                        t.wait_ge(lw1[p].h, lw1_w)
                        if t512 * 32 + fo >= 4:
                            t.wait_ge(vm.h, bank_free_c[fo % 4])
                        else:
                            t.wait_ge(sm.h, sm.n)
                            t.wait_ge(vm.h, vm.n)
                        ins = None
                        for dt in range(8):
                            ins = t.matmul(bank(fo),
                                           w1b[:, p * 4096 + dt * 512 + fo4 * 128: p * 4096 + dt * 512 + (fo4 + 1) * 128],
                                           xbf[:, dt * TOK + t512 * 512:(dt) * TOK + (t512 + 1) * 512],
                                           start=(dt == 0), stop=(dt == 7))
                        tm.inc(ins)
                        w1_used[p] = tm.n
                    TE(_)

                    lpq = fo % 2

                    def _(s, fo=fo, t512=t512, lpq=lpq):
                        s.wait_ge(tm.h, tm.n)
                        s.wait_ge(vm.h, s1free[lpq])
                        if t512 == 0 and fo == 0:
                            for _hh, _nn in p1b_st:
                                s.wait_ge(_hh, _nn)
                        sm.inc(s.activation(S1P[lpq], bank(fo), AF.Sin, scale=0.1))
                        sm.inc(s.activation(S2P[lpq], bank(fo), AF.Sin, scale=0.1,
                                            bias=np.float32(1.5707963267948966)))
                    SC(_)
                    smW1e = sm.n

                    def _(v, fo=fo, t512=t512, lpq=lpq, smW1e=smW1e):
                        v.wait_ge(sm.h, smW1e)
                        vchain(v, v.scalar_tensor_tensor(S1P[lpq], S1P[lpq], CpCc[:, fo:fo + 1],
                                                         bank(fo), ALU.mult, ALU.add))
                        bank_free_c[fo % 4] = vm.n
                        vchain(v, v.scalar_tensor_tensor(S2P[lpq], S2P[lpq], CpSc[:, fo:fo + 1],
                                                         S1P[lpq], ALU.mult, ALU.add))
                        vchain(v, v.tensor_scalar(B2[:, fo * 512:(fo + 1) * 512], S2P[lpq],
                                                  B1C[:, fo:fo + 1], 0.0, ALU.add, ALU.max))
                        s1free[lpq] = vm.n
                    VE(_)

            for dog in range(2):
                for fth in range(2):
                    def _(g, t512=t512, dog=dog, fth=fth):
                        if not (t512 == 0 and dog == 0 and fth == 0):
                            g.wait_ge(tm.h, w2_used[0])
                        lg.inc(g.dma_start(
                            out=w2b[:].rearrange("q (a n) -> q a n", a=16),
                            in_=W2.rearrange("(a q) n -> q a n", q=128)[:, fth * 16:(fth + 1) * 16, dog * 512:(dog + 1) * 512]), 16)
                    GP(_)
                    lg_w2 = lg.n

                    def _(t, t512=t512, dog=dog, fth=fth, lg_w2=lg_w2):
                        t.wait_ge(lg.h, lg_w2)
                        if fth == 0:
                            t.wait_ge(vm.h, vm.n)
                        ins = None
                        for ft16 in range(16):
                            ftg = fth * 16 + ft16
                            for do4 in range(4):
                                ins = t.matmul(bank(do4),
                                               w2b[:, ft16 * 512 + do4 * 128: ft16 * 512 + (do4 + 1) * 128],
                                               B2[:, ftg * 512:(ftg + 1) * 512],
                                               start=(ftg == 0), stop=(ftg == 31))
                        tm.inc(ins)
                        w2_used[0] = tm.n
                    TE(_)

                for do4 in range(4):
                    do = dog * 4 + do4

                    def _(sync, t512=t512, do=do):
                        sync.wait_ge(vm.h, vm.n)
                        sync.wait_ge(*stX[t512 * 8 + do])
                        ld.inc(sync.dma_start(out=ldscr[:, 0:512],
                                              in_=x1_dram[do * 128:(do + 1) * 128, t512 * 512:(t512 + 1) * 512]), 16)
                        sync.wait_ge(ld.h, ld.n)
                    SY(_)

                    def _(v, do=do, do4=do4, t512=t512):
                        v.wait_ge(ld.h, ld.n)
                        v.wait_ge(tm.h, tm.n)
                        if t512 == 0 and do == 0:
                            for _hh, _nn in p1b_st:
                                v.wait_ge(_hh, _nn)
                        vchain(v, v.scalar_tensor_tensor(YW, bank(do4), B2C[:, do:do + 1],
                                                         ldscr[:, 0:512], ALU.add, ALU.add))
                        vchain(v, v.tensor_copy(ybf2[:], YW))
                    VE(_)

                    def _(s, do=do):
                        s.wait_ge(vm.h, vm.n)
                        schain(s, s.activation(sqscr[:], ybf2[:], AF.Square))
                    SC(_)

                    def _(t, t512=t512, do=do):
                        t.wait_ge(sm.h, sm.n)
                        if do == 0:
                            t.wait_ge(vm.h, vm.n)
                        t.matmul(SB[0:1, 0:512], onesb[:], ybf2[:],
                                 start=(do == 0), stop=(do == 7))
                        ins = t.matmul(QB[0:1, 0:512], onesb[:], sqscr[:],
                                       start=(do == 0), stop=(do == 7))
                        tm.inc(ins)
                    TE(_)

                    def _(sync, t512=t512, do=do):
                        sync.wait_ge(vm.h, vm.n)
                        st.inc(sync.dma_start(out=y2_dram[do * 128:(do + 1) * 128, t512 * 512:(t512 + 1) * 512],
                                              in_=YW), 16)
                        sync.wait_ge(st.h, st.n)
                    SY(_)

            def _(v, t512=t512):
                v.wait_ge(tm.h, tm.n)
                mu = rows[0:1, t512 * 512:(t512 + 1) * 512]
                vchain(v, v.tensor_scalar(mu, SB[0:1, 0:512], 1.0 / 1024.0, None, ALU.mult))
                vchain(v, v.tensor_scalar(rscr[0:1, 0:512], QB[0:1, 0:512], 1.0 / 1024.0, None, ALU.mult))
                vchain(v, v.tensor_mul(rscr[0:1, 512:1024], mu, mu))
                vchain(v, v.tensor_sub(rscr[0:1, 0:512], rscr[0:1, 0:512], rscr[0:1, 512:1024]))
            VE(_)

            def _(s, t512=t512):
                s.wait_ge(vm.h, vm.n)
                schain(s, s.activation(rscr[0:1, 512:1024], rscr[0:1, 0:512], AF.Sqrt, bias=epsc[0:1, 0:1]))
            SC(_)

            def _(v, t512=t512):
                v.wait_ge(sm.h, sm.n)
                vchain(v, v.reciprocal(rows[0:1, 2048 + t512 * 512: 2048 + (t512 + 1) * 512],
                                       rscr[0:1, 512:1024]))
            VE(_)

        vmAp2 = {}
        ldnB2 = {}
        stX2 = {}
        p2_vm = vm.n
        p2_sg = [(sg.h, sg.n), (sg2.h, sg2.n)]
        for nc4 in range(4):
            def _(t, nc4=nc4):
                t.wait_ge(vm.h, vm.n)
                t.matmul(UB[:, 0:512], onesf[:], rows[0:1, nc4 * 512:(nc4 + 1) * 512],
                         start=True, stop=True)
                ins = t.matmul(QB[:, 0:512], onesf[:], rows[0:1, 2048 + nc4 * 512: 2048 + (nc4 + 1) * 512],
                               start=True, stop=True)
                tm.inc(ins)
            TE(_)
            tmUB2 = tm.n

            for et in range(8):
                gi = nc4 * 8 + et
                lp = gi % 2

                def _(sync, nc4=nc4, et=et, gi=gi, lp=lp):
                    sync.wait_ge(vm.h, vmAp2.get(gi - 2, p2_vm))
                    sync.wait_ge(ld.h, ld.n)
                    ld.inc(sync.dma_start(out=ldscr[:, lp * 512:(lp + 1) * 512],
                                          in_=y2_dram[et * 128:(et + 1) * 128, nc4 * 512:(nc4 + 1) * 512]), 16)
                SY(_)
                ldnB2[gi] = ld.n

                def _(v, nc4=nc4, et=et, gi=gi, lp=lp, tmUB2=tmUB2):
                    v.wait_ge(ld.h, ldnB2[gi])
                    v.wait_ge(tm.h, tmUB2)
                    for _hh, _nn in ([stX2[gi - 2]] if gi >= 2 else p2_sg):
                        v.wait_ge(_hh, _nn)
                    vchain(v, v.tensor_sub(EVAC[lp], ldscr[:, lp * 512:(lp + 1) * 512], UB[:, 0:512]))
                    vchain(v, v.tensor_mul(YWP[lp], EVAC[lp], QB[:, 0:512]))
                    vchain(v, v.tensor_copy(xbf[:, et * TOK + nc4 * 512: et * TOK + (nc4 + 1) * 512],
                                            YWP[lp]))
                VE(_)
                vmAp2[gi] = vm.n

                def _(g, nc4=nc4, et=et, gi=gi, lp=lp):
                    g.wait_ge(vm.h, vmAp2[gi])
                    g.wait_ge(sgp[lp].h, sgp[lp].n)
                    sgp[lp].inc(g.dma_start(out=x2T[et * 128:(et + 1) * 128, nc4 * 512:(nc4 + 1) * 512],
                                            in_=YWP[lp]), 16)
                GP(_)
                stX2[gi] = (sgp[lp].h, sgp[lp].n)

        def _(v):
            v.wait_ge(vm.h, vm.n)
            ap = B1[:].rearrange("q (t h c) -> q t h c", t=16, h=16, c=65)
            vchain(v, v.memset(ap[:, :, :, 64:65], 1.0))
        VE(_)

        for pi, W in enumerate((Wq, Wk1, Wv1)):
            def _(g, pi=pi, W=W):
                g.wait_ge(tm.h, tm.n)
                for dt in range(8):
                    lg.inc(g.dma_start(out=wsq[:, dt * 1024:(dt + 1) * 1024],
                                       in_=W[dt * 128:(dt + 1) * 128, :]), 16)
            GP(_)
            lg_w = lg.n

            for et in range(8):
                for nc4 in range(4):
                    gi = et * 4 + nc4

                    def _(t, pi=pi, et=et, nc4=nc4, gi=gi, lg_w=lg_w):
                        t.wait_ge(lg.h, lg_w)
                        t.wait_ge(sm.h, sm.n)
                        t.wait_ge(vm.h, vm.n)
                        ins = None
                        for dt in range(8):
                            ins = t.matmul(bank(gi),
                                           wsq[:, dt * 1024 + et * 128: dt * 1024 + (et + 1) * 128],
                                           xbf[:, dt * TOK + nc4 * 512: dt * TOK + (nc4 + 1) * 512],
                                           start=(dt == 0), stop=(dt == 7))
                        tm.inc(ins)
                    TE(_)

                    if pi < 2:
                        def _(s, pi=pi, et=et, nc4=nc4, gi=gi):
                            s.wait_ge(tm.h, tm.n)
                            if nc4 == 0:
                                s.wait_ge(st.h, st.n)
                            schain(s, s.activation(
                                B2[:, (et % 2) * 2048 + nc4 * 512:(et % 2) * 2048 + (nc4 + 1) * 512],
                                bank(gi), AF.Copy))
                        SC(_)

                        if nc4 == 3:
                            def _(sync, pi=pi, et=et):
                                sync.wait_ge(sm.h, sm.n)
                                dst = (q_fm[et * 128:(et + 1) * 128, :] if pi == 0
                                       else io["kq_st"](et))
                                st.inc(sync.dma_start(out=dst,
                                                      in_=B2[:, (et % 2) * 2048:(et % 2) * 2048 + 2048]), 16)
                                sync.wait_ge(st.h, st.n)
                            SY(_)
                    else:
                        def _(s, et=et, nc4=nc4, gi=gi):
                            s.wait_ge(tm.h, tm.n)
                            schain(s, s.activation(B2[:, nc4 * 512:(nc4 + 1) * 512], bank(gi), AF.Copy))
                        SC(_)

                        for tt in range(4):
                            def _(t, et=et, nc4=nc4, tt=tt):
                                t.wait_ge(sm.h, sm.n)
                                t.wait_ge(vm.h, vm.n)
                                ins = t.transpose(ps_tr[:, 0:128],
                                                  B2[:, nc4 * 512 + tt * 128: nc4 * 512 + (tt + 1) * 128],
                                                  ident[:])
                                tm.inc(ins)
                            TE(_)

                            def _(v, et=et, nc4=nc4, tt=tt):
                                v.wait_ge(tm.h, tm.n)
                                blk = nc4 * 4 + tt
                                vchain(v, v.tensor_copy(
                                    B1[:, blk * 1040 + (2 * et) * 65: blk * 1040 + (2 * et) * 65 + 64],
                                    ps_tr[:, 0:64]))
                                vchain(v, v.tensor_copy(
                                    B1[:, blk * 1040 + (2 * et + 1) * 65: blk * 1040 + (2 * et + 1) * 65 + 64],
                                    ps_tr[:, 64:128]))
                            VE(_)

        for blk in range(16):
            def _(g, blk=blk):
                g.wait_ge(vm.h, vm.n)
                for half in range(2):
                    sg.inc(g.dma_start(out=io["vt_st"](blk, half),
                                       in_=B1[:, blk * 1040 + half * 520: blk * 1040 + (half + 1) * 520]), 16)
                g.wait_ge(sg.h, sg.n)
            GP(_)

    es.close()
    return EQ


def _flush(nc, EQs):
    if not isinstance(EQs, list):
        EQs = [EQs]
    with nc.Block() as block:
        block.sync(lambda e: [q["sync"].replay(e) for q in EQs])
        block.gpsimd(lambda e: [q["gpsimd"].replay(e) for q in EQs])
        block.vector(lambda e: [q["vector"].replay(e) for q in EQs])
        block.scalar(lambda e: [q["scalar"].replay(e) for q in EQs])
        block.tensor(lambda e: [q["tensor"].replay(e) for q in EQs])


def _declare_io(nc, io, inputs, outputs):
    DT = {"F32": F32, "BF16": BF16, "I32": I32}
    for n, shp, dt in inputs:
        io[n] = nc.dram_tensor(n, shp, DT[dt], kind="ExternalInput").ap()
    for n, shp, dt in outputs:
        io[n] = nc.dram_tensor(n, shp, DT[dt], kind="ExternalOutput").ap()


def build_launch_C():
    nc = bass.Bass("TRN2", target_bir_lowering=False)
    io = {}
    _declare_io(nc, io, C_INPUTS, C_OUTPUTS)
    io["wkv_load"] = lambda g, dst, ft, th: g.dma_start(
        out=dst, in_=io["wkv_in"][ft * 128:(ft + 1) * 128, th * 1024:(th + 1) * 1024])
    io["kq_st"] = lambda et: io["k_fm"][et * 128:(et + 1) * 128, :]
    io["vt_st"] = lambda blk, half: io["v_tm"][blk * 128:(blk + 1) * 128,
                                               half * 520:(half + 1) * 520]
    EQ = emit_C(nc, io)
    _flush(nc, EQ)
    return nc


SWEEPS = [3584, 4096, 7680, 8192]
QGOFF = [0, 28, 60, 120]
NSLOT = 184


def qg_positions(core):
    b, h0, t0 = core_chunks(core)
    return sorted([h0, h0 + 512, t0, t0 + 512])


D_INPUTS = [("q_in", [D, TOK], "BF16"), ("kdiag", [D, TOK], "BF16"),
            ("vdiag", [TOK, NH * 65], "BF16"), ("bias_tbl", [128, NH * NSLOT], "F32"),
            ("dcol_in", [128, NH], "F32"), ("fvb_in", [128, NH], "F32"),
            ("x2T_in", [D, TOK], "F32"), ("Wo1", [D, D], "F32"),
            ("w1T", [128, 8 * FF], "F32"), ("w2T", [128, 8 * FF], "F32"),
            ("h1T", [128, 32 * FF], "F32"), ("h2T", [128, 32 * FF], "F32"),
            ("w3T", [128, 8 * FF], "F32"),
            ("s1c_in", [128, 32], "F32"), ("s2c_in", [128, 32], "F32"),
            ("sh1_in", [128, 32], "F32"), ("sh2_in", [128, 32], "F32"),
            ("s3c_in", [128, 8], "F32")]
D_OUTPUTS = [("x4T", [D, TOK], "F32")]


def emit_D(nc, io):
    es = ExitStack()
    (q_in, kdiag, vdiag, bias_tbl, dcol_in, fvb_in, x2T_in, Wo1, w1T, w2T,
     h1T, h2T, w3T, s1c_in, s2c_in, sh1_in, sh2_in, s3c_in) = (io[n] for n, _, _ in D_INPUTS)
    x4T = io["x4T"]
    kf_ap = io["kf_ap"]
    vt_ap = io["vt_ap"]

    y3_dram = nc.dram_tensor("y3_dram", [D, TOK], F32, kind="Internal").ap()
    x3bf_dram = nc.dram_tensor("x3bf_dram", [D, TOK], BF16, kind="Internal").ap()
    x3_dram = nc.dram_tensor("x3_dram", [D, TOK], F32, kind="Internal").ap()
    y4_dram = nc.dram_tensor("y4_dram", [D, TOK], F32, kind="Internal").ap()

    xbfp = es.enter_context(nc.sbuf_tensor("d_xbfp", [128, 2 * 512], BF16))
    fscr = es.enter_context(nc.sbuf_tensor("d_fscr", [128, 4 * 512], F32))
    ldscr = es.enter_context(nc.sbuf_tensor("d_ldscr", [128, 2 * 512], F32))
    sqscr = es.enter_context(nc.sbuf_tensor("d_sqscr", [128, 2 * 512], BF16))
    ybf2 = es.enter_context(nc.sbuf_tensor("d_ybf2", [128, 2 * 512], BF16))
    rscr = es.enter_context(nc.sbuf_tensor("d_rscr", [1, 2 * 512], F32))
    onesb = es.enter_context(nc.sbuf_tensor("d_onesb", [128, 1], BF16))
    onesf = es.enter_context(nc.sbuf_tensor("d_onesf", [1, 128], F32))
    epsc = es.enter_context(nc.sbuf_tensor("d_epsc", [128, 1], F32))
    scols = es.enter_context(nc.sbuf_tensor("d_scols", [128, 136], F32))

    ps = es.enter_context(nc.psum_tensor("d_ps", [128, 8 * 512], F32))

    ld = Sem(nc, es, "ld")
    lg = Sem(nc, es, "lg")
    lk = [Sem(nc, es, "lka"), Sem(nc, es, "lkb")]
    lv = [Sem(nc, es, "lva"), Sem(nc, es, "lvb")]
    lw = [Sem(nc, es, "lwa"), Sem(nc, es, "lwb")]
    vm = Sem(nc, es, "vm")
    sm = Sem(nc, es, "sm")
    tm = Sem(nc, es, "tm")
    st = Sem(nc, es, "st")
    gm = Sem(nc, es, "gm")
    sg = Sem(nc, es, "d_sg")
    sg2 = Sem(nc, es, "d_sg2")
    st2 = Sem(nc, es, "d_st2")
    sh = Sem(nc, es, "d_sh")
    sh2 = Sem(nc, es, "d_sh2")

    S1C = scols[:, 0:32]
    S2C = scols[:, 32:64]
    SH1C = scols[:, 64:96]
    SH2C = scols[:, 96:128]
    S3C = scols[:, 128:136]

    YT, YW = fscr[:, 0:512], fscr[:, 512:1024]
    YT2, YW2 = fscr[:, 1024:1536], fscr[:, 1536:2048]

    def bank(i):
        return ps[:, (i % 4) * 512:(i % 4 + 1) * 512]

    SCB = [ps[:, (4 + i) * 512:(5 + i) * 512] for i in range(4)]
    SB = ps[:, 6 * 512:7 * 512]
    QB = ps[:, 7 * 512:8 * 512]

    es2 = ExitStack()
    es3 = ExitStack()
    att = es3.enter_context(nc.sbuf_tensor("d_att", [128, 8 * TOK], BF16, side="right"))
    q_sb = es2.enter_context(nc.sbuf_tensor("d_q_sb", [128, 2 * 2 * TOK], BF16, side="right"))
    fv = es2.enter_context(nc.sbuf_tensor("d_fv", [128, 16 * 512], BF16, side="right"))
    gt = es2.enter_context(nc.sbuf_tensor("d_gt", [128, 4 * 512], F32, side="right"))
    msk = es2.enter_context(nc.sbuf_tensor("d_msk", [128, 512], F32, side="right"))
    dti = es2.enter_context(nc.sbuf_tensor("d_dti", [128, 5 * 512], F32, side="right"))
    iscr2 = es2.enter_context(nc.sbuf_tensor("d_iscr2", [128, 512], I32, side="right"))
    btab = es2.enter_context(nc.sbuf_tensor("d_btab", [128, 2 * 4 * NSLOT], F32, side="right"))
    dcol = es2.enter_context(nc.sbuf_tensor("d_dcol", [128, NH], F32, side="right"))
    fvb = es2.enter_context(nc.sbuf_tensor("d_fvb", [128, NH], F32, side="right"))
    ksb = es2.enter_context(nc.sbuf_tensor("d_ksb", [128, 2 * 2 * 512], BF16, side="right"))
    vsb = es2.enter_context(nc.sbuf_tensor("d_vsb", [128, 2 * 4 * 260], BF16, side="right"))
    kdsb = es2.enter_context(nc.sbuf_tensor("d_kdsb", [128, 2 * 2 * TOK], BF16, side="right"))
    vdsb = es2.enter_context(nc.sbuf_tensor("d_vdsb", [128, 2 * 16 * 260], BF16, side="right"))
    psb = es2.enter_context(nc.sbuf_tensor("d_psb", [128, 4 * 512], BF16, side="right"))
    pdg = es2.enter_context(nc.sbuf_tensor("d_pdg", [128, 2 * 512], BF16, side="right"))
    praw = es2.enter_context(nc.sbuf_tensor("d_praw", [128, 2 * 512], F32, side="right"))
    blw = es2.enter_context(nc.sbuf_tensor("d_blw", [128, 4 * 512], BF16, side="right"))
    mrg4 = es2.enter_context(nc.sbuf_tensor("d_mrg4", [128, 4 * 512], F32, side="right"))

    kv_used = {0: 0, 1: 0}
    kv_seen = {0: False, 1: False}
    score_done = {}
    pv_done = {0: 0, 1: 0, 2: 0, 3: 0}
    praw_free = {0: 0, 1: 0}
    pdg_free = {0: 0, 1: 0}
    pvh = {}
    rscr_free = {0: 0, 1: 0}
    hgtm = {0: 0, 1: 0}
    hgsm = {0: 0, 1: 0}
    bank_free = {0: 0, 1: 0, 2: 0, 3: 0}
    sci_ctr = [0]

    for _i in range(4):
        score_done[_i] = (sm.h, 0)

    EQ = {n_: EngQ() for n_ in ("sync", "gpsimd", "vector", "scalar", "tensor")}
    SY = lambda f: f(EQ["sync"])
    GP = lambda f: f(EQ["gpsimd"])
    VE = lambda f: f(EQ["vector"])
    SC = lambda f: f(EQ["scalar"])
    TE = lambda f: f(EQ["tensor"])
    if True:
        def vchain(v, ins):
            vm.inc(ins)
            v.wait_ge(vm.h, vm.n)

        def schain(s, ins):
            sm.inc(ins)
            s.wait_ge(sm.h, sm.n)

        def _(sync):
            for t_, dst in ((dcol_in, dcol[:]), (fvb_in, fvb[:]),
                            (s1c_in, S1C), (s2c_in, S2C), (sh1_in, SH1C),
                            (sh2_in, SH2C), (s3c_in, S3C)):
                ld.inc(sync.dma_start(out=dst, in_=t_), 16)
                sync.wait_ge(ld.h, ld.n)
        SY(_)

        for toff in range(5):
            def _(g, toff=toff):
                g.wait_ge(vm.h, vm.n)
                if toff < 4:
                    gm.inc(g.iota(iscr2[:], pattern=[[-1, 512]], base=128 * toff + 127,
                                  channel_multiplier=1))
                else:
                    gm.inc(g.iota(iscr2[:], pattern=[[1, 512]], base=0,
                                  channel_multiplier=0))
            GP(_)

            def _(v, toff=toff):
                v.wait_ge(gm.h, gm.n)
                vchain(v, v.tensor_copy(dti[:, toff * 512:(toff + 1) * 512], iscr2[:]))
            VE(_)

        def _(v):
            vchain(v, v.memset(onesb[:], 1.0))
            vchain(v, v.memset(onesf[:], 1.0))
            vchain(v, v.memset(epsc[:], EPS_LN))
        VE(_)

        for hg in range(4):
            heads = [4 * hg + i for i in range(4)]

            for i4, h in enumerate(heads):
                def _(v, i4=i4, h=h):
                    v.wait_ge(ld.h, ld.n)
                    for toff in range(4):
                        fslc = fv[:, (i4 * 4 + toff) * 512:(i4 * 4 + toff + 1) * 512]
                        vchain(v, v.tensor_scalar(fslc, dti[:, toff * 512:(toff + 1) * 512],
                                                  float(np.float32(SLOPES[h])), fvb[:, h:h + 1],
                                                  ALU.mult, ALU.add))
                        vchain(v, v.tensor_scalar(fslc, fslc, 50.0, None, ALU.min))
                VE(_)

                def _(s, i4=i4, h=h):
                    s.wait_ge(vm.h, vm.n)
                    for toff in range(4):
                        fslc = fv[:, (i4 * 4 + toff) * 512:(i4 * 4 + toff + 1) * 512]
                        sm.inc(s.activation(fslc, fslc, AF.Exp))
                    schain(s, s.activation(gt[:, i4 * 512:(i4 + 1) * 512],
                                           dti[:, 4 * 512:5 * 512], AF.Exp,
                                           scale=float(np.float32(-SLOPES[h]))))
                SC(_)

                def _(v, i4=i4):
                    v.wait_ge(sm.h, sm.n)
                    for toff in range(4):
                        fslc = fv[:, (i4 * 4 + toff) * 512:(i4 * 4 + toff + 1) * 512]
                        vchain(v, v.tensor_scalar(msk[:], dti[:, toff * 512:(toff + 1) * 512],
                                                  127.5, None, ALU.is_lt))
                        vchain(v, v.tensor_mul(fslc, fslc, msk[:]))
                VE(_)

            par = hg % 2

            def _(g, hg=hg, par=par):
                g.wait_ge(tm.h, hgtm[par])
                g.wait_ge(sm.h, hgsm[par])
                g.wait_ge(lg.h, lg.n)
                for ct in range(2):
                    lg.inc(g.dma_start(out=kdsb[:, par * 2 * TOK + ct * TOK: par * 2 * TOK + (ct + 1) * TOK],
                                       in_=io["kdiag_ap"](hg, ct)), 16)
                    lg.inc(g.dma_start(out=q_sb[:, par * 2 * TOK + ct * TOK: par * 2 * TOK + (ct + 1) * TOK],
                                       in_=q_in[(2 * hg + ct) * 128:(2 * hg + ct + 1) * 128, :]), 16)
                lg.inc(g.dma_start(out=vdsb[:, par * 4160:(par + 1) * 4160].rearrange("p (a n) -> p a n", a=16),
                                   in_=io["vdiag_ap"](hg)), 16)
                lg.inc(g.dma_start(out=btab[:, par * 4 * NSLOT:(par + 1) * 4 * NSLOT],
                                   in_=bias_tbl[:, 4 * hg * NSLOT:(4 * hg + 4) * NSLOT]), 16)
            GP(_)
            lg_diag = lg.n

            for qg in range(4):
                nkv = SWEEPS[qg] // 512

                units = [(kc, sub, pr) for kc in range(nkv)
                         for sub in range(4) for pr in range(2)]
                NB = len(units)
                ubank = []
                for _u in range(NB):
                    sci_ctr[0] += 2
                    ubank.append((sci_ctr[0] % 4, (sci_ctr[0] + 1) % 4))
                kcln = {}
                tmS = {}
                smE = {}

                def emit_loads(kc, hg=hg, qg=qg):
                    pk = kc % 2

                    def _(sync, hg=hg, qg=qg, kc=kc, pk=pk):
                        if hg == 2 and qg == 0 and kc == 0 and "agw" in io:
                            io["agw"](sync)
                        if kv_seen[pk]:
                            sync.wait_ge(tm.h, kv_used[pk])
                            sync.wait_ge(lk[pk].h, lk[pk].n)
                            sync.wait_ge(lv[pk].h, lv[pk].n)
                        kv_seen[pk] = True
                        for ct in range(2):
                            lk[pk].inc(sync.dma_start(
                                out=ksb[:, (pk * 2 + ct) * 512:(pk * 2 + ct + 1) * 512],
                                in_=kf_ap((2 * hg + ct) * 128, kc)), 16)
                        lv[pk].inc(sync.dma_start(
                            out=vsb[:, pk * 1040:(pk + 1) * 1040].rearrange("p (a n) -> p a n", a=4),
                            in_=vt_ap(kc, hg)), 16)
                    SY(_)
                    kcln[kc] = (lk[pk].n, lv[pk].n)

                def emit_S(n, qg=qg):
                    kc, sub, pr = units[n]
                    pk = kc % 2
                    scA, scB = ubank[n]
                    lk_n, lv_n = kcln[kc]

                    def _(t, qg=qg, sub=sub, pr=pr, scA=scA, scB=scB, pk=pk,
                          lk_n=lk_n, lv_n=lv_n, lg_diag=lg_diag):
                        t.wait_ge(lk[pk].h, lk_n)
                        t.wait_ge(lv[pk].h, lv_n)
                        t.wait_ge(lg.h, lg_diag)
                        kt = ksb[:, (pk * 2 + pr) * 512 + sub * 128:
                                 (pk * 2 + pr) * 512 + (sub + 1) * 128]
                        qt = q_sb[:, par * 2 * TOK + pr * TOK + qg * 512:
                                  par * 2 * TOK + pr * TOK + (qg + 1) * 512]
                        t.wait_ge(*score_done[scA])
                        t.matmul(SCB[scA][:, 0:512], kt[0:64, :], qt[0:64, :],
                                 start=True, stop=True)
                        t.wait_ge(*score_done[scB])
                        ins = t.matmul(SCB[scB][:, 0:512], kt[64:128, :], qt[64:128, :],
                                       start=True, stop=True)
                        tm.inc(ins)
                        kv_used[pk] = tm.n
                    TE(_)
                    tmS[n] = tm.n

                def emit_exp(n, qg=qg, hg=hg):
                    kc, sub, pr = units[n]
                    slot = kc * 4 + sub
                    scA, scB = ubank[n]

                    def _(s, qg=qg, slot=slot, pr=pr, scA=scA, scB=scB, n=n):
                        s.wait_ge(tm.h, tmS[n])
                        for ih, sc in ((2 * pr, scA), (2 * pr + 1, scB)):
                            s.wait_ge(tm.h, pv_done[sc])
                            col = btab[:, par * 4 * NSLOT + ih * NSLOT + QGOFF[qg] + slot:
                                       par * 4 * NSLOT + ih * NSLOT + QGOFF[qg] + slot + 1]
                            sm.inc(s.activation(psb[:, sc * 512:(sc + 1) * 512],
                                                SCB[sc][:, 0:512], AF.Exp,
                                                scale=0.125, bias=col))
                            score_done[sc] = (sm.h, sm.n)
                    SC(_)
                    smE[n] = sm.n

                def emit_PV(n, nkv=nkv):
                    kc, sub, pr = units[n]
                    slot = kc * 4 + sub
                    pk = kc % 2
                    last = (slot == nkv * 4 - 1)
                    scA, scB = ubank[n]

                    def _(t, slot=slot, pr=pr, scA=scA, scB=scB, pk=pk, last=last, n=n):
                        t.wait_ge(sm.h, smE[n])
                        for ih, sc in ((2 * pr, scA), (2 * pr + 1, scB)):
                            ins = t.matmul(bank(ih)[0:65, :],
                                           vsb[:, pk * 1040 + (slot % 4) * 260 + ih * 65:
                                               pk * 1040 + (slot % 4) * 260 + ih * 65 + 65],
                                           psb[:, sc * 512:(sc + 1) * 512],
                                           start=(slot == 0), stop=last)
                            tm.inc(ins)
                            pv_done[sc] = tm.n
                        kv_used[pk] = tm.n
                    TE(_)

                emit_loads(0)
                emit_S(0)
                for n in range(NB):
                    emit_exp(n)
                    if n + 1 < NB:
                        if units[n + 1][0] != units[n][0]:
                            emit_loads(units[n + 1][0])
                        emit_S(n + 1)
                    emit_PV(n)

                for i4 in range(4):
                    def _(v, i4=i4):
                        v.wait_ge(tm.h, tm.n)
                        vchain(v, v.tensor_mul(blw[:, i4 * 512:(i4 + 1) * 512][0:65, :],
                                               bank(i4)[0:65, :],
                                               gt[:, i4 * 512:(i4 + 1) * 512][0:65, :]))
                    VE(_)

                dunits = [(toff, pr) for toff in range(4) for pr in range(2)]
                dbank = []
                for _u in range(8):
                    sci_ctr[0] += 2
                    dbank.append((sci_ctr[0] % 4, (sci_ctr[0] + 1) % 4))
                tmS2 = {}
                smE2 = {}
                vmD = {}

                def emit_S2(u, qg=qg, lg_diag=lg_diag):
                    toff, pr = dunits[u]
                    scA, scB = dbank[u]

                    def _(t, qg=qg, toff=toff, pr=pr, scA=scA, scB=scB, lg_diag=lg_diag, u=u):
                        t.wait_ge(lg.h, lg_diag)
                        if u == 0:
                            t.wait_ge(vm.h, vm.n)
                        kt = kdsb[:, par * 2 * TOK + pr * TOK + qg * 512 + toff * 128:
                                  par * 2 * TOK + pr * TOK + qg * 512 + (toff + 1) * 128]
                        qt = q_sb[:, par * 2 * TOK + pr * TOK + qg * 512:
                                  par * 2 * TOK + pr * TOK + (qg + 1) * 512]
                        t.wait_ge(*score_done[scA])
                        t.matmul(SCB[scA][:, 0:512], kt[0:64, :], qt[0:64, :],
                                 start=True, stop=True)
                        t.wait_ge(*score_done[scB])
                        ins = t.matmul(SCB[scB][:, 0:512], kt[64:128, :], qt[64:128, :],
                                       start=True, stop=True)
                        tm.inc(ins)
                    TE(_)
                    tmS2[u] = tm.n

                def emit_dexp(u, j2, hg=hg):
                    toff, pr = dunits[u]
                    ih = 2 * pr + j2
                    h = 4 * hg + ih
                    sc_i = dbank[u][j2]
                    pr_i = (toff * 4 + ih) % 2

                    def _(s, h=h, sc_i=sc_i, pr_i=pr_i, u=u):
                        s.wait_ge(tm.h, tmS2[u])
                        s.wait_ge(vm.h, praw_free[pr_i])
                        sm.inc(s.activation(praw[:, pr_i * 512:(pr_i + 1) * 512],
                                            SCB[sc_i][:, 0:512], AF.Exp,
                                            scale=0.125, bias=dcol[:, h:h + 1]))
                        score_done[sc_i] = (sm.h, sm.n)
                    SC(_)
                    smE2[(u, j2)] = sm.n

                def emit_dmul(u, j2):
                    toff, pr = dunits[u]
                    ih = 2 * pr + j2
                    pr_i = (toff * 4 + ih) % 2

                    def _(v, ih=ih, toff=toff, pr_i=pr_i, u=u, j2=j2):
                        v.wait_ge(sm.h, smE2[(u, j2)])
                        v.wait_ge(tm.h, pdg_free[pr_i])
                        vchain(v, v.tensor_mul(pdg[:, pr_i * 512:(pr_i + 1) * 512],
                                               praw[:, pr_i * 512:(pr_i + 1) * 512],
                                               fv[:, (ih * 4 + toff) * 512:(ih * 4 + toff + 1) * 512]))
                        praw_free[pr_i] = vm.n
                    VE(_)
                    vmD[(u, j2)] = vm.n

                def emit_dPV(u, j2, qg=qg):
                    toff, pr = dunits[u]
                    ih = 2 * pr + j2
                    pr_i = (toff * 4 + ih) % 2

                    def _(t, qg=qg, toff=toff, ih=ih, pr_i=pr_i, u=u, j2=j2):
                        t.wait_ge(vm.h, vmD[(u, j2)])
                        blk = qg * 4 + toff
                        ins = t.matmul(bank(ih)[0:65, :],
                                       vdsb[:, par * 4160 + blk * 260 + ih * 65:
                                           par * 4160 + blk * 260 + ih * 65 + 65],
                                       pdg[:, pr_i * 512:(pr_i + 1) * 512],
                                       start=(toff == 0), stop=(toff == 3))
                        tm.inc(ins)
                        pdg_free[pr_i] = tm.n
                    TE(_)
                    if toff == 3:
                        pvh[ih] = tm.n

                emit_S2(0)
                for u in range(8):
                    emit_dexp(u, 0)
                    emit_dmul(u, 0)
                    if u + 1 < 8:
                        emit_S2(u + 1)
                    emit_dPV(u, 0)
                    emit_dexp(u, 1)
                    emit_dmul(u, 1)
                    emit_dPV(u, 1)

                tmPV_all = tm.n
                vmM = {}
                smR = {}
                tmB = {}
                QBs = [SB, QB]
                for i4 in range(4):
                    h = heads[i4]
                    mpar = i4 % 2

                    def _(v, i4=i4, tmPV_all=tmPV_all):
                        v.wait_ge(tm.h, pvh[i4])
                        vchain(v, v.tensor_add(mrg4[:, i4 * 512:(i4 + 1) * 512][0:65, :],
                                               blw[:, i4 * 512:(i4 + 1) * 512][0:65, :],
                                               bank(i4)[0:65, :]))
                    VE(_)
                    vmM[i4] = vm.n

                    def _(v, i4=i4, mpar=mpar):
                        v.wait_ge(tm.h, rscr_free[mpar])
                        vchain(v, v.reciprocal(rscr[0:1, mpar * 512:(mpar + 1) * 512],
                                               mrg4[64:65, i4 * 512:(i4 + 1) * 512]))
                    VE(_)
                    smR[i4] = vm.n

                    def _(t, i4=i4, mpar=mpar):
                        t.wait_ge(vm.h, smR[i4])
                        t.wait_ge(*score_done[2 + mpar])
                        ins = t.matmul(QBs[mpar][0:64, :], onesf[0:1, 0:64],
                                       rscr[0:1, mpar * 512:(mpar + 1) * 512],
                                       start=True, stop=True)
                        tm.inc(ins)
                        rscr_free[mpar] = tm.n
                    TE(_)
                    tmB[i4] = tm.n

                    def _(v, i4=i4, h=h, qg=qg, mpar=mpar):
                        v.wait_ge(tm.h, tmB[i4])
                        at = att[:, (h // 2) * TOK + qg * 512:(h // 2) * TOK + (qg + 1) * 512]
                        vchain(v, v.tensor_mul(at[(h % 2) * 64:(h % 2) * 64 + 64, :],
                                               mrg4[0:64, i4 * 512:(i4 + 1) * 512],
                                               QBs[mpar][0:64, :]))
                        score_done[2 + mpar] = (vm.h, vm.n)
                    VE(_)

            hgtm[par] = tm.n
            hgsm[par] = sm.n

        es2.close()
        rows = es.enter_context(nc.sbuf_tensor("d_rows", [1, 4096], F32))
        wo_sb = es3.enter_context(nc.sbuf_tensor("d_wo_sb", [128, 8 * 1024], BF16, side="right"))

        def _(g):
            g.wait_ge(tm.h, tm.n)
            g.wait_ge(vm.h, vm.n)
            for dt in range(8):
                lg.inc(g.dma_start(out=wo_sb[:, dt * 1024:(dt + 1) * 1024],
                                   in_=Wo1[dt * 128:(dt + 1) * 128, :]), 16)
        GP(_)
        lg_wo = lg.n

        EVACD = [YT, YT2]
        YWD = [YW, YW2]
        sgp = [sg, sg2]
        shp = [sh, sh2]
        tmP3 = {}
        ldn3 = {}
        smEv3 = {}
        vmAd3 = {}
        smSq3 = {}
        ldfree3 = {0: 0, 1: 0}
        evfree3 = {0: 0, 1: 0}
        stY3 = {}
        rows_done3 = 0
        for nc4 in range(4):
            for et in range(8):
                gi = nc4 * 8 + et
                lp = gi % 2

                def _(t, nc4=nc4, et=et, gi=gi):
                    if gi == 0:
                        t.wait_ge(lg.h, lg_wo)
                        t.wait_ge(vm.h, vm.n)
                    if gi >= 4:
                        t.wait_ge(vm.h, bank_free[gi % 4])
                    ins = None
                    for dt in range(8):
                        ins = t.matmul(bank(gi),
                                       wo_sb[:, dt * 1024 + et * 128: dt * 1024 + (et + 1) * 128],
                                       att[:, dt * TOK + nc4 * 512: dt * TOK + (nc4 + 1) * 512],
                                       start=(dt == 0), stop=(dt == 7))
                    tm.inc(ins)
                TE(_)
                tmP3[gi] = tm.n

                def _(sync, nc4=nc4, et=et, lp=lp):
                    sync.wait_ge(vm.h, ldfree3[lp])
                    sync.wait_ge(ld.h, ld.n)
                    ld.inc(sync.dma_start(out=ldscr[:, lp * 512:(lp + 1) * 512],
                                          in_=x2T_in[et * 128:(et + 1) * 128,
                                                     nc4 * 512:(nc4 + 1) * 512]), 16)
                SY(_)
                ldn3[gi] = ld.n

                def _(s, gi=gi, lp=lp):
                    s.wait_ge(tm.h, tmP3[gi])
                    s.wait_ge(vm.h, evfree3[lp])
                    sm.inc(s.activation(EVACD[lp], bank(gi), AF.Copy))
                SC(_)
                smEv3[gi] = sm.n

                def _(v, nc4=nc4, et=et, gi=gi, lp=lp):
                    v.wait_ge(sm.h, smEv3[gi])
                    v.wait_ge(ld.h, ldn3[gi])
                    v.wait_ge(*stY3.get(gi - 2, (sh.h, 0)))
                    v.wait_ge(tm.h, tm.n)
                    vchain(v, v.tensor_add(YWD[lp], EVACD[lp], ldscr[:, lp * 512:(lp + 1) * 512]))
                    vchain(v, v.tensor_copy(ybf2[:, 0:512], YWD[lp]))
                    bank_free[gi % 4] = vm.n
                    ldfree3[lp] = vm.n
                    evfree3[lp] = vm.n
                VE(_)
                vmAd3[gi] = vm.n

                def _(s, gi=gi):
                    s.wait_ge(vm.h, vmAd3[gi])
                    s.wait_ge(tm.h, tm.n)
                    sm.inc(s.activation(sqscr[:, 0:512], ybf2[:, 0:512], AF.Square))
                SC(_)
                smSq3[gi] = sm.n

                def _(t, nc4=nc4, et=et, gi=gi, rd=rows_done3):
                    t.wait_ge(sm.h, smSq3[gi])
                    if et == 0:
                        t.wait_ge(vm.h, rd)
                    t.matmul(SB[0:1, 0:512], onesb[:], ybf2[:, 0:512],
                             start=(et == 0), stop=(et == 7))
                    ins = t.matmul(QB[0:1, 0:512], onesb[:], sqscr[:, 0:512],
                                   start=(et == 0), stop=(et == 7))
                    tm.inc(ins)
                TE(_)

                def _(s, nc4=nc4, et=et, gi=gi, lp=lp):
                    s.wait_ge(vm.h, vmAd3[gi])
                    s.wait_ge(shp[lp].h, shp[lp].n)
                    shp[lp].inc(s.dma_start(out=y3_dram[et * 128:(et + 1) * 128,
                                                        nc4 * 512:(nc4 + 1) * 512],
                                            in_=YWD[lp]), 16)
                SC(_)
                stY3[gi] = (shp[lp].h, shp[lp].n)

            def _(v, nc4=nc4):
                v.wait_ge(tm.h, tm.n)
                mu = rows[0:1, nc4 * 512:(nc4 + 1) * 512]
                vchain(v, v.tensor_scalar(mu, SB[0:1, 0:512], 1.0 / 1024.0, None, ALU.mult))
                vchain(v, v.tensor_scalar(rscr[0:1, 0:512], QB[0:1, 0:512], 1.0 / 1024.0, None, ALU.mult))
                vchain(v, v.tensor_mul(rscr[0:1, 512:1024], mu, mu))
                vchain(v, v.tensor_sub(rscr[0:1, 0:512], rscr[0:1, 0:512], rscr[0:1, 512:1024]))
            VE(_)

            def _(s):
                s.wait_ge(vm.h, vm.n)
                schain(s, s.activation(rscr[0:1, 512:1024], rscr[0:1, 0:512], AF.Sqrt,
                                       bias=epsc[0:1, 0:1]))
            SC(_)

            def _(v, nc4=nc4):
                v.wait_ge(sm.h, sm.n)
                vchain(v, v.reciprocal(rows[0:1, 2048 + nc4 * 512: 2048 + (nc4 + 1) * 512],
                                       rscr[0:1, 512:1024]))
            VE(_)
            rows_done3 = vm.n

        vmAp3 = {}
        ldnB3 = {}
        stX3 = {}
        p2_vmD = vm.n
        p2_sgD = [(sh.h, sh.n), (sh2.h, sh2.n)]
        for nc4 in range(4):
            def _(t, nc4=nc4):
                t.wait_ge(vm.h, vm.n)
                t.matmul(SB[:, 0:512], onesf[:], rows[0:1, nc4 * 512:(nc4 + 1) * 512],
                         start=True, stop=True)
                ins = t.matmul(QB[:, 0:512], onesf[:],
                               rows[0:1, 2048 + nc4 * 512: 2048 + (nc4 + 1) * 512],
                               start=True, stop=True)
                tm.inc(ins)
            TE(_)
            tmUB3 = tm.n

            for et in range(8):
                gi = nc4 * 8 + et
                lp = gi % 2

                def _(sync, nc4=nc4, et=et, gi=gi, lp=lp):
                    sync.wait_ge(vm.h, vmAp3.get(gi - 2, p2_vmD))
                    sync.wait_ge(*stY3[gi])
                    sync.wait_ge(ld.h, ld.n)
                    ld.inc(sync.dma_start(out=ldscr[:, lp * 512:(lp + 1) * 512],
                                          in_=y3_dram[et * 128:(et + 1) * 128,
                                                      nc4 * 512:(nc4 + 1) * 512]), 16)
                SY(_)
                ldnB3[gi] = ld.n

                def _(v, nc4=nc4, et=et, gi=gi, lp=lp, tmUB3=tmUB3):
                    v.wait_ge(ld.h, ldnB3[gi])
                    v.wait_ge(tm.h, tmUB3)
                    for _hh, _nn in ([stX3[gi - 2]] if gi >= 2 else p2_sgD):
                        v.wait_ge(_hh, _nn)
                    vchain(v, v.tensor_sub(EVACD[lp], ldscr[:, lp * 512:(lp + 1) * 512], SB[:, 0:512]))
                    vchain(v, v.tensor_mul(YWD[lp], EVACD[lp], QB[:, 0:512]))
                    vchain(v, v.tensor_copy(xbfp[:, lp * 512:(lp + 1) * 512], YWD[lp]))
                VE(_)
                vmAp3[gi] = vm.n

                def _(g, nc4=nc4, et=et, gi=gi, lp=lp):
                    g.wait_ge(vm.h, vmAp3[gi])
                    g.wait_ge(sgp[lp].h, sgp[lp].n)
                    sgp[lp].inc(g.dma_start(out=x3_dram[et * 128:(et + 1) * 128,
                                                        nc4 * 512:(nc4 + 1) * 512],
                                            in_=YWD[lp]), 16)
                    sgp[lp].inc(g.dma_start(out=x3bf_dram[et * 128:(et + 1) * 128,
                                                          nc4 * 512:(nc4 + 1) * 512],
                                            in_=xbfp[:, lp * 512:(lp + 1) * 512]), 16)
                GP(_)
                stX3[gi] = (sgp[lp].h, sgp[lp].n)
        ln3_sg = [(sg.h, sg.n), (sg2.h, sg2.n)]
        ln3_vm = vm.n
        p3_tm = tm.n
        vmS4 = {}
        smS4 = {}
        ldn4 = {}
        stY4 = {}
        stats4n = {}
        rows4_done = 0

        es3.close()
        ffscr = es.enter_context(nc.sbuf_tensor("d_ffscr", [128, 1024], BF16, side="right"))
        AH = ffscr[:, 0:512]
        BH = ffscr[:, 512:1024]
        h_sb = es.enter_context(nc.sbuf_tensor("d_h_sb", [128, 32 * 1024], BF16, side="right"))
        g_sb = es.enter_context(nc.sbuf_tensor("d_g_sb", [128, 32 * 1024], BF16, side="right"))
        xq_sb = es.enter_context(nc.sbuf_tensor("d_xq_sb", [128, 8 * 1024], BF16, side="right"))
        wfb = es.enter_context(nc.sbuf_tensor("d_wfb", [128, 2 * 2 * 8 * 128], BF16, side="right"))
        whb = es.enter_context(nc.sbuf_tensor("d_whb", [128, 2 * 8 * 128], BF16, side="right"))
        w3b = es.enter_context(nc.sbuf_tensor("d_w3b", [128, 2 * 4 * 512], BF16, side="right"))

        wf_used = {0: 0, 1: 0}
        lwf_tot = {}
        wh_used = {0: 0, 1: 0, 2: 0, 3: 0}
        lwh_tot = {0: 0, 1: 0, 2: 0, 3: 0}
        w3_used = {0: 0, 1: 0}
        hfree = {0: 0, 1: 0}
        gfree = {0: 0, 1: 0}

        s1_tm = {0: 0}
        lgx = {}
        for th_ in range(2):
            tb = th_ * 1024

            def _(g, th_=th_, tb=tb):
                if th_ == 0:
                    for _hh, _nn in ln3_sg:
                        g.wait_ge(_hh, _nn)
                else:
                    g.wait_ge(tm.h, s1_tm[0])
                g.wait_ge(lg.h, lg.n)
                for dt in range(8):
                    lg.inc(g.dma_start(out=xq_sb[:, dt * 1024:(dt + 1) * 1024],
                                       in_=x3bf_dram[dt * 128:(dt + 1) * 128, tb:tb + 1024]), 16)
            GP(_)
            lgx[th_] = lg.n

            for fo in range(32):
                fset = fo % 2
                for wi, WT in enumerate((w1T, w2T)):
                    def _(g, th_=th_, fo=fo, wi=wi, WT=WT, fset=fset):
                        if th_ * 32 + fo >= 2:
                            g.wait_ge(tm.h, wf_used[fset])
                        g.wait_ge(lw[wi].h, lw[wi].n)
                        for hx in range(2):
                            lw[wi].inc(g.dma_start(
                                out=wfb[:, (fset * 2 + wi) * 1024 + hx * 512:
                                    (fset * 2 + wi) * 1024 + (hx + 1) * 512],
                                in_=WT[:, fo * 1024 + hx * 512: fo * 1024 + (hx + 1) * 512]), 16)
                    GP(_)
                    lwf_tot[wi] = lw[wi].n

                def _(t, th_=th_, fo=fo, lwa=lwf_tot[0], lwb=lwf_tot[1], tb=tb, fset=fset):
                    t.wait_ge(lw[0].h, lwa)
                    t.wait_ge(lw[1].h, lwb)
                    if fo == 0:
                        t.wait_ge(lg.h, lgx[th_])
                        t.wait_ge(vm.h, vm.n)
                        t.wait_ge(sm.h, sm.n)
                    elif th_ * 32 + fo >= 2:
                        t.wait_ge(vm.h, hfree[fo % 2])
                    bsel = (fo % 2) * 4
                    ins = None
                    for wi in range(2):
                        for half in range(2):
                            bk = bsel + wi * 2 + half
                            for dt in range(8):
                                ins = t.matmul(
                                    ps[:, bk * 512:(bk + 1) * 512],
                                    wfb[:, (fset * 2 + wi) * 1024 + dt * 128:
                                        (fset * 2 + wi) * 1024 + (dt + 1) * 128],
                                    xq_sb[:, dt * 1024 + half * 512: dt * 1024 + (half + 1) * 512],
                                    start=(dt == 0), stop=(dt == 7))
                    tm.inc(ins)
                    wf_used[fset] = tm.n
                TE(_)

                for half in range(2):
                    def _(s, fo=fo, half=half):
                        s.wait_ge(tm.h, tm.n)
                        s.wait_ge(vm.h, vm.n)
                        bsel = (fo % 2) * 4
                        schain(s, s.activation(AH, ps[:, (bsel + half) * 512:(bsel + half + 1) * 512],
                                               AF.Silu, scale=S1C[:, fo:fo + 1]))
                    SC(_)

                    def _(v, fo=fo, half=half):
                        v.wait_ge(sm.h, sm.n)
                        bsel = (fo % 2) * 4
                        vchain(v, v.tensor_scalar(BH, ps[:, (bsel + 2 + half) * 512:(bsel + 3 + half) * 512],
                                                  S2C[:, fo:fo + 1], None, ALU.mult))
                        vchain(v, v.tensor_mul(h_sb[:, fo * 1024 + half * 512: fo * 1024 + (half + 1) * 512],
                                               AH, BH))
                        hfree[fo % 2] = vm.n
                    VE(_)

            s1_tm[0] = tm.n
            for si, (HT, SHC, SRC, DST) in enumerate(
                    ((h1T, SH1C, h_sb, g_sb), (h2T, SH2C, g_sb, h_sb))):
                for o2 in range(32):
                    for kih in range(4):
                        g2k = ((th_ * 2 + si) * 32 + o2) * 4 + kih
                        p4 = g2k % 2
                        p2 = g2k % 2

                        def _(g, o2=o2, kih=kih, HT=HT, p2=p2, p4=p4, g2k=g2k):
                            if g2k >= 2:
                                g.wait_ge(tm.h, wh_used[p4])
                            g.wait_ge(lw[p2].h, lw[p2].n)
                            for hx in range(2):
                                lw[p2].inc(g.dma_start(
                                    out=whb[:, p4 * 1024 + hx * 512: p4 * 1024 + (hx + 1) * 512],
                                    in_=HT[:, (o2 * 4 + kih) * 1024 + hx * 512:
                                        (o2 * 4 + kih) * 1024 + (hx + 1) * 512]), 16)
                        GP(_)
                        lwh_tot[p4] = lw[p2].n

                        def _(t, si=si, o2=o2, kih=kih, lwh=lwh_tot[p4], th_=th_, p2=p2, p4=p4):
                            t.wait_ge(lw[p2].h, lwh)
                            if kih == 0:
                                if th_ * 64 + si * 32 + o2 >= 2:
                                    t.wait_ge(sm.h, gfree[o2 % 2])
                                if o2 == 0:
                                    t.wait_ge(vm.h, vm.n)
                                    t.wait_ge(sm.h, sm.n)
                            bb = (o2 % 2) * 2
                            ins = None
                            for ki8 in range(8):
                                ki = kih * 8 + ki8
                                for half in range(2):
                                    ins = t.matmul(
                                        ps[:, (bb + half) * 512:(bb + half + 1) * 512],
                                        whb[:, p4 * 1024 + ki8 * 128: p4 * 1024 + (ki8 + 1) * 128],
                                        SRC[:, ki * 1024 + half * 512: ki * 1024 + (half + 1) * 512],
                                        start=(ki == 0), stop=(ki == 31))
                            tm.inc(ins)
                            wh_used[p4] = tm.n
                        TE(_)

                    def _(s, si=si, o2=o2, SHC=SHC, DST=DST):
                        s.wait_ge(tm.h, tm.n)
                        bb = (o2 % 2) * 2
                        sm.inc(s.activation(DST[:, o2 * 1024: o2 * 1024 + 512],
                                            ps[:, bb * 512:(bb + 1) * 512],
                                            AF.Gelu, scale=SHC[:, o2:o2 + 1]))
                        schain(s, s.activation(DST[:, o2 * 1024 + 512: o2 * 1024 + 1024],
                                               ps[:, (bb + 1) * 512:(bb + 2) * 512],
                                               AF.Gelu, scale=SHC[:, o2:o2 + 1]))
                        gfree[o2 % 2] = sm.n
                    SC(_)

            for hh in range(2):
                cb = tb + hh * 512
                tsl = th_ * 2 + hh
                for dog in range(2):
                    for kiq in range(8):
                        w3i = ((th_ * 2 + hh) * 2 + dog) * 8 + kiq
                        s3 = w3i % 2

                        def _(g, dog=dog, kiq=kiq, s3=s3, w3i=w3i):
                            if w3i >= 2:
                                g.wait_ge(tm.h, w3_used[s3])
                            g.wait_ge(lg.h, lg.n)
                            for hx in range(2):
                                lg.inc(g.dma_start(
                                    out=w3b[:, s3 * 2048 + hx * 1024: s3 * 2048 + (hx + 1) * 1024],
                                    in_=w3T[:, (dog * 8 + kiq) * 2048 + hx * 1024:
                                        (dog * 8 + kiq) * 2048 + (hx + 1) * 1024]), 16)
                        GP(_)
                        lg_w3 = lg.n

                        def _(t, dog=dog, kiq=kiq, lg_w3=lg_w3, cb=cb, tb=tb, s3=s3):
                            t.wait_ge(lg.h, lg_w3)
                            if kiq == 0:
                                t.wait_ge(sm.h, sm.n)
                                t.wait_ge(vm.h, vm.n)
                            ins = None
                            for ki4 in range(4):
                                kig = kiq * 4 + ki4
                                for do4 in range(4):
                                    ins = t.matmul(bank(do4),
                                                   w3b[:, s3 * 2048 + ki4 * 512 + do4 * 128:
                                                       s3 * 2048 + ki4 * 512 + (do4 + 1) * 128],
                                                   h_sb[:, kig * 1024 + (cb - tb): kig * 1024 + (cb - tb) + 512],
                                                   start=(kig == 0), stop=(kig == 31))
                            tm.inc(ins)
                            w3_used[s3] = tm.n
                        TE(_)

                    s4_tm = tm.n
                    for do4 in range(4):
                        do = dog * 4 + do4
                        di = ((th_ * 2 + hh) * 2 + dog) * 4 + do4
                        lp = di % 2

                        def _(sync, do=do, cb=cb, di=di, lp=lp):
                            sync.wait_ge(vm.h, vmS4.get(di - 2, ln3_vm))
                            sync.wait_ge(*stX3[(cb // 512) * 8 + do])
                            sync.wait_ge(ld.h, ld.n)
                            ld.inc(sync.dma_start(out=ldscr[:, lp * 512:(lp + 1) * 512],
                                                  in_=x3_dram[do * 128:(do + 1) * 128, cb: cb + 512]), 16)
                        SY(_)
                        ldn4[di] = ld.n

                        def _(v, do=do, do4=do4, di=di, lp=lp, s4_tm=s4_tm):
                            v.wait_ge(ld.h, ldn4[di])
                            v.wait_ge(tm.h, max(s4_tm, stats4n.get(di - 2, p3_tm)))
                            for _hh, _nn in ([stY4[di - 2]] if di >= 2 else ln3_sg):
                                v.wait_ge(_hh, _nn)
                            vchain(v, v.scalar_tensor_tensor(YWD[lp], bank(do4), S3C[:, do:do + 1],
                                                             ldscr[:, lp * 512:(lp + 1) * 512],
                                                             ALU.mult, ALU.add))
                            vchain(v, v.tensor_copy(ybf2[:, lp * 512:(lp + 1) * 512], YWD[lp]))
                        VE(_)
                        vmS4[di] = vm.n

                        def _(s, do=do, di=di, lp=lp):
                            s.wait_ge(vm.h, vmS4[di])
                            s.wait_ge(tm.h, stats4n.get(di - 2, p3_tm))
                            sm.inc(s.activation(sqscr[:, lp * 512:(lp + 1) * 512],
                                                ybf2[:, lp * 512:(lp + 1) * 512], AF.Square))
                        SC(_)
                        smS4[di] = sm.n

                        def _(t, do=do, di=di, lp=lp, rd=rows4_done):
                            t.wait_ge(sm.h, smS4[di])
                            if do == 0:
                                t.wait_ge(vm.h, rd)
                            t.matmul(SB[0:1, 0:512], onesb[:], ybf2[:, lp * 512:(lp + 1) * 512],
                                     start=(do == 0), stop=(do == 7))
                            ins = t.matmul(QB[0:1, 0:512], onesb[:], sqscr[:, lp * 512:(lp + 1) * 512],
                                           start=(do == 0), stop=(do == 7))
                            tm.inc(ins)
                        TE(_)
                        stats4n[di] = tm.n

                        def _(s, do=do, cb=cb, di=di, lp=lp):
                            s.wait_ge(vm.h, vmS4[di])
                            s.wait_ge(shp[lp].h, shp[lp].n)
                            shp[lp].inc(s.dma_start(out=y4_dram[do * 128:(do + 1) * 128, cb: cb + 512],
                                                    in_=YWD[lp]), 16)
                        SC(_)
                        stY4[di] = (shp[lp].h, shp[lp].n)

                def _(v, tsl=tsl):
                    v.wait_ge(tm.h, tm.n)
                    mu = rows[0:1, tsl * 512:(tsl + 1) * 512]
                    vchain(v, v.tensor_scalar(mu, SB[0:1, 0:512], 1.0 / 1024.0, None, ALU.mult))
                    vchain(v, v.tensor_scalar(rscr[0:1, 0:512], QB[0:1, 0:512], 1.0 / 1024.0, None, ALU.mult))
                    vchain(v, v.tensor_mul(rscr[0:1, 512:1024], mu, mu))
                    vchain(v, v.tensor_sub(rscr[0:1, 0:512], rscr[0:1, 0:512], rscr[0:1, 512:1024]))
                VE(_)

                def _(s):
                    s.wait_ge(vm.h, vm.n)
                    schain(s, s.activation(rscr[0:1, 512:1024], rscr[0:1, 0:512], AF.Sqrt,
                                           bias=epsc[0:1, 0:1]))
                SC(_)

                def _(v, tsl=tsl):
                    v.wait_ge(sm.h, sm.n)
                    vchain(v, v.reciprocal(rows[0:1, 2048 + tsl * 512: 2048 + (tsl + 1) * 512],
                                           rscr[0:1, 512:1024]))
                VE(_)
                rows4_done = vm.n

        vmP4 = {}
        ldnP4 = {}
        stP4 = {}
        stp4 = [st, st2]
        s4_vm_end = vm.n
        s4_sg_end = [(sh.h, sh.n), (sh2.h, sh2.n)]
        for nc4 in range(4):
            def _(t, nc4=nc4):
                t.wait_ge(vm.h, vm.n)
                t.matmul(SB[:, 0:512], onesf[:], rows[0:1, nc4 * 512:(nc4 + 1) * 512],
                         start=True, stop=True)
                ins = t.matmul(QB[:, 0:512], onesf[:],
                               rows[0:1, 2048 + nc4 * 512: 2048 + (nc4 + 1) * 512],
                               start=True, stop=True)
                tm.inc(ins)
            TE(_)
            tmUB4 = tm.n

            for et in range(8):
                gi = nc4 * 8 + et
                lp = gi % 2

                def _(sync, nc4=nc4, et=et, gi=gi, lp=lp):
                    sync.wait_ge(vm.h, vmP4.get(gi - 2, s4_vm_end))
                    sync.wait_ge(*stY4[nc4 * 8 + et])
                    sync.wait_ge(ld.h, ld.n)
                    ld.inc(sync.dma_start(out=ldscr[:, lp * 512:(lp + 1) * 512],
                                          in_=y4_dram[et * 128:(et + 1) * 128,
                                                      nc4 * 512:(nc4 + 1) * 512]), 16)
                SY(_)
                ldnP4[gi] = ld.n

                def _(v, nc4=nc4, et=et, gi=gi, lp=lp, tmUB4=tmUB4):
                    v.wait_ge(ld.h, ldnP4[gi])
                    v.wait_ge(tm.h, tmUB4)
                    if gi < 2:
                        for _hh, _nn in s4_sg_end:
                            v.wait_ge(_hh, _nn)
                    else:
                        v.wait_ge(*stP4[gi - 2])
                    vchain(v, v.tensor_sub(EVACD[lp], ldscr[:, lp * 512:(lp + 1) * 512], SB[:, 0:512]))
                    vchain(v, v.tensor_mul(YWD[lp], EVACD[lp], QB[:, 0:512]))
                VE(_)
                vmP4[gi] = vm.n

                def _(g, nc4=nc4, et=et, gi=gi, lp=lp):
                    g.wait_ge(vm.h, vmP4[gi])
                    g.wait_ge(sgp[lp].h, sgp[lp].n)
                    sgp[lp].inc(g.dma_start(out=x4T[et * 128:(et + 1) * 128,
                                                    nc4 * 512:(nc4 + 1) * 512],
                                            in_=YWD[lp]), 16)
                    stP4[gi] = (sgp[lp].h, sgp[lp].n)
                GP(_)

        def _(sync):
            sync.wait_ge(sg.h, sg.n)
            sync.wait_ge(sg2.h, sg2.n)
        SY(_)

    es.close()
    return EQ


def build_launch_D():
    nc = bass.Bass("TRN2", target_bir_lowering=False)
    io = {}
    _declare_io(nc, io, D_INPUTS + [("kf_in", [D, T], "BF16"), ("vt_in", [T, NH * 65], "BF16")],
                D_OUTPUTS)
    io["kf_ap"] = lambda rows0, kc: io["kf_in"][rows0:rows0 + 128, kc * 512:(kc + 1) * 512]
    io["vt_ap"] = lambda kc, hg: io["vt_in"].rearrange("(a p) n -> p a n", p=128)[
        :, kc * 4:(kc + 1) * 4, hg * 260:(hg + 1) * 260]
    io["kdiag_ap"] = lambda hg, ct: io["kdiag"][(2 * hg + ct) * 128:(2 * hg + ct + 1) * 128, :]
    io["vdiag_ap"] = lambda hg: io["vdiag"].rearrange("(a p) n -> p a n", p=128)[
        :, :, hg * 260:(hg + 1) * 260]
    EQ = emit_D(nc, io)
    _flush(nc, EQ)
    return nc



RG_BATCH = [[0, 1, 2, 3], [4, 5, 6, 7]]


def _inject_waits(EQ, waits):
    for q in EQ.values():
        q.calls[0:0] = [("wait_ge", (h, n), {}, None) for h, n in waits]


def build_fused():
    nc = bass.Bass("TRN2", target_bir_lowering=False)
    es = ExitStack()
    io = {}
    _declare_io(nc, io, A_INPUTS + [("td2", [128, 2], "F32"), ("tf2", [128, 2], "F32"),
                                    ("gat_idx", [128, 32], "I32")]
                + [x for x in C_INPUTS if x[0] not in ("xT", "sr_in", "wkv_in")]
                + [x for x in D_INPUTS if x[0] not in ("q_in", "kdiag", "vdiag", "x2T_in")],
                D_OUTPUTS)

    kvin = [nc.dram_tensor(f"kvin_{i}", [2048, 1024], F32, kind="Internal") for i in range(2)]
    kvag = [nc.dram_tensor(f"kvag_{i}", [4 * 2048, 1024], F32, kind="Internal") for i in range(2)]
    sr_dram = nc.dram_tensor("sr_dram", [D, TOK], F32, kind="Internal")
    x_own = nc.dram_tensor("x_own", [D, TOK], F32, kind="Internal")
    wkvo = [nc.dram_tensor(f"wkvo_{i}", [1024, 1024], F32, kind="Internal") for i in range(2)]
    wkvag = [nc.dram_tensor(f"wkvag_{i}", [4 * 1024, 1024], F32, kind="Internal") for i in range(2)]
    q_dram = nc.dram_tensor("q_dram", [D, TOK], BF16, kind="Internal")
    kqo = [nc.dram_tensor(f"kqo_{i}", [D // 2, TOK], BF16, kind="Internal") for i in range(2)]
    vto = [nc.dram_tensor(f"vto_{i}", [TOK, 520], BF16, kind="Internal") for i in range(2)]
    x2_dram = nc.dram_tensor("x2_dram", [D, TOK], F32, kind="Internal")
    kq_all = [nc.dram_tensor(f"kq_all_{i}", [4 * D // 2, TOK], BF16, kind="Internal") for i in range(2)]
    vt_all = [nc.dram_tensor(f"vt_all_{i}", [4 * TOK, 520], BF16, kind="Internal") for i in range(2)]

    cc = Sem(nc, es, "cc")
    gidx = es.enter_context(nc.sbuf_tensor("gidx", [128, 32], I32))

    def phase_sems(mark):
        return [(s.h, s.n) for s in Sem._all[mark:] if s.n > 0]

    ioA = {n: io[n] for n, _, _ in A_INPUTS}

    def kv_dst(pi, et):
        assert pi == 2
        return sr_dram.ap()[et * 128:(et + 1) * 128, :]

    def kv_sth(pi, et, half):
        j, sub = et // 2, (et % 2) * 128
        base = 512 * j + (0 if pi == 0 else 256) + sub
        return kvin[half].ap()[base:base + 128, :]
    ioA["kv_dst"] = kv_dst
    ioA["kv_sth"] = kv_sth
    markA = len(Sem._all)
    EQ_A, stA = emit_A(nc, ioA)
    gq = EQ_A["gpsimd"]
    for dst, src in ((gidx[:, :], io["gat_idx"]),
                     (x_own.ap()[:, 0:1024], io["xT"][:, 1:1025]),
                     (x_own.ap()[:, 1024:2048], io["xT"][:, 1026:2050])):
        fi = FakeIns()
        gq.calls.append(("dma_start", (), {"out": dst, "in_": src}, fi))
        fi.incs.append((cc.h, 16))
        cc.n += 16
    gq.wait_ge(stA.h, 512)
    for half in range(2):
        fi3 = FakeIns()
        gq.calls.append(("collective_compute", ("AllGather", ALU.bypass),
                         {"replica_groups": RG_BATCH, "ins": [kvin[half].ap()],
                          "outs": [kvag[half].ap()]}, fi3))
        fi3.incs.append((cc.h, 1))
        cc.n += 1
        if half == 0:
            cc_kvh = cc.n
    cc_kvt = cc.n
    wA = phase_sems(markA) + [(cc.h, cc_kvh)]

    ioB = {"td2": io["td2"], "tf2": io["tf2"]}

    def kv_load(g, dst, kv, dt2, c, half):
        g_ = 2 * c + half
        s, lc = _g2src(g_)
        col = s * 4 + kv * 2 + dt2
        if lc:
            g.wait_ge(cc.h, cc_kvt)
        return g.indirect_dma_start(dst, None, kvag[lc // 1024].ap(),
                                    IndirectOffsetOnAxis(gidx[:, col:col + 1], 0))

    def wkv_dst(dt2, c, half):
        g_ = 2 * c + half
        return wkvo[g_ // 4].ap()[256 * (g_ % 4) + dt2 * 128: 256 * (g_ % 4) + dt2 * 128 + 128, :]
    ioB["kv_load"] = kv_load
    ioB["wkv_dst"] = wkv_dst
    ccs = {}

    def wkv_ag_hook(g):
        ins = g.collective_compute("AllGather", ALU.bypass,
                                   replica_groups=RG_BATCH, ins=[wkvo[0].ap()],
                                   outs=[wkvag[0].ap()])
        cc.inc(ins, 1)
        ccs["a"] = cc.n
    ioB["wkv_ag_hook"] = wkv_ag_hook
    markB = len(Sem._all)
    EQ_B, stB = emit_B(nc, ioB)
    _inject_waits(EQ_B, wA)
    gq = EQ_B["gpsimd"]
    gq.wait_ge(stB.h, stB.n)
    fi = FakeIns()
    gq.calls.append(("collective_compute", ("AllGather", ALU.bypass),
                     {"replica_groups": RG_BATCH, "ins": [wkvo[1].ap()],
                      "outs": [wkvag[1].ap()]}, fi))
    fi.incs.append((cc.h, 1))
    cc.n += 1
    ccs["b"] = cc.n
    wB = phase_sems(markB) + [(cc.h, ccs["a"])]

    ioC = {n: io[n] for n, _, _ in C_INPUTS if n in io}
    ioC["xT"] = x_own.ap()
    ioC["sr_in"] = sr_dram.ap()
    ioC["wkv_in"] = wkvag[0].ap()

    def wkv_load(g, dst, ft, th):
        g.wait_ge(cc.h, ccs["a"] if th == 0 else ccs["b"])
        return g.indirect_dma_start(dst, None, wkvag[th].ap(),
                                    IndirectOffsetOnAxis(gidx[:, 16 + ft * 2 + th:17 + ft * 2 + th], 0))
    ioC["wkv_load"] = wkv_load
    ioC["x2T"] = x2_dram.ap()
    ioC["q_fm"] = q_dram.ap()
    ioC["k_fm"] = kqo[0].ap()
    ioC["v_tm"] = vto[0].ap()
    ioC["kq_st"] = lambda et: kqo[et // 4].ap()[(et % 4) * 128:(et % 4 + 1) * 128, :]
    ioC["vt_st"] = lambda blk, half: vto[half].ap()[blk * 128:(blk + 1) * 128, :]
    markC = len(Sem._all)
    EQ_C = emit_C(nc, ioC)
    _inject_waits(EQ_C, wB)
    gq = EQ_C["gpsimd"]
    for h, n in phase_sems(markC):
        gq.wait_ge(h, n)
    for half in (0, 1):
        for src_t, dst_t in ((kqo[half], kq_all[half]), (vto[half], vt_all[half])):
            fi = FakeIns()
            gq.calls.append(("collective_compute", ("AllGather", ALU.bypass),
                             {"replica_groups": RG_BATCH, "ins": [src_t.ap()],
                              "outs": [dst_t.ap()]}, fi))
            fi.incs.append((cc.h, 1))
            cc.n += 1
        if half == 0:
            cc_half = cc.n
    wC_nocc = phase_sems(markC)
    wC = wC_nocc + [(cc.h, cc_half)]
    cc_full = cc.n

    ioD = {n: io[n] for n, _, _ in D_INPUTS if n in io}
    ioD["q_in"] = q_dram.ap()
    ioD["kdiag"] = kqo[0].ap()
    ioD["vdiag"] = vto[0].ap()
    ioD["x2T_in"] = x2_dram.ap()
    ioD["x4T"] = io["x4T"]

    def kf_ap(rows0, kc):
        g_ = kc // 2
        s, lc0 = _g2src(g_)
        lc = lc0 + (kc % 2) * 512
        hv, r = divmod(rows0, 512)
        return kq_all[hv].ap()[512 * s + r: 512 * s + r + 128, lc:lc + 512]

    def vt_ap(kc, hg):
        g_ = kc // 2
        s, lc0 = _g2src(g_)
        r0 = 2048 * s + lc0 + (kc % 2) * 512
        return vt_all[hg // 2].ap().rearrange("(a p) n -> p a n", p=128)[
            :, r0 // 128:(r0 // 128) + 4, (hg % 2) * 260:(hg % 2 + 1) * 260]
    ioD["kf_ap"] = kf_ap
    ioD["vt_ap"] = vt_ap
    ioD["kdiag_ap"] = lambda hg, ct: kqo[(2 * hg + ct) // 4].ap()[
        ((2 * hg + ct) % 4) * 128:((2 * hg + ct) % 4 + 1) * 128, :]
    ioD["vdiag_ap"] = lambda hg: vto[hg // 2].ap().rearrange("(a p) n -> p a n", p=128)[
        :, :, (hg % 2) * 260:(hg % 2 + 1) * 260]
    ioD["agw"] = lambda sync: sync.wait_ge(cc.h, cc_full)
    EQ_D = emit_D(nc, ioD)
    _inject_waits({k: v for k, v in EQ_D.items() if k != "gpsimd"}, wC)
    _inject_waits({"gpsimd": EQ_D["gpsimd"]}, wC_nocc)

    from concourse.aot_env import allow_ici_all_to_all
    with allow_ici_all_to_all():
        _flush(nc, [EQ_A, EQ_B, EQ_C, EQ_D])
    es.close()
    return nc


def _cc(a):
    return np.ascontiguousarray(a)


def _s1w(w):
    wT = np.asarray(w, np.float32).T
    return _cc(wT.reshape(8, 128, 32, 128).transpose(1, 2, 0, 3).reshape(128, 8 * FF))


def _s23w(w):
    wT = np.asarray(w, np.float32).T
    return _cc(wT.reshape(4, 8, 128, 32, 128).transpose(2, 3, 0, 1, 4).reshape(128, 32 * FF))


def _s4w(w):
    wT = np.asarray(w, np.float32).T
    return _cc(wT.reshape(8, 4, 128, 2, 512).transpose(2, 3, 0, 1, 4).reshape(128, 8 * FF))


def fused_maps(inp):
    x = np.asarray(inp["L0_attn_x"], dtype=np.float32)
    xT = _cc(x.transpose(0, 2, 1))
    tmk = _cc(np.asarray(inp["L0_attn_time_mix_k"], np.float32).reshape(8, 128).T)
    tmv = _cc(np.asarray(inp["L0_attn_time_mix_v"], np.float32).reshape(8, 128).T)
    tmr = _cc(np.asarray(inp["L0_attn_time_mix_r"], np.float32).reshape(8, 128).T)
    WkT = _cc(np.asarray(inp["L0_attn_Wk"], np.float32).T)
    WvT = _cc(np.asarray(inp["L0_attn_Wv"], np.float32).T)
    WrT = _cc(np.asarray(inp["L0_attn_Wr"], np.float32).T)
    td = np.asarray(inp["L0_attn_time_decay"], np.float32)
    tf = np.asarray(inp["L0_attn_time_first"], np.float32)

    c32 = lambda v: _cc(np.asarray(v, np.float32).reshape(32, 128).T)
    jp = np.arange(128)
    shared = {
        "tmk": tmk, "tmv": tmv, "tmr": tmr, "WkT": WkT, "WvT": WvT, "WrT": WrT,
        "WoT": _cc(np.asarray(inp["L0_attn_Wo"], np.float32).T),
        "W1": _cc(np.asarray(inp["L0_ffn_W1"], np.float32)),
        "b1": c32(inp["L0_ffn_b1"]),
        "th": c32(inp["L0_ffn_quantum_phases"]),
        "qwr": c32(inp["L0_ffn_quantum_weights_real"]),
        "qwi": c32(inp["L0_ffn_quantum_weights_imag"]),
        "W2": _cc(np.asarray(inp["L0_ffn_W2"], np.float32)),
        "b2": _cc(np.asarray(inp["L0_ffn_b2"], np.float32).reshape(8, 128).T),
        "Wq": _cc(np.asarray(inp["L1_attn_Wq"], np.float32)),
        "Wk1": _cc(np.asarray(inp["L1_attn_Wk"], np.float32)),
        "Wv1": _cc(np.asarray(inp["L1_attn_Wv"], np.float32)),
        "ident_in": np.eye(128, dtype=np.float32),
        "Wo1": _cc(np.asarray(inp["L1_attn_Wo"], np.float32)),
        "w1T": _s1w(inp["L1_ffn_w1_w"]),
        "w2T": _s1w(inp["L1_ffn_w2_w"]),
        "h1T": _s23w(inp["L1_ffn_h1_w"]),
        "h2T": _s23w(inp["L1_ffn_h2_w"]),
        "w3T": _s4w(inp["L1_ffn_w3_w"]),
        "s1c_in": c32(inp["L1_ffn_w1_s"]), "s2c_in": c32(inp["L1_ffn_w2_s"]),
        "sh1_in": c32(inp["L1_ffn_h1_s"]), "sh2_in": c32(inp["L1_ffn_h2_s"]),
        "s3c_in": _cc(np.asarray(inp["L1_ffn_w3_s"], np.float32).reshape(8, 128).T),
        "dcol_in": np.stack([SLOPES[h] * (jp - 127) + DIAG_OFF - MS for h in range(NH)],
                            axis=1).astype(np.float32),
        "fvb_in": np.stack([-SLOPES[h] * jp - DIAG_OFF for h in range(NH)],
                           axis=1).astype(np.float32),
    }

    maps = []
    for c in range(NCORES):
        b, h0, t0 = core_chunks(c)
        j = c % 4
        m = dict(shared)
        xin = np.zeros((D, 2050), dtype=np.float32)
        if h0 > 0:
            xin[:, 0] = xT[b, :, h0 - 1]
        xin[:, 1:1025] = xT[b, :, h0:h0 + 1024]
        xin[:, 1025] = xT[b, :, t0 - 1]
        xin[:, 1026:2050] = xT[b, :, t0:t0 + 1024]
        m["xT"] = xin
        fsl = slice(256 * j, 256 * j + 256)
        m["td2"] = _cc(td[fsl].reshape(2, 128).T)
        m["tf2"] = _cc(tf[fsl].reshape(2, 128).T)
        gi = np.zeros((128, 32), dtype=np.int32)
        for s in range(4):
            for kv in range(2):
                for dt2 in range(2):
                    gi[:, s * 4 + kv * 2 + dt2] = 2048 * s + 512 * j + 256 * kv + 128 * dt2 + jp
        for ft in range(8):
            s2, dt2 = divmod(ft, 2)
            for th in range(2):
                g_ = j if th == 0 else 7 - j
                gi[:, 16 + ft * 2 + th] = 1024 * s2 + 256 * (g_ % 4) + 128 * dt2 + jp
        m["gat_idx"] = gi
        qgpos = qg_positions(c)
        bias_tbl = np.zeros((128, NSLOT * NH), dtype=np.float32)
        for qg in range(4):
            q0 = qgpos[qg]
            for slot in range(SWEEPS[qg] // 128):
                j0 = slot * 128
                for h in range(NH):
                    col = h * NSLOT + QGOFF[qg] + slot
                    if j0 < q0:
                        bias_tbl[:, col] = SLOPES[h] * (j0 + jp - q0) - MS
                    else:
                        bias_tbl[:, col] = -120.0
        m["bias_tbl"] = bias_tbl
        maps.append(m)
    return maps


def kernel(**inp):
    maps = fused_maps(inp)
    res = run_bass_kernel_spmd(build_fused(), maps, core_ids=list(range(NCORES))).results
    out = np.zeros((B, T, D), dtype=np.float32)
    for c in range(NCORES):
        b, h0, t0 = core_chunks(c)
        a = np.asarray(res[c]["x4T"], dtype=np.float32)
        out[b, h0:h0 + 1024] = a[:, 0:1024].T
        out[b, t0:t0 + 1024] = a[:, 1024:2048].T
    return out


def _kernel_multilaunch_dead(**inp):
    resA = None

    def reasm_fm(res, key, dtype=np.float32):
        out = np.zeros((B, D, T), dtype=dtype)
        for c in range(NCORES):
            b, h0, t0 = core_chunks(c)
            a = np.asarray(res[c][key])
            out[b, :, h0:h0 + 1024] = a[:, 0:1024]
            out[b, :, t0:t0 + 1024] = a[:, 1024:2048]
        return out

    kfull = reasm_fm(resA, "k_out")
    vfull = reasm_fm(resA, "v_out")

    td = np.asarray(inp["L0_attn_time_decay"], np.float32)
    tf = np.asarray(inp["L0_attn_time_first"], np.float32)
    maps = []
    for c in range(NCORES):
        sl = slice(128 * c, 128 * (c + 1))
        maps.append({"k_in": _cc(kfull[:, sl, :]), "v_in": _cc(vfull[:, sl, :]),
                     "td": _cc(td[sl].reshape(128, 1)), "tf": _cc(tf[sl].reshape(128, 1))})
    resB = run_bass_kernel_spmd(build_launch_B(), maps, core_ids=list(range(NCORES))).results
    wkv_all = np.zeros((B, D, T), dtype=np.float32)
    for c in range(NCORES):
        wkv_all[:, 128 * c:128 * (c + 1), :] = np.asarray(resB[c]["wkv_out"])

    c32 = lambda v: _cc(np.asarray(v, np.float32).reshape(32, 128).T)
    sharedC = {
        "WoT": _cc(np.asarray(inp["L0_attn_Wo"], np.float32).T),
        "W1": _cc(np.asarray(inp["L0_ffn_W1"], np.float32)),
        "b1": c32(inp["L0_ffn_b1"]),
        "th": c32(inp["L0_ffn_quantum_phases"]),
        "qwr": c32(inp["L0_ffn_quantum_weights_real"]),
        "qwi": c32(inp["L0_ffn_quantum_weights_imag"]),
        "W2": _cc(np.asarray(inp["L0_ffn_W2"], np.float32)),
        "b2": _cc(np.asarray(inp["L0_ffn_b2"], np.float32).reshape(8, 128).T),
        "Wq": _cc(np.asarray(inp["L1_attn_Wq"], np.float32)),
        "Wk1": _cc(np.asarray(inp["L1_attn_Wk"], np.float32)),
        "Wv1": _cc(np.asarray(inp["L1_attn_Wv"], np.float32)),
        "ident_in": np.eye(128, dtype=np.float32),
    }
    maps = []
    for c in range(NCORES):
        b, h0, t0 = core_chunks(c)
        m = dict(sharedC)
        m["xT"] = _cc(np.concatenate([xT[b, :, h0:h0 + 1024], xT[b, :, t0:t0 + 1024]], axis=1))
        m["sr_in"] = np.asarray(resA[c]["sr_out"])
        m["wkv_in"] = _cc(np.concatenate([wkv_all[b, :, h0:h0 + 1024], wkv_all[b, :, t0:t0 + 1024]], axis=1))
        maps.append(m)
    resC = run_bass_kernel_spmd(build_launch_C(), maps, core_ids=list(range(NCORES))).results

    kf_b = np.zeros((B, D, T), dtype=BF16NP)
    vt_b = np.zeros((B, T, NH * 65), dtype=BF16NP)
    for c in range(NCORES):
        b, h0, t0 = core_chunks(c)
        kq = np.asarray(resC[c]["k_fm"])
        kf_b[b, :, h0:h0 + 1024] = kq[:, 0:1024]
        kf_b[b, :, t0:t0 + 1024] = kq[:, 1024:2048]
        vt = np.asarray(resC[c]["v_tm"])
        vt_b[b, h0:h0 + 1024] = vt[0:1024]
        vt_b[b, t0:t0 + 1024] = vt[1024:2048]

    jp = np.arange(128)
    sharedD = {
        "Wo1": _cc(np.asarray(inp["L1_attn_Wo"], np.float32)),
        "w1T": _s1w(inp["L1_ffn_w1_w"]),
        "w2T": _s1w(inp["L1_ffn_w2_w"]),
        "h1T": _s23w(inp["L1_ffn_h1_w"]),
        "h2T": _s23w(inp["L1_ffn_h2_w"]),
        "w3T": _s4w(inp["L1_ffn_w3_w"]),
        "s1c_in": c32(inp["L1_ffn_w1_s"]), "s2c_in": c32(inp["L1_ffn_w2_s"]),
        "sh1_in": c32(inp["L1_ffn_h1_s"]), "sh2_in": c32(inp["L1_ffn_h2_s"]),
        "s3c_in": _cc(np.asarray(inp["L1_ffn_w3_s"], np.float32).reshape(8, 128).T),
        "dcol_in": np.stack([SLOPES[h] * (jp - 127) + DIAG_OFF - MS for h in range(NH)],
                            axis=1).astype(np.float32),
        "fvb_in": np.stack([-SLOPES[h] * jp - DIAG_OFF for h in range(NH)],
                           axis=1).astype(np.float32),
    }
    maps = []
    for c in range(NCORES):
        b, h0, t0 = core_chunks(c)
        m = dict(sharedD)
        m["q_in"] = np.asarray(resC[c]["q_fm"])
        m["kf_in"] = kf_b[b]
        m["vt_in"] = vt_b[b]
        m["kdiag"] = np.asarray(resC[c]["k_fm"])
        m["vdiag"] = np.asarray(resC[c]["v_tm"])
        m["x2T_in"] = np.asarray(resC[c]["x2T"])
        qgpos = qg_positions(c)
        bias_tbl = np.zeros((128, NSLOT * NH), dtype=np.float32)
        for qg in range(4):
            q0 = qgpos[qg]
            for slot in range(SWEEPS[qg] // 128):
                j0 = slot * 128
                for h in range(NH):
                    col = h * NSLOT + QGOFF[qg] + slot
                    if j0 < q0:
                        bias_tbl[:, col] = SLOPES[h] * (j0 + jp - q0) - MS
                    else:
                        bias_tbl[:, col] = -120.0
        m["bias_tbl"] = bias_tbl
        maps.append(m)
    resD = run_bass_kernel_spmd(build_launch_D(), maps, core_ids=list(range(NCORES))).results

    out = np.zeros((B, T, D), dtype=np.float32)
    for c in range(NCORES):
        b, h0, t0 = core_chunks(c)
        a = np.asarray(resD[c]["x4T"], dtype=np.float32)
        out[b, h0:h0 + 1024] = a[:, 0:1024].T
        out[b, t0:t0 + 1024] = a[:, 1024:2048].T
    return out


# version 25 (final)
# speedup vs baseline: 1.0068x; 1.0068x over previous
import numpy as np
import ml_dtypes
from contextlib import ExitStack

import concourse.bass as bass
import concourse.mybir as mybir
from concourse.bass import IndirectOffsetOnAxis
from concourse.bass_utils import run_bass_kernel_spmd

F32 = mybir.dt.float32
BF16 = mybir.dt.bfloat16
I32 = mybir.dt.int32
AF = mybir.ActivationFunctionType
ALU = mybir.AluOpType
BF16NP = ml_dtypes.bfloat16

NCORES = 8
B, T, D, FF = 2, 8192, 1024, 4096
TOK = 2048
NH, HD = 16, 64
EPS_LN = 1e-5
MS = 34.0
DIAG_OFF = 45.0
TWO_PI = 6.283185307179586

SLOPES = [2 ** (-0.5 * (h + 1)) for h in range(NH)]


def core_chunks(c):
    s = c % 4
    return c // 4, 1024 * s, 8192 - 1024 * (s + 1)


class Sem:

    _ctr = [0]
    _all = []

    def __init__(self, nc, es, name):
        Sem._ctr[0] += 1
        self.h = es.enter_context(nc.semaphore(f"{name}_{Sem._ctr[0]}"))
        self.n = 0
        Sem._all.append(self)

    def inc(self, instr, step=1):
        instr.then_inc(self.h, step)
        self.n += step
        return self.n


class FakeIns:

    __slots__ = ("incs",)

    def __init__(self):
        self.incs = []

    def then_inc(self, sem, n=1):
        self.incs.append((sem, n))
        return self


class EngQ:

    def __init__(self):
        self.calls = []

    def __getattr__(self, name):
        def m(*a, **k):
            fi = FakeIns()
            self.calls.append((name, a, k, fi))
            return fi
        return m

    def wait_ge(self, sem, val):
        self.calls.append(("wait_ge", (sem, val), {}, None))

    def replay(self, e):
        for name, a, k, fi in self.calls:
            ins = getattr(e, name)(*a, **k)
            if fi is not None:
                for sem, n in fi.incs:
                    ins.then_inc(sem, n)



A_INPUTS = [("xT", [D, 2050], "F32"), ("tmk", [128, 8], "F32"), ("tmv", [128, 8], "F32"),
            ("tmr", [128, 8], "F32"), ("WkT", [D, D], "F32"), ("WvT", [D, D], "F32"),
            ("WrT", [D, D], "F32")]


def emit_A(nc, io):
    es = ExitStack()
    xT, tmk, tmv, tmr, WkT, WvT, WrT = (io[n] for n, _, _ in A_INPUTS)
    kv_dst = io["kv_dst"]

    x_sb = es.enter_context(nc.sbuf_tensor("a_x_sb", [128, 2 * 2050], F32, side="right"))
    mix = {m: es.enter_context(nc.sbuf_tensor(f"a_mix_{m}", [128, 8 * TOK], BF16, side="right"))
           for m in "kvr"}
    tm_sb = es.enter_context(nc.sbuf_tensor("a_tm_sb", [128, 3 * 8], F32, side="right"))
    dscr = es.enter_context(nc.sbuf_tensor("a_dscr", [128, 1024], F32, side="right"))
    w_sb = es.enter_context(nc.sbuf_tensor("a_w_sb", [128, 2 * 8 * 1024], BF16, side="right"))
    stage = es.enter_context(nc.sbuf_tensor("a_stage", [128, 2 * TOK], F32, side="right"))
    ps = es.enter_context(nc.psum_tensor("a_ps", [128, 4 * 512], F32))

    ld = Sem(nc, es, "a_ld")
    ldx = [Sem(nc, es, "a_ldx0"), Sem(nc, es, "a_ldx1")]
    lw = [Sem(nc, es, "a_lw0"), Sem(nc, es, "a_lw1")]
    vm = Sem(nc, es, "a_vm")
    tsm = Sem(nc, es, "a_tsm")
    ssm = Sem(nc, es, "a_ssm")
    st = Sem(nc, es, "a_st")

    EQ = {n_: EngQ() for n_ in ("sync", "gpsimd", "vector", "scalar", "tensor")}
    SY = lambda f: f(EQ["sync"])
    GP = lambda f: f(EQ["gpsimd"])
    VE = lambda f: f(EQ["vector"])
    SC = lambda f: f(EQ["scalar"])
    TE = lambda f: f(EQ["tensor"])

    def _(sync):
        for i, tmx in enumerate((tmk, tmv, tmr)):
            ld.inc(sync.dma_start(out=tm_sb[:, i * 8:(i + 1) * 8], in_=tmx), 16)
            sync.wait_ge(ld.h, ld.n)
        for ft in range(8):
            if ft >= 2:
                sync.wait_ge(vm.h, 8 * (ft - 1))
            p = ft % 2
            ldx[p].inc(sync.dma_start(out=x_sb[:, p * 2050:(p + 1) * 2050],
                                      in_=xT[ft * 128:(ft + 1) * 128, :]), 16)
    SY(_)

    def _(g):
        for pi, W in enumerate((WkT, WvT, WrT)):
            if pi == 2:
                g.wait_ge(tsm.h, 32)
            p = pi % 2
            g.wait_ge(lw[p].h, lw[p].n)
            base = p * 8192
            for dt in range(8):
                lw[p].inc(g.dma_start(out=w_sb[:, base + dt * 1024: base + (dt + 1) * 1024],
                                      in_=W[dt * 128:(dt + 1) * 128, :]), 16)
    GP(_)

    def _(v):
        v.wait_ge(ld.h, 48)
        for ft in range(8):
            p = ft % 2
            v.wait_ge(ldx[p].h, 16 * (ft // 2 + 1))
            xb = x_sb[:, p * 2050:(p + 1) * 2050]
            for ch in range(2):
                cur = xb[:, 1 + ch * 1025: 1 + ch * 1025 + 1024]
                last = xb[:, ch * 1025: ch * 1025 + 1024]
                vm.inc(v.tensor_sub(dscr[:], cur, last))
                v.wait_ge(vm.h, vm.n)
                for i in range(3):
                    m = "kvr"[i]
                    vm.inc(v.scalar_tensor_tensor(
                        mix[m][:, ft * TOK + ch * 1024: ft * TOK + ch * 1024 + 1024],
                        dscr[:], tm_sb[:, i * 8 + ft: i * 8 + ft + 1], last,
                        ALU.mult, ALU.add))
                v.wait_ge(vm.h, vm.n)
    VE(_)

    def _(t):
        t.wait_ge(vm.h, 64)
        for pi in range(3):
            p = pi % 2
            t.wait_ge(lw[p].h, 128 if pi < 2 else 256)
            wbase = p * 8192
            for et in range(8):
                for nc4 in range(4):
                    g_ = pi * 32 + et * 4 + nc4
                    if g_ >= 4:
                        t.wait_ge(ssm.h, g_ - 3)
                    ins = None
                    for dt in range(8):
                        ins = t.matmul(
                            ps[:, (g_ % 4) * 512:(g_ % 4 + 1) * 512],
                            w_sb[:, wbase + dt * 1024 + et * 128: wbase + dt * 1024 + (et + 1) * 128],
                            mix["kvr"[pi]][:, dt * TOK + nc4 * 512: dt * TOK + (nc4 + 1) * 512],
                            start=(dt == 0), stop=(dt == 7))
                    tsm.inc(ins)
    TE(_)

    def _(s):
        def stcnt(m):
            return 32 * min(m + 1, 16) + 16 * max(0, m - 15)
        for pi in range(3):
            for et in range(8):
                ei = pi * 8 + et
                for nc4 in range(4):
                    g_ = pi * 32 + et * 4 + nc4
                    s.wait_ge(tsm.h, g_ + 1)
                    if nc4 == 0 and ei >= 2:
                        s.wait_ge(st.h, stcnt(ei - 2))
                    ssm.inc(s.activation(
                        stage[:, (ei % 2) * TOK + nc4 * 512: (ei % 2) * TOK + (nc4 + 1) * 512],
                        ps[:, (g_ % 4) * 512:(g_ % 4 + 1) * 512],
                        AF.Sigmoid if pi == 2 else AF.Copy))
    SC(_)

    def _(sync):
        for pi in range(3):
            for et in range(8):
                ei = pi * 8 + et
                sync.wait_ge(ssm.h, (ei + 1) * 4)
                if pi == 2:
                    st.inc(sync.dma_start(out=kv_dst(pi, et),
                                          in_=stage[:, (ei % 2) * TOK:(ei % 2) * TOK + TOK]), 16)
                else:
                    for half in range(2):
                        st.inc(sync.dma_start(out=io["kv_sth"](pi, et, half),
                                              in_=stage[:, (ei % 2) * TOK + half * 1024:
                                                        (ei % 2) * TOK + (half + 1) * 1024]), 16)
                sync.wait_ge(st.h, st.n)
    SY(_)

    es.close()
    return EQ, st


def build_launch_A():
    nc = bass.Bass("TRN2", target_bir_lowering=False)
    io = {}
    _declare_io(nc, io, A_INPUTS,
                [("kv_out", [2048, TOK], "F32"), ("sr_out", [D, TOK], "F32")])
    def kv_dst(pi, et):
        if pi == 2:
            return io["sr_out"][et * 128:(et + 1) * 128, :]
        j, sub = et // 2, (et % 2) * 128
        base = 512 * j + (0 if pi == 0 else 256) + sub
        return io["kv_out"][base:base + 128, :]
    io["kv_dst"] = kv_dst
    io["kv_sth"] = lambda pi, et, half: kv_dst(pi, et)[:, half * 1024:(half + 1) * 1024]
    EQ, _st = emit_A(nc, io)
    _flush(nc, EQ)
    return nc


def emit_B(nc, io):
    es = ExitStack()
    CH = 2048
    NCH = T // CH
    OPS = 10
    kv_load = io["kv_load"]
    wkv_dst = io["wkv_dst"]
    td2, tf2 = io["td2"], io["tf2"]

    kc = es.enter_context(nc.sbuf_tensor("b_kc", [128, 2 * CH], F32, side="right"))
    vc = es.enter_context(nc.sbuf_tensor("b_vc", [128, 2 * CH], F32, side="right"))
    ek = es.enter_context(nc.sbuf_tensor("b_ek", [128, CH], F32, side="right"))
    euk = es.enter_context(nc.sbuf_tensor("b_euk", [128, CH], F32, side="right"))
    da = es.enter_context(nc.sbuf_tensor("b_da", [128, CH], F32, side="right"))
    abuf = es.enter_context(nc.sbuf_tensor("b_abuf", [128, 2 * (CH + 1)], F32, side="right"))
    bbuf = es.enter_context(nc.sbuf_tensor("b_bbuf", [128, 2 * (CH + 1)], F32, side="right"))
    num = es.enter_context(nc.sbuf_tensor("b_num", [128, CH], F32, side="right"))
    den = es.enter_context(nc.sbuf_tensor("b_den", [128, CH], F32, side="right"))
    lam = es.enter_context(nc.sbuf_tensor("b_lam", [128, 2 * CH], F32, side="right"))
    wkv_sb = es.enter_context(nc.sbuf_tensor("b_wkv", [128, 2 * CH], F32, side="right"))
    vec = es.enter_context(nc.sbuf_tensor("b_vec", [128, 8], F32, side="right"))

    ldv = Sem(nc, es, "b_ldv")
    ldc = [Sem(nc, es, "b_ldc0"), Sem(nc, es, "b_ldc1")]
    vm = Sem(nc, es, "b_vm")
    sm = Sem(nc, es, "b_sm")
    st = Sem(nc, es, "b_st")

    EQ = {n_: EngQ() for n_ in ("sync", "gpsimd", "vector", "scalar", "tensor")}
    ORDER = [(i % 2, i // 2) for i in range(2 * NCH)]

    def _(sync):
        ldv.inc(sync.dma_start(out=vec[:, 0:2], in_=td2), 16)
        ldv.inc(sync.dma_start(out=vec[:, 2:4], in_=tf2), 16)
    _(EQ["sync"])

    def _loads(g, i):
        dt2, c = ORDER[i]
        p = i % 2
        if i >= 2:
            g.wait_ge(sm.h, 2 + (i - 1) * 2)
            g.wait_ge(ldc[p].h, ldc[p].n)
        for half in range(2):
            ldc[p].inc(kv_load(g, kc[:, p * CH + half * 1024: p * CH + (half + 1) * 1024],
                               0, dt2, c, half), 16)
            ldc[p].inc(kv_load(g, vc[:, p * CH + half * 1024: p * CH + (half + 1) * 1024],
                               1, dt2, c, half), 16)

    def _(s):
        s.wait_ge(ldv.h, 32)
        s.activation(vec[:, 4:6], vec[:, 0:2], AF.Exp).then_inc(sm.h, 1)
        s.wait_ge(sm.h, 1)
        s.activation(vec[:, 6:8], vec[:, 4:6], AF.Exp, scale=-1.0).then_inc(sm.h, 1)
        sm.n = 2
        for i, (dt2, c) in enumerate(ORDER):
                p = i % 2
                s.wait_ge(ldc[p].h, 64 * (i // 2 + 1))
                if i >= 1:
                    s.wait_ge(vm.h, 4 + (i - 1) * OPS + 8)
                sm.inc(s.activation(ek[:], kc[:, p * CH:(p + 1) * CH], AF.Exp))
                sm.inc(s.activation(euk[:], kc[:, p * CH:(p + 1) * CH], AF.Exp,
                                    bias=vec[:, 2 + dt2:3 + dt2]))
    _(EQ["scalar"])

    def _(v):
        v.wait_ge(sm.h, 2)
        vm.inc(v.memset(lam[:], 1.0))
        v.wait_ge(vm.h, vm.n)
        for dt2 in range(2):
            vm.inc(v.tensor_scalar(lam[:, dt2 * CH:(dt2 + 1) * CH], lam[:, dt2 * CH:(dt2 + 1) * CH],
                                   vec[:, 6 + dt2:7 + dt2], None, ALU.mult))
        v.wait_ge(vm.h, vm.n)
        vm.inc(v.memset(da[:], 0.0))
        for i, (dt2, c) in enumerate(ORDER):
                bf = (i % 2) * CH
                ab = abuf[:, dt2 * (CH + 1):(dt2 + 1) * (CH + 1)]
                bb = bbuf[:, dt2 * (CH + 1):(dt2 + 1) * (CH + 1)]
                if c == 0:
                    vm.inc(v.memset(ab[:, 0:1], 0.0))
                    vm.inc(v.memset(bb[:, 0:1], 0.0))
                else:
                    vm.inc(v.tensor_copy(ab[:, 0:1], ab[:, CH:CH + 1]))
                    vm.inc(v.tensor_copy(bb[:, 0:1], bb[:, CH:CH + 1]))
                v.wait_ge(sm.h, 2 + (i + 1) * 2)
                v.wait_ge(vm.h, vm.n)
                vm.inc(v.tensor_mul(da[:], ek[:], vc[:, bf:bf + CH]))
                v.wait_ge(vm.h, vm.n)
                vm.inc(v.tensor_tensor_scan(ab[:, 1:CH + 1], lam[:, dt2 * CH:(dt2 + 1) * CH], da[:],
                                            ab[:, 0:1], ALU.mult, ALU.add))
                vm.inc(v.tensor_tensor_scan(bb[:, 1:CH + 1], lam[:, dt2 * CH:(dt2 + 1) * CH], ek[:],
                                            bb[:, 0:1], ALU.mult, ALU.add))
                v.wait_ge(vm.h, vm.n)
                vm.inc(v.tensor_mul(num[:], euk[:], vc[:, bf:bf + CH]))
                v.wait_ge(vm.h, vm.n)
                vm.inc(v.tensor_add(num[:], num[:], ab[:, 0:CH]))
                vm.inc(v.tensor_add(den[:], euk[:], bb[:, 0:CH]))
                v.wait_ge(vm.h, vm.n)
                vm.inc(v.reciprocal(den[:], den[:]))
                v.wait_ge(vm.h, vm.n)
                if i >= 2:
                    v.wait_ge(st.h, (i - 1) * 32)
                vm.inc(v.tensor_mul(wkv_sb[:, bf:bf + CH], num[:], den[:]))
    _(EQ["vector"])

    def _(g):
        _loads(g, 0)
        _loads(g, 1)
        for i, (dt2, c) in enumerate(ORDER):
                g.wait_ge(vm.h, 4 + (i + 1) * OPS)
                for half in range(2):
                    st.inc(g.dma_start(out=wkv_dst(dt2, c, half),
                                       in_=wkv_sb[:, (i % 2) * CH + half * 1024:(i % 2) * CH + (half + 1) * 1024]), 16)
                g.wait_ge(st.h, st.n)
                if i == 3 and "wkv_ag_hook" in io:
                    io["wkv_ag_hook"](g)
                if i + 2 < 2 * NCH:
                    _loads(g, i + 2)
    _(EQ["gpsimd"])

    es.close()
    return EQ, st


def _g2src(g_):
    if g_ < 4:
        return g_, 0
    return 7 - g_, 1024


def build_launch_B():
    nc = bass.Bass("TRN2", target_bir_lowering=False)
    io = {}
    _declare_io(nc, io, [("kv_all", [2048, 2048], "F32"), ("td2", [128, 2], "F32"),
                         ("tf2", [128, 2], "F32")],
                [("wkv_out", [1024, 2048], "F32")])
    def kv_load(g, dst, kv, dt2, c, half):
        g_ = 2 * c + half
        s, lc = _g2src(g_)
        base = 512 * s + (0 if kv == 0 else 256) + dt2 * 128
        return g.dma_start(out=dst, in_=io["kv_all"][base:base + 128, lc:lc + 1024])
    def wkv_dst(dt2, c, half):
        g_ = 2 * c + half
        s, lc = _g2src(g_)
        return io["wkv_out"][256 * s + dt2 * 128: 256 * s + dt2 * 128 + 128, lc:lc + 1024]
    io["kv_load"] = kv_load
    io["wkv_dst"] = wkv_dst
    EQ, _st = emit_B(nc, io)
    _flush(nc, EQ)
    return nc


C_INPUTS = [("xT", [D, TOK], "F32"), ("sr_in", [D, TOK], "F32"), ("wkv_in", [D, TOK], "F32"),
            ("WoT", [D, D], "F32"), ("W1", [D, FF], "F32"), ("b1", [128, 32], "F32"),
            ("th", [128, 32], "F32"), ("qwr", [128, 32], "F32"), ("qwi", [128, 32], "F32"),
            ("W2", [FF, D], "F32"), ("b2", [128, 8], "F32"), ("Wq", [D, D], "F32"),
            ("Wk1", [D, D], "F32"), ("Wv1", [D, D], "F32"), ("ident_in", [128, 128], "F32")]
C_OUTPUTS = [("x2T", [D, TOK], "F32"), ("q_fm", [D, TOK], "BF16"),
             ("k_fm", [D, TOK], "BF16"), ("v_tm", [TOK, NH * 65], "BF16")]


def emit_C(nc, io):
    es = ExitStack()
    (xT, sr_in, wkv_in, WoT, W1, b1, th, qwr, qwi, W2, b2, Wq, Wk1, Wv1, ident_in) = (
        io[n] for n, _, _ in C_INPUTS)
    x2T, q_fm, k_fm, v_tm = (io[n] for n, _, _ in C_OUTPUTS)

    y_dram = nc.dram_tensor("y_dram", [D, TOK], F32, kind="Internal").ap()
    x1_dram = nc.dram_tensor("x1_dram", [D, TOK], F32, kind="Internal").ap()
    y2_dram = nc.dram_tensor("y2_dram", [D, TOK], F32, kind="Internal").ap()

    xbf = es.enter_context(nc.sbuf_tensor("c_xbf", [128, 8 * TOK], BF16))
    B1 = es.enter_context(nc.sbuf_tensor("c_B1", [128, 16 * 1040], BF16))
    B2 = es.enter_context(nc.sbuf_tensor("c_B2", [128, 16 * 1024], BF16))
    wsq = es.enter_context(nc.sbuf_tensor("c_wsq", [128, 8 * 1024], BF16))
    w1b = es.enter_context(nc.sbuf_tensor("c_w1b", [128, 2 * 8 * 512], BF16))
    w2b = es.enter_context(nc.sbuf_tensor("c_w2b", [128, 16 * 512], BF16))
    fscr = es.enter_context(nc.sbuf_tensor("c_fscr", [128, 6 * 512], F32))
    ldscr = es.enter_context(nc.sbuf_tensor("c_ldscr", [128, 4096], F32))
    iscr = es.enter_context(nc.sbuf_tensor("c_iscr", [128, 512], I32))
    sqscr = es.enter_context(nc.sbuf_tensor("c_sqscr", [128, 512], BF16))
    ybf2 = es.enter_context(nc.sbuf_tensor("c_ybf2", [128, 512], BF16))
    vcols = es.enter_context(nc.sbuf_tensor("c_vcols", [128, 320], F32))
    vscr = es.enter_context(nc.sbuf_tensor("c_vscr", [128, 192], F32))
    rows = es.enter_context(nc.sbuf_tensor("c_rows", [1, 4096], F32))
    rscr = es.enter_context(nc.sbuf_tensor("c_rscr", [1, 2 * 512], F32))
    onesb = es.enter_context(nc.sbuf_tensor("c_onesb", [128, 1], BF16))
    onesf = es.enter_context(nc.sbuf_tensor("c_onesf", [1, 128], F32))
    epsc = es.enter_context(nc.sbuf_tensor("c_epsc", [128, 1], F32))
    ident = es.enter_context(nc.sbuf_tensor("c_ident", [128, 128], BF16))

    ps = es.enter_context(nc.psum_tensor("c_ps", [128, 7 * 512], F32))
    ps_tr = es.enter_context(nc.psum_tensor("c_ps_tr", [128, 128], BF16))

    ld = Sem(nc, es, "ld")
    lw = Sem(nc, es, "lw")
    lwb = Sem(nc, es, "lwb")
    lg = Sem(nc, es, "lg")
    lw1 = [Sem(nc, es, "lw1a"), Sem(nc, es, "lw1b")]
    vm = Sem(nc, es, "vm")
    sm = Sem(nc, es, "sm")
    tm = Sem(nc, es, "tm")
    st = Sem(nc, es, "st")
    sg = Sem(nc, es, "sg")
    sg2 = Sem(nc, es, "sg2")
    sh = Sem(nc, es, "c_sh")
    sh2 = Sem(nc, es, "c_sh2")

    ZARG, MARG, SSIN, YT, YW, SP5 = (fscr[:, i * 512:(i + 1) * 512] for i in range(6))
    B1C, THC, QWRC, QWIC = (vcols[:, i * 32:(i + 1) * 32] for i in range(4))
    B2C = vcols[:, 128:136]
    ZCC = vcols[:, 160:192]
    CPC = vcols[:, 192:224]
    CpCc = vcols[:, 256:288]
    CpSc = vcols[:, 288:320]

    def bank(i):
        return ps[:, (i % 4) * 512:(i % 4 + 1) * 512]

    SB = ps[:, 4 * 512:5 * 512]
    QB = ps[:, 5 * 512:6 * 512]
    UB = ps[:, 6 * 512:7 * 512]

    bank_free_c = {0: 0, 1: 0, 2: 0, 3: 0}
    w1_used = {0: 0, 1: 0}
    w2_used = {0: 0}
    lw1_tot = {0: 0, 1: 0}

    EQ = {n_: EngQ() for n_ in ("sync", "gpsimd", "vector", "scalar", "tensor")}
    SY = lambda f: f(EQ["sync"])
    GP = lambda f: f(EQ["gpsimd"])
    VE = lambda f: f(EQ["vector"])
    SC = lambda f: f(EQ["scalar"])
    TE = lambda f: f(EQ["tensor"])
    if True:
        def vchain(v, ins):
            vm.inc(ins)
            v.wait_ge(vm.h, vm.n)

        def schain(s, ins):
            sm.inc(ins)
            s.wait_ge(sm.h, sm.n)

        def _(sync):
            for t_, dst in ((b1, B1C), (th, THC), (qwr, QWRC), (qwi, QWIC), (b2, B2C)):
                ld.inc(sync.dma_start(out=dst, in_=t_), 16)
                sync.wait_ge(ld.h, ld.n)
        SY(_)

        def _(g):
            lg.inc(g.dma_start(out=ident[:], in_=ident_in), 16)
            g.wait_ge(lg.h, lg.n)
            for dt in range(8):
                lg.inc(g.dma_start(out=wsq[:, dt * 1024:(dt + 1) * 1024],
                                   in_=WoT[dt * 128:(dt + 1) * 128, :]), 16)
            g.wait_ge(lg.h, lg.n)
        GP(_)
        lg_wo = lg.n

        def _(v):
            vchain(v, v.memset(onesb[:], 1.0))
            vchain(v, v.memset(onesf[:], 1.0))
            vchain(v, v.memset(epsc[:], EPS_LN))
            v.wait_ge(ld.h, ld.n)
            vchain(v, v.tensor_scalar(iscr[:, 0:32], THC, 1.0 / TWO_PI, None, ALU.mult))
            vchain(v, v.tensor_copy(vscr[:, 0:32], iscr[:, 0:32]))
            vchain(v, v.scalar_tensor_tensor(vscr[:, 0:32], vscr[:, 0:32], -TWO_PI, THC,
                                             ALU.mult, ALU.add))
            vchain(v, v.tensor_scalar(vscr[:, 96:128], vscr[:, 0:32], 3.141592653589793, -TWO_PI,
                                      ALU.is_gt, ALU.mult))
            vchain(v, v.tensor_add(vscr[:, 0:32], vscr[:, 0:32], vscr[:, 96:128]))
            vchain(v, v.tensor_scalar(vscr[:, 96:128], vscr[:, 0:32], -3.141592653589793, TWO_PI,
                                      ALU.is_lt, ALU.mult))
            vchain(v, v.tensor_add(vscr[:, 0:32], vscr[:, 0:32], vscr[:, 96:128]))
            vchain(v, v.scalar_tensor_tensor(ZCC, B1C, 0.1, vscr[:, 0:32], ALU.mult, ALU.add))
        VE(_)

        def _(s):
            s.wait_ge(vm.h, vm.n)
            schain(s, s.activation(vscr[:, 32:64], vscr[:, 0:32], AF.Sin))
            schain(s, s.activation(vscr[:, 64:96], vscr[:, 0:32], AF.Sin, scale=0.5))
            schain(s, s.activation(vscr[:, 96:128], vscr[:, 64:96], AF.Square))
        SC(_)

        def _(v):
            v.wait_ge(sm.h, sm.n)
            vchain(v, v.tensor_scalar(vscr[:, 128:160], vscr[:, 96:128], -2.0, 1.0,
                                      ALU.mult, ALU.add))
            vchain(v, v.tensor_mul(vscr[:, 160:192], vscr[:, 128:160], QWRC))
            vchain(v, v.tensor_mul(vscr[:, 96:128], vscr[:, 32:64], QWIC))
            vchain(v, v.tensor_add(vscr[:, 160:192], vscr[:, 160:192], vscr[:, 96:128]))
            vchain(v, v.tensor_mul(vscr[:, 160:192], vscr[:, 160:192], QWIC))
            vchain(v, v.tensor_scalar(CPC, vscr[:, 160:192], 0.1, None, ALU.mult))
            vchain(v, v.tensor_scalar(vscr[:, 0:32], ZCC, 3.141592653589793, -TWO_PI,
                                      ALU.is_gt, ALU.mult))
            vchain(v, v.tensor_add(vscr[:, 32:64], ZCC, vscr[:, 0:32]))
            vchain(v, v.tensor_scalar(vscr[:, 0:32], vscr[:, 32:64], -3.141592653589793, TWO_PI,
                                      ALU.is_lt, ALU.mult))
            vchain(v, v.tensor_add(vscr[:, 64:96], vscr[:, 32:64], vscr[:, 0:32]))
        VE(_)

        def _(s):
            s.wait_ge(vm.h, vm.n)
            schain(s, s.activation(vscr[:, 96:128], vscr[:, 64:96], AF.Sin))
            schain(s, s.activation(vscr[:, 128:160], vscr[:, 64:96], AF.Sin, scale=0.5))
            schain(s, s.activation(vscr[:, 160:192], vscr[:, 128:160], AF.Square))
        SC(_)

        def _(v):
            v.wait_ge(sm.h, sm.n)
            vchain(v, v.tensor_scalar(vscr[:, 0:32], vscr[:, 160:192], -2.0, 1.0,
                                      ALU.mult, ALU.add))
            vchain(v, v.tensor_mul(CpCc, CPC, vscr[:, 0:32]))
            vchain(v, v.tensor_mul(CpSc, CPC, vscr[:, 96:128]))
        VE(_)

        wkv_load = io["wkv_load"]
        lwp = [lw, lwb]
        vm_rw = {}
        ldn_rw = {}
        lwn_rw = {}
        for u, (ft, th) in enumerate([(f_, t_) for t_ in range(2) for f_ in range(8)]):
                sl = u % 2

                def _(sync, ft=ft, th=th, u=u, sl=sl):
                    sync.wait_ge(vm.h, vm_rw.get(u - 2, 0))
                    sync.wait_ge(ld.h, ld.n)
                    ld.inc(sync.dma_start(out=ldscr[:, sl * 1024:(sl + 1) * 1024],
                                          in_=sr_in[ft * 128:(ft + 1) * 128,
                                                    th * 1024:(th + 1) * 1024]), 16)
                SY(_)
                ldn_rw[u] = ld.n

                def _(g, ft=ft, th=th, u=u, sl=sl):
                    g.wait_ge(vm.h, vm_rw.get(u - 2, 0))
                    g.wait_ge(lwp[sl].h, lwp[sl].n)
                    lwp[sl].inc(wkv_load(g, ldscr[:, 2048 + sl * 1024:2048 + (sl + 1) * 1024],
                                         ft, th), 16)
                GP(_)
                lwn_rw[u] = lwp[sl].n

                def _(v, ft=ft, th=th, u=u, sl=sl):
                    v.wait_ge(ld.h, ldn_rw[u])
                    v.wait_ge(lwp[sl].h, lwn_rw[u])
                    vm.inc(v.tensor_mul(B1[:, ft * 2048 + th * 1024: ft * 2048 + (th + 1) * 1024],
                                        ldscr[:, sl * 1024:(sl + 1) * 1024],
                                        ldscr[:, 2048 + sl * 1024:2048 + (sl + 1) * 1024]))
                VE(_)
                vm_rw[u] = vm.n

        tmP = {}
        ldn1 = {}
        smEv = {}
        vmAd = {}
        smSq = {}
        ldfree = {0: vm.n, 1: vm.n}
        evfree = {0: 0, 1: 0}
        sgp = [sg, sg2]
        shp = [sh, sh2]
        stY = {}
        rows_done = 0
        EVAC = [YT, SP5]
        YWP = [YW, ZARG]
        S1P = [ZARG, SSIN]
        S2P = [MARG, SP5]
        s1free = {0: 0, 1: 0}
        for nc4 in range(4):
            for et in range(8):
                gi = nc4 * 8 + et
                lp = gi % 2

                def _(t, nc4=nc4, et=et, gi=gi):
                    if gi == 0:
                        t.wait_ge(vm.h, vm.n)
                        t.wait_ge(lg.h, lg_wo)
                    if gi >= 4:
                        t.wait_ge(vm.h, bank_free_c[gi % 4])
                    ins = None
                    for dt in range(8):
                        ins = t.matmul(bank(gi),
                                       wsq[:, dt * 1024 + et * 128: dt * 1024 + (et + 1) * 128],
                                       B1[:, dt * 2048 + nc4 * 512: dt * 2048 + (nc4 + 1) * 512],
                                       start=(dt == 0), stop=(dt == 7))
                    tm.inc(ins)
                TE(_)
                tmP[gi] = tm.n

                def _(sync, nc4=nc4, et=et, gi=gi, lp=lp):
                    sync.wait_ge(vm.h, ldfree[lp])
                    sync.wait_ge(ld.h, ld.n)
                    ld.inc(sync.dma_start(out=ldscr[:, lp * 512:(lp + 1) * 512],
                                          in_=xT[et * 128:(et + 1) * 128, nc4 * 512:(nc4 + 1) * 512]), 16)
                SY(_)
                ldn1[gi] = ld.n

                def _(s, gi=gi, lp=lp):
                    s.wait_ge(tm.h, tmP[gi])
                    s.wait_ge(vm.h, evfree[lp])
                    sm.inc(s.activation(EVAC[lp], bank(gi), AF.Copy))
                SC(_)
                smEv[gi] = sm.n

                def _(v, nc4=nc4, et=et, gi=gi, lp=lp):
                    v.wait_ge(sm.h, smEv[gi])
                    v.wait_ge(ld.h, ldn1[gi])
                    v.wait_ge(*stY.get(gi - 2, (sh.h, 0)))
                    vchain(v, v.tensor_add(YWP[lp], EVAC[lp], ldscr[:, lp * 512:(lp + 1) * 512]))
                    vchain(v, v.tensor_copy(xbf[:, et * TOK + nc4 * 512: et * TOK + (nc4 + 1) * 512],
                                            YWP[lp]))
                    bank_free_c[gi % 4] = vm.n
                    ldfree[lp] = vm.n
                    evfree[lp] = vm.n
                VE(_)
                vmAd[gi] = vm.n

                def _(s, nc4=nc4, et=et, gi=gi):
                    s.wait_ge(vm.h, vmAd[gi])
                    s.wait_ge(tm.h, tm.n)
                    sm.inc(s.activation(sqscr[:], xbf[:, et * TOK + nc4 * 512: et * TOK + (nc4 + 1) * 512],
                                        AF.Square))
                SC(_)
                smSq[gi] = sm.n

                def _(t, nc4=nc4, et=et, gi=gi, rd=rows_done):
                    t.wait_ge(sm.h, smSq[gi])
                    if et == 0:
                        t.wait_ge(vm.h, rd)
                    t.matmul(SB[0:1, 0:512], onesb[:],
                             xbf[:, et * TOK + nc4 * 512: et * TOK + (nc4 + 1) * 512],
                             start=(et == 0), stop=(et == 7))
                    ins = t.matmul(QB[0:1, 0:512], onesb[:], sqscr[:],
                                   start=(et == 0), stop=(et == 7))
                    tm.inc(ins)
                TE(_)

                def _(s, nc4=nc4, et=et, gi=gi, lp=lp):
                    s.wait_ge(vm.h, vmAd[gi])
                    s.wait_ge(shp[lp].h, shp[lp].n)
                    shp[lp].inc(s.dma_start(out=y_dram[et * 128:(et + 1) * 128, nc4 * 512:(nc4 + 1) * 512],
                                            in_=YWP[lp]), 16)
                SC(_)
                stY[gi] = (shp[lp].h, shp[lp].n)

            def _(v, nc4=nc4):
                v.wait_ge(tm.h, tm.n)
                mu = rows[0:1, nc4 * 512:(nc4 + 1) * 512]
                vchain(v, v.tensor_scalar(mu, SB[0:1, 0:512], 1.0 / 1024.0, None, ALU.mult))
                vchain(v, v.tensor_scalar(rscr[0:1, 0:512], QB[0:1, 0:512], 1.0 / 1024.0, None, ALU.mult))
                vchain(v, v.tensor_mul(rscr[0:1, 512:1024], mu, mu))
                vchain(v, v.tensor_sub(rscr[0:1, 0:512], rscr[0:1, 0:512], rscr[0:1, 512:1024]))
            VE(_)

            def _(s, nc4=nc4):
                s.wait_ge(vm.h, vm.n)
                schain(s, s.activation(rscr[0:1, 512:1024], rscr[0:1, 0:512], AF.Sqrt, bias=epsc[0:1, 0:1]))
            SC(_)

            def _(v, nc4=nc4):
                v.wait_ge(sm.h, sm.n)
                vchain(v, v.reciprocal(rows[0:1, 2048 + nc4 * 512: 2048 + (nc4 + 1) * 512],
                                       rscr[0:1, 512:1024]))
            VE(_)
            rows_done = vm.n

        vmAp = {}
        ldnB = {}
        stX = {}
        p1_vm = vm.n
        p1_st = [(sh.h, sh.n), (sh2.h, sh2.n)]
        for nc4 in range(4):
            def _(t, nc4=nc4):
                t.wait_ge(vm.h, vm.n)
                t.matmul(UB[:, 0:512], onesf[:], rows[0:1, nc4 * 512:(nc4 + 1) * 512],
                         start=True, stop=True)
                ins = t.matmul(QB[:, 0:512], onesf[:], rows[0:1, 2048 + nc4 * 512: 2048 + (nc4 + 1) * 512],
                               start=True, stop=True)
                tm.inc(ins)
            TE(_)
            tmUB = tm.n

            for et in range(8):
                gi = nc4 * 8 + et
                lp = gi % 2

                def _(sync, nc4=nc4, et=et, gi=gi, lp=lp):
                    sync.wait_ge(vm.h, vmAp.get(gi - 2, p1_vm))
                    sync.wait_ge(*stY[gi])
                    sync.wait_ge(ld.h, ld.n)
                    ld.inc(sync.dma_start(out=ldscr[:, lp * 512:(lp + 1) * 512],
                                          in_=y_dram[et * 128:(et + 1) * 128, nc4 * 512:(nc4 + 1) * 512]), 16)
                SY(_)
                ldnB[gi] = ld.n

                def _(v, nc4=nc4, et=et, gi=gi, lp=lp, tmUB=tmUB):
                    v.wait_ge(ld.h, ldnB[gi])
                    v.wait_ge(tm.h, tmUB)
                    for _hh, _nn in ([stX[gi - 2]] if gi >= 2 else p1_st):
                        v.wait_ge(_hh, _nn)
                    vchain(v, v.tensor_sub(EVAC[lp], ldscr[:, lp * 512:(lp + 1) * 512], UB[:, 0:512]))
                    vchain(v, v.tensor_mul(YWP[lp], EVAC[lp], QB[:, 0:512]))
                    vchain(v, v.tensor_copy(xbf[:, et * TOK + nc4 * 512: et * TOK + (nc4 + 1) * 512],
                                            YWP[lp]))
                VE(_)
                vmAp[gi] = vm.n

                def _(s, nc4=nc4, et=et, gi=gi, lp=lp):
                    s.wait_ge(vm.h, vmAp[gi])
                    s.wait_ge(shp[lp].h, shp[lp].n)
                    shp[lp].inc(s.dma_start(out=x1_dram[et * 128:(et + 1) * 128, nc4 * 512:(nc4 + 1) * 512],
                                            in_=YWP[lp]), 16)
                SC(_)
                stX[gi] = (shp[lp].h, shp[lp].n)
        p1b_st = [(sh.h, sh.n), (sh2.h, sh2.n)]

        for t512 in range(4):
            for fog in range(8):
                p = fog % 2

                def _(g, t512=t512, fog=fog, p=p):
                    if t512 * 8 + fog >= 2:
                        g.wait_ge(tm.h, w1_used[p])
                    lw1[p].inc(g.dma_start(
                        out=w1b[:, p * 4096:(p + 1) * 4096].rearrange("q (a n) -> q a n", a=8),
                        in_=W1.rearrange("(a q) n -> q a n", q=128)[:, :, fog * 512:(fog + 1) * 512]), 16)
                GP(_)
                lw1_tot[p] = lw1[p].n

                for fo4 in range(4):
                    fo = fog * 4 + fo4

                    def _(t, t512=t512, fo=fo, fo4=fo4, p=p, lw1_w=lw1_tot[p]):
                        t.wait_ge(lw1[p].h, lw1_w)
                        if t512 * 32 + fo >= 4:
                            t.wait_ge(vm.h, bank_free_c[fo % 4])
                        else:
                            t.wait_ge(sm.h, sm.n)
                            t.wait_ge(vm.h, vm.n)
                        ins = None
                        for dt in range(8):
                            ins = t.matmul(bank(fo),
                                           w1b[:, p * 4096 + dt * 512 + fo4 * 128: p * 4096 + dt * 512 + (fo4 + 1) * 128],
                                           xbf[:, dt * TOK + t512 * 512:(dt) * TOK + (t512 + 1) * 512],
                                           start=(dt == 0), stop=(dt == 7))
                        tm.inc(ins)
                        w1_used[p] = tm.n
                    TE(_)

                    lpq = fo % 2

                    def _(s, fo=fo, t512=t512, lpq=lpq):
                        s.wait_ge(tm.h, tm.n)
                        s.wait_ge(vm.h, s1free[lpq])
                        if t512 == 0 and fo == 0:
                            for _hh, _nn in p1b_st:
                                s.wait_ge(_hh, _nn)
                        sm.inc(s.activation(S1P[lpq], bank(fo), AF.Sin, scale=0.1))
                        sm.inc(s.activation(S2P[lpq], bank(fo), AF.Sin, scale=0.1,
                                            bias=np.float32(1.5707963267948966)))
                    SC(_)
                    smW1e = sm.n

                    def _(v, fo=fo, t512=t512, lpq=lpq, smW1e=smW1e):
                        v.wait_ge(sm.h, smW1e)
                        vchain(v, v.scalar_tensor_tensor(S1P[lpq], S1P[lpq], CpCc[:, fo:fo + 1],
                                                         bank(fo), ALU.mult, ALU.add))
                        bank_free_c[fo % 4] = vm.n
                        vchain(v, v.scalar_tensor_tensor(S2P[lpq], S2P[lpq], CpSc[:, fo:fo + 1],
                                                         S1P[lpq], ALU.mult, ALU.add))
                        vchain(v, v.tensor_scalar(B2[:, fo * 512:(fo + 1) * 512], S2P[lpq],
                                                  B1C[:, fo:fo + 1], 0.0, ALU.add, ALU.max))
                        s1free[lpq] = vm.n
                    VE(_)

            for dog in range(2):
                for fth in range(2):
                    def _(g, t512=t512, dog=dog, fth=fth):
                        if not (t512 == 0 and dog == 0 and fth == 0):
                            g.wait_ge(tm.h, w2_used[0])
                        lg.inc(g.dma_start(
                            out=w2b[:].rearrange("q (a n) -> q a n", a=16),
                            in_=W2.rearrange("(a q) n -> q a n", q=128)[:, fth * 16:(fth + 1) * 16, dog * 512:(dog + 1) * 512]), 16)
                    GP(_)
                    lg_w2 = lg.n

                    def _(t, t512=t512, dog=dog, fth=fth, lg_w2=lg_w2):
                        t.wait_ge(lg.h, lg_w2)
                        if fth == 0:
                            t.wait_ge(vm.h, vm.n)
                        ins = None
                        for ft16 in range(16):
                            ftg = fth * 16 + ft16
                            for do4 in range(4):
                                ins = t.matmul(bank(do4),
                                               w2b[:, ft16 * 512 + do4 * 128: ft16 * 512 + (do4 + 1) * 128],
                                               B2[:, ftg * 512:(ftg + 1) * 512],
                                               start=(ftg == 0), stop=(ftg == 31))
                        tm.inc(ins)
                        w2_used[0] = tm.n
                    TE(_)

                for do4 in range(4):
                    do = dog * 4 + do4

                    def _(sync, t512=t512, do=do):
                        sync.wait_ge(vm.h, vm.n)
                        sync.wait_ge(*stX[t512 * 8 + do])
                        ld.inc(sync.dma_start(out=ldscr[:, 0:512],
                                              in_=x1_dram[do * 128:(do + 1) * 128, t512 * 512:(t512 + 1) * 512]), 16)
                        sync.wait_ge(ld.h, ld.n)
                    SY(_)

                    def _(v, do=do, do4=do4, t512=t512):
                        v.wait_ge(ld.h, ld.n)
                        v.wait_ge(tm.h, tm.n)
                        if t512 == 0 and do == 0:
                            for _hh, _nn in p1b_st:
                                v.wait_ge(_hh, _nn)
                        vchain(v, v.scalar_tensor_tensor(YW, bank(do4), B2C[:, do:do + 1],
                                                         ldscr[:, 0:512], ALU.add, ALU.add))
                        vchain(v, v.tensor_copy(ybf2[:], YW))
                    VE(_)

                    def _(s, do=do):
                        s.wait_ge(vm.h, vm.n)
                        schain(s, s.activation(sqscr[:], ybf2[:], AF.Square))
                    SC(_)

                    def _(t, t512=t512, do=do):
                        t.wait_ge(sm.h, sm.n)
                        if do == 0:
                            t.wait_ge(vm.h, vm.n)
                        t.matmul(SB[0:1, 0:512], onesb[:], ybf2[:],
                                 start=(do == 0), stop=(do == 7))
                        ins = t.matmul(QB[0:1, 0:512], onesb[:], sqscr[:],
                                       start=(do == 0), stop=(do == 7))
                        tm.inc(ins)
                    TE(_)

                    def _(sync, t512=t512, do=do):
                        sync.wait_ge(vm.h, vm.n)
                        st.inc(sync.dma_start(out=y2_dram[do * 128:(do + 1) * 128, t512 * 512:(t512 + 1) * 512],
                                              in_=YW), 16)
                        sync.wait_ge(st.h, st.n)
                    SY(_)

            def _(v, t512=t512):
                v.wait_ge(tm.h, tm.n)
                mu = rows[0:1, t512 * 512:(t512 + 1) * 512]
                vchain(v, v.tensor_scalar(mu, SB[0:1, 0:512], 1.0 / 1024.0, None, ALU.mult))
                vchain(v, v.tensor_scalar(rscr[0:1, 0:512], QB[0:1, 0:512], 1.0 / 1024.0, None, ALU.mult))
                vchain(v, v.tensor_mul(rscr[0:1, 512:1024], mu, mu))
                vchain(v, v.tensor_sub(rscr[0:1, 0:512], rscr[0:1, 0:512], rscr[0:1, 512:1024]))
            VE(_)

            def _(s, t512=t512):
                s.wait_ge(vm.h, vm.n)
                schain(s, s.activation(rscr[0:1, 512:1024], rscr[0:1, 0:512], AF.Sqrt, bias=epsc[0:1, 0:1]))
            SC(_)

            def _(v, t512=t512):
                v.wait_ge(sm.h, sm.n)
                vchain(v, v.reciprocal(rows[0:1, 2048 + t512 * 512: 2048 + (t512 + 1) * 512],
                                       rscr[0:1, 512:1024]))
            VE(_)

        vmAp2 = {}
        ldnB2 = {}
        stX2 = {}
        p2_vm = vm.n
        p2_sg = [(sh.h, sh.n), (sh2.h, sh2.n)]
        for nc4 in range(4):
            def _(t, nc4=nc4):
                t.wait_ge(vm.h, vm.n)
                t.matmul(UB[:, 0:512], onesf[:], rows[0:1, nc4 * 512:(nc4 + 1) * 512],
                         start=True, stop=True)
                ins = t.matmul(QB[:, 0:512], onesf[:], rows[0:1, 2048 + nc4 * 512: 2048 + (nc4 + 1) * 512],
                               start=True, stop=True)
                tm.inc(ins)
            TE(_)
            tmUB2 = tm.n

            for et in range(8):
                gi = nc4 * 8 + et
                lp = gi % 2

                def _(sync, nc4=nc4, et=et, gi=gi, lp=lp):
                    sync.wait_ge(vm.h, vmAp2.get(gi - 2, p2_vm))
                    sync.wait_ge(ld.h, ld.n)
                    ld.inc(sync.dma_start(out=ldscr[:, lp * 512:(lp + 1) * 512],
                                          in_=y2_dram[et * 128:(et + 1) * 128, nc4 * 512:(nc4 + 1) * 512]), 16)
                SY(_)
                ldnB2[gi] = ld.n

                def _(v, nc4=nc4, et=et, gi=gi, lp=lp, tmUB2=tmUB2):
                    v.wait_ge(ld.h, ldnB2[gi])
                    v.wait_ge(tm.h, tmUB2)
                    for _hh, _nn in ([stX2[gi - 2]] if gi >= 2 else p2_sg):
                        v.wait_ge(_hh, _nn)
                    vchain(v, v.tensor_sub(EVAC[lp], ldscr[:, lp * 512:(lp + 1) * 512], UB[:, 0:512]))
                    vchain(v, v.tensor_mul(YWP[lp], EVAC[lp], QB[:, 0:512]))
                    vchain(v, v.tensor_copy(xbf[:, et * TOK + nc4 * 512: et * TOK + (nc4 + 1) * 512],
                                            YWP[lp]))
                VE(_)
                vmAp2[gi] = vm.n

                def _(s, nc4=nc4, et=et, gi=gi, lp=lp):
                    s.wait_ge(vm.h, vmAp2[gi])
                    s.wait_ge(shp[lp].h, shp[lp].n)
                    shp[lp].inc(s.dma_start(out=x2T[et * 128:(et + 1) * 128, nc4 * 512:(nc4 + 1) * 512],
                                            in_=YWP[lp]), 16)
                SC(_)
                stX2[gi] = (shp[lp].h, shp[lp].n)

        def _(v):
            v.wait_ge(vm.h, vm.n)
            ap = B1[:].rearrange("q (t h c) -> q t h c", t=16, h=16, c=65)
            vchain(v, v.memset(ap[:, :, :, 64:65], 1.0))
        VE(_)

        for pi, W in enumerate((Wq, Wk1, Wv1)):
            def _(g, pi=pi, W=W):
                g.wait_ge(tm.h, tm.n)
                for dt in range(8):
                    lg.inc(g.dma_start(out=wsq[:, dt * 1024:(dt + 1) * 1024],
                                       in_=W[dt * 128:(dt + 1) * 128, :]), 16)
            GP(_)
            lg_w = lg.n

            for et in range(8):
                for nc4 in range(4):
                    gi = et * 4 + nc4

                    def _(t, pi=pi, et=et, nc4=nc4, gi=gi, lg_w=lg_w):
                        t.wait_ge(lg.h, lg_w)
                        t.wait_ge(sm.h, sm.n)
                        t.wait_ge(vm.h, vm.n)
                        ins = None
                        for dt in range(8):
                            ins = t.matmul(bank(gi),
                                           wsq[:, dt * 1024 + et * 128: dt * 1024 + (et + 1) * 128],
                                           xbf[:, dt * TOK + nc4 * 512: dt * TOK + (nc4 + 1) * 512],
                                           start=(dt == 0), stop=(dt == 7))
                        tm.inc(ins)
                    TE(_)

                    if pi < 2:
                        def _(s, pi=pi, et=et, nc4=nc4, gi=gi):
                            s.wait_ge(tm.h, tm.n)
                            if nc4 == 0:
                                s.wait_ge(st.h, st.n)
                            schain(s, s.activation(
                                B2[:, (et % 2) * 2048 + nc4 * 512:(et % 2) * 2048 + (nc4 + 1) * 512],
                                bank(gi), AF.Copy))
                        SC(_)

                        if nc4 == 3:
                            def _(sync, pi=pi, et=et):
                                sync.wait_ge(sm.h, sm.n)
                                dst = (q_fm[et * 128:(et + 1) * 128, :] if pi == 0
                                       else io["kq_st"](et))
                                st.inc(sync.dma_start(out=dst,
                                                      in_=B2[:, (et % 2) * 2048:(et % 2) * 2048 + 2048]), 16)
                                sync.wait_ge(st.h, st.n)
                            SY(_)
                    else:
                        def _(s, et=et, nc4=nc4, gi=gi):
                            s.wait_ge(tm.h, tm.n)
                            schain(s, s.activation(B2[:, nc4 * 512:(nc4 + 1) * 512], bank(gi), AF.Copy))
                        SC(_)

                        for tt in range(4):
                            def _(t, et=et, nc4=nc4, tt=tt):
                                t.wait_ge(sm.h, sm.n)
                                t.wait_ge(vm.h, vm.n)
                                ins = t.transpose(ps_tr[:, 0:128],
                                                  B2[:, nc4 * 512 + tt * 128: nc4 * 512 + (tt + 1) * 128],
                                                  ident[:])
                                tm.inc(ins)
                            TE(_)

                            def _(v, et=et, nc4=nc4, tt=tt):
                                v.wait_ge(tm.h, tm.n)
                                blk = nc4 * 4 + tt
                                vchain(v, v.tensor_copy(
                                    B1[:, blk * 1040 + (2 * et) * 65: blk * 1040 + (2 * et) * 65 + 64],
                                    ps_tr[:, 0:64]))
                                vchain(v, v.tensor_copy(
                                    B1[:, blk * 1040 + (2 * et + 1) * 65: blk * 1040 + (2 * et + 1) * 65 + 64],
                                    ps_tr[:, 64:128]))
                            VE(_)

        for blk in range(16):
            def _(g, blk=blk):
                g.wait_ge(vm.h, vm.n)
                for half in range(2):
                    sg.inc(g.dma_start(out=io["vt_st"](blk, half),
                                       in_=B1[:, blk * 1040 + half * 520: blk * 1040 + (half + 1) * 520]), 16)
                g.wait_ge(sg.h, sg.n)
            GP(_)

    es.close()
    return EQ


def _flush(nc, EQs):
    if not isinstance(EQs, list):
        EQs = [EQs]
    with nc.Block() as block:
        block.sync(lambda e: [q["sync"].replay(e) for q in EQs])
        block.gpsimd(lambda e: [q["gpsimd"].replay(e) for q in EQs])
        block.vector(lambda e: [q["vector"].replay(e) for q in EQs])
        block.scalar(lambda e: [q["scalar"].replay(e) for q in EQs])
        block.tensor(lambda e: [q["tensor"].replay(e) for q in EQs])


def _declare_io(nc, io, inputs, outputs):
    DT = {"F32": F32, "BF16": BF16, "I32": I32}
    for n, shp, dt in inputs:
        io[n] = nc.dram_tensor(n, shp, DT[dt], kind="ExternalInput").ap()
    for n, shp, dt in outputs:
        io[n] = nc.dram_tensor(n, shp, DT[dt], kind="ExternalOutput").ap()


def build_launch_C():
    nc = bass.Bass("TRN2", target_bir_lowering=False)
    io = {}
    _declare_io(nc, io, C_INPUTS, C_OUTPUTS)
    io["wkv_load"] = lambda g, dst, ft, th: g.dma_start(
        out=dst, in_=io["wkv_in"][ft * 128:(ft + 1) * 128, th * 1024:(th + 1) * 1024])
    io["kq_st"] = lambda et: io["k_fm"][et * 128:(et + 1) * 128, :]
    io["vt_st"] = lambda blk, half: io["v_tm"][blk * 128:(blk + 1) * 128,
                                               half * 520:(half + 1) * 520]
    EQ = emit_C(nc, io)
    _flush(nc, EQ)
    return nc


SWEEPS = [3584, 4096, 7680, 8192]
QGOFF = [0, 28, 60, 120]
NSLOT = 184


def qg_positions(core):
    b, h0, t0 = core_chunks(core)
    return sorted([h0, h0 + 512, t0, t0 + 512])


D_INPUTS = [("q_in", [D, TOK], "BF16"), ("kdiag", [D, TOK], "BF16"),
            ("vdiag", [TOK, NH * 65], "BF16"), ("bias_tbl", [128, NH * NSLOT], "F32"),
            ("dcol_in", [128, NH], "F32"), ("fvb_in", [128, NH], "F32"),
            ("x2T_in", [D, TOK], "F32"), ("Wo1", [D, D], "F32"),
            ("w1T", [128, 8 * FF], "F32"), ("w2T", [128, 8 * FF], "F32"),
            ("h1T", [128, 32 * FF], "F32"), ("h2T", [128, 32 * FF], "F32"),
            ("w3T", [128, 8 * FF], "F32"),
            ("s1c_in", [128, 32], "F32"), ("s2c_in", [128, 32], "F32"),
            ("sh1_in", [128, 32], "F32"), ("sh2_in", [128, 32], "F32"),
            ("s3c_in", [128, 8], "F32")]
D_OUTPUTS = [("x4T", [D, TOK], "F32")]


def emit_D(nc, io):
    es = ExitStack()
    (q_in, kdiag, vdiag, bias_tbl, dcol_in, fvb_in, x2T_in, Wo1, w1T, w2T,
     h1T, h2T, w3T, s1c_in, s2c_in, sh1_in, sh2_in, s3c_in) = (io[n] for n, _, _ in D_INPUTS)
    x4T = io["x4T"]
    kf_ap = io["kf_ap"]
    vt_ap = io["vt_ap"]

    y3_dram = nc.dram_tensor("y3_dram", [D, TOK], F32, kind="Internal").ap()
    x3bf_dram = nc.dram_tensor("x3bf_dram", [D, TOK], BF16, kind="Internal").ap()
    x3_dram = nc.dram_tensor("x3_dram", [D, TOK], F32, kind="Internal").ap()
    y4_dram = nc.dram_tensor("y4_dram", [D, TOK], F32, kind="Internal").ap()

    xbfp = es.enter_context(nc.sbuf_tensor("d_xbfp", [128, 2 * 512], BF16))
    fscr = es.enter_context(nc.sbuf_tensor("d_fscr", [128, 4 * 512], F32))
    ldscr = es.enter_context(nc.sbuf_tensor("d_ldscr", [128, 2 * 512], F32))
    sqscr = es.enter_context(nc.sbuf_tensor("d_sqscr", [128, 2 * 512], BF16))
    ybf2 = es.enter_context(nc.sbuf_tensor("d_ybf2", [128, 2 * 512], BF16))
    rscr = es.enter_context(nc.sbuf_tensor("d_rscr", [1, 2 * 512], F32))
    onesb = es.enter_context(nc.sbuf_tensor("d_onesb", [128, 1], BF16))
    onesf = es.enter_context(nc.sbuf_tensor("d_onesf", [1, 128], F32))
    epsc = es.enter_context(nc.sbuf_tensor("d_epsc", [128, 1], F32))
    scols = es.enter_context(nc.sbuf_tensor("d_scols", [128, 136], F32))

    ps = es.enter_context(nc.psum_tensor("d_ps", [128, 8 * 512], F32))

    ld = Sem(nc, es, "ld")
    lg = Sem(nc, es, "lg")
    lk = [Sem(nc, es, "lka"), Sem(nc, es, "lkb")]
    lv = [Sem(nc, es, "lva"), Sem(nc, es, "lvb")]
    lw = [Sem(nc, es, "lwa"), Sem(nc, es, "lwb")]
    vm = Sem(nc, es, "vm")
    sm = Sem(nc, es, "sm")
    tm = Sem(nc, es, "tm")
    st = Sem(nc, es, "st")
    gm = Sem(nc, es, "gm")
    sg = Sem(nc, es, "d_sg")
    sg2 = Sem(nc, es, "d_sg2")
    st2 = Sem(nc, es, "d_st2")
    sh = Sem(nc, es, "d_sh")
    sh2 = Sem(nc, es, "d_sh2")

    S1C = scols[:, 0:32]
    S2C = scols[:, 32:64]
    SH1C = scols[:, 64:96]
    SH2C = scols[:, 96:128]
    S3C = scols[:, 128:136]

    YT, YW = fscr[:, 0:512], fscr[:, 512:1024]
    YT2, YW2 = fscr[:, 1024:1536], fscr[:, 1536:2048]

    def bank(i):
        return ps[:, (i % 4) * 512:(i % 4 + 1) * 512]

    SCB = [ps[:, (4 + i) * 512:(5 + i) * 512] for i in range(4)]
    SB = ps[:, 6 * 512:7 * 512]
    QB = ps[:, 7 * 512:8 * 512]

    es2 = ExitStack()
    es3 = ExitStack()
    att = es3.enter_context(nc.sbuf_tensor("d_att", [128, 8 * TOK], BF16, side="right"))
    q_sb = es2.enter_context(nc.sbuf_tensor("d_q_sb", [128, 2 * 2 * TOK], BF16, side="right"))
    fv = es2.enter_context(nc.sbuf_tensor("d_fv", [128, 16 * 512], BF16, side="right"))
    gt = es2.enter_context(nc.sbuf_tensor("d_gt", [128, 4 * 512], F32, side="right"))
    msk = es2.enter_context(nc.sbuf_tensor("d_msk", [128, 512], F32, side="right"))
    dti = es2.enter_context(nc.sbuf_tensor("d_dti", [128, 5 * 512], F32, side="right"))
    iscr2 = es2.enter_context(nc.sbuf_tensor("d_iscr2", [128, 512], I32, side="right"))
    btab = es2.enter_context(nc.sbuf_tensor("d_btab", [128, 2 * 4 * NSLOT], F32, side="right"))
    dcol = es2.enter_context(nc.sbuf_tensor("d_dcol", [128, NH], F32, side="right"))
    fvb = es2.enter_context(nc.sbuf_tensor("d_fvb", [128, NH], F32, side="right"))
    ksb = es2.enter_context(nc.sbuf_tensor("d_ksb", [128, 2 * 2 * 512], BF16, side="right"))
    vsb = es2.enter_context(nc.sbuf_tensor("d_vsb", [128, 2 * 4 * 260], BF16, side="right"))
    kdsb = es2.enter_context(nc.sbuf_tensor("d_kdsb", [128, 2 * 2 * TOK], BF16, side="right"))
    vdsb = es2.enter_context(nc.sbuf_tensor("d_vdsb", [128, 2 * 16 * 260], BF16, side="right"))
    psb = es2.enter_context(nc.sbuf_tensor("d_psb", [128, 4 * 512], BF16, side="right"))
    pdg = es2.enter_context(nc.sbuf_tensor("d_pdg", [128, 2 * 512], BF16, side="right"))
    praw = es2.enter_context(nc.sbuf_tensor("d_praw", [128, 2 * 512], F32, side="right"))
    blw = es2.enter_context(nc.sbuf_tensor("d_blw", [128, 4 * 512], BF16, side="right"))
    mrg4 = es2.enter_context(nc.sbuf_tensor("d_mrg4", [128, 4 * 512], F32, side="right"))

    kv_used = {0: 0, 1: 0}
    kv_seen = {0: False, 1: False}
    score_done = {}
    pv_done = {0: 0, 1: 0, 2: 0, 3: 0}
    praw_free = {0: 0, 1: 0}
    pdg_free = {0: 0, 1: 0}
    pvh = {}
    rscr_free = {0: 0, 1: 0}
    hgtm = {0: 0, 1: 0}
    hgsm = {0: 0, 1: 0}
    bank_free = {0: 0, 1: 0, 2: 0, 3: 0}
    sci_ctr = [0]

    for _i in range(4):
        score_done[_i] = (sm.h, 0)

    EQ = {n_: EngQ() for n_ in ("sync", "gpsimd", "vector", "scalar", "tensor")}
    SY = lambda f: f(EQ["sync"])
    GP = lambda f: f(EQ["gpsimd"])
    VE = lambda f: f(EQ["vector"])
    SC = lambda f: f(EQ["scalar"])
    TE = lambda f: f(EQ["tensor"])
    if True:
        def vchain(v, ins):
            vm.inc(ins)
            v.wait_ge(vm.h, vm.n)

        def schain(s, ins):
            sm.inc(ins)
            s.wait_ge(sm.h, sm.n)

        def _(sync):
            for t_, dst in ((dcol_in, dcol[:]), (fvb_in, fvb[:]),
                            (s1c_in, S1C), (s2c_in, S2C), (sh1_in, SH1C),
                            (sh2_in, SH2C), (s3c_in, S3C)):
                ld.inc(sync.dma_start(out=dst, in_=t_), 16)
                sync.wait_ge(ld.h, ld.n)
        SY(_)

        for toff in range(5):
            def _(g, toff=toff):
                g.wait_ge(vm.h, vm.n)
                if toff < 4:
                    gm.inc(g.iota(iscr2[:], pattern=[[-1, 512]], base=128 * toff + 127,
                                  channel_multiplier=1))
                else:
                    gm.inc(g.iota(iscr2[:], pattern=[[1, 512]], base=0,
                                  channel_multiplier=0))
            GP(_)

            def _(v, toff=toff):
                v.wait_ge(gm.h, gm.n)
                vchain(v, v.tensor_copy(dti[:, toff * 512:(toff + 1) * 512], iscr2[:]))
            VE(_)

        def _(v):
            vchain(v, v.memset(onesb[:], 1.0))
            vchain(v, v.memset(onesf[:], 1.0))
            vchain(v, v.memset(epsc[:], EPS_LN))
        VE(_)

        for hg in range(4):
            heads = [4 * hg + i for i in range(4)]

            for i4, h in enumerate(heads):
                def _(v, i4=i4, h=h):
                    v.wait_ge(ld.h, ld.n)
                    for toff in range(4):
                        fslc = fv[:, (i4 * 4 + toff) * 512:(i4 * 4 + toff + 1) * 512]
                        vchain(v, v.tensor_scalar(fslc, dti[:, toff * 512:(toff + 1) * 512],
                                                  float(np.float32(SLOPES[h])), fvb[:, h:h + 1],
                                                  ALU.mult, ALU.add))
                        vchain(v, v.tensor_scalar(fslc, fslc, 50.0, None, ALU.min))
                VE(_)

                def _(s, i4=i4, h=h):
                    s.wait_ge(vm.h, vm.n)
                    for toff in range(4):
                        fslc = fv[:, (i4 * 4 + toff) * 512:(i4 * 4 + toff + 1) * 512]
                        sm.inc(s.activation(fslc, fslc, AF.Exp))
                    schain(s, s.activation(gt[:, i4 * 512:(i4 + 1) * 512],
                                           dti[:, 4 * 512:5 * 512], AF.Exp,
                                           scale=float(np.float32(-SLOPES[h]))))
                SC(_)

                def _(v, i4=i4):
                    v.wait_ge(sm.h, sm.n)
                    for toff in range(4):
                        fslc = fv[:, (i4 * 4 + toff) * 512:(i4 * 4 + toff + 1) * 512]
                        vchain(v, v.tensor_scalar(msk[:], dti[:, toff * 512:(toff + 1) * 512],
                                                  127.5, None, ALU.is_lt))
                        vchain(v, v.tensor_mul(fslc, fslc, msk[:]))
                VE(_)

            par = hg % 2

            def _(g, hg=hg, par=par):
                g.wait_ge(tm.h, hgtm[par])
                g.wait_ge(sm.h, hgsm[par])
                g.wait_ge(lg.h, lg.n)
                for ct in range(2):
                    lg.inc(g.dma_start(out=kdsb[:, par * 2 * TOK + ct * TOK: par * 2 * TOK + (ct + 1) * TOK],
                                       in_=io["kdiag_ap"](hg, ct)), 16)
                    lg.inc(g.dma_start(out=q_sb[:, par * 2 * TOK + ct * TOK: par * 2 * TOK + (ct + 1) * TOK],
                                       in_=q_in[(2 * hg + ct) * 128:(2 * hg + ct + 1) * 128, :]), 16)
                lg.inc(g.dma_start(out=vdsb[:, par * 4160:(par + 1) * 4160].rearrange("p (a n) -> p a n", a=16),
                                   in_=io["vdiag_ap"](hg)), 16)
                lg.inc(g.dma_start(out=btab[:, par * 4 * NSLOT:(par + 1) * 4 * NSLOT],
                                   in_=bias_tbl[:, 4 * hg * NSLOT:(4 * hg + 4) * NSLOT]), 16)
            GP(_)
            lg_diag = lg.n

            for qg in range(4):
                nkv = SWEEPS[qg] // 512

                units = [(kc, sub, pr) for kc in range(nkv)
                         for sub in range(4) for pr in range(2)]
                NB = len(units)
                ubank = []
                for _u in range(NB):
                    sci_ctr[0] += 2
                    ubank.append((sci_ctr[0] % 4, (sci_ctr[0] + 1) % 4))
                kcln = {}
                tmS = {}
                smE = {}

                def emit_loads(kc, hg=hg, qg=qg):
                    pk = kc % 2

                    def _(sync, hg=hg, qg=qg, kc=kc, pk=pk):
                        if hg == 2 and qg == 0 and kc == 0 and "agw" in io:
                            io["agw"](sync)
                        if kv_seen[pk]:
                            sync.wait_ge(tm.h, kv_used[pk])
                            sync.wait_ge(lk[pk].h, lk[pk].n)
                            sync.wait_ge(lv[pk].h, lv[pk].n)
                        kv_seen[pk] = True
                        for ct in range(2):
                            lk[pk].inc(sync.dma_start(
                                out=ksb[:, (pk * 2 + ct) * 512:(pk * 2 + ct + 1) * 512],
                                in_=kf_ap((2 * hg + ct) * 128, kc)), 16)
                        lv[pk].inc(sync.dma_start(
                            out=vsb[:, pk * 1040:(pk + 1) * 1040].rearrange("p (a n) -> p a n", a=4),
                            in_=vt_ap(kc, hg)), 16)
                    SY(_)
                    kcln[kc] = (lk[pk].n, lv[pk].n)

                def emit_S(n, qg=qg):
                    kc, sub, pr = units[n]
                    pk = kc % 2
                    scA, scB = ubank[n]
                    lk_n, lv_n = kcln[kc]

                    def _(t, qg=qg, sub=sub, pr=pr, scA=scA, scB=scB, pk=pk,
                          lk_n=lk_n, lv_n=lv_n, lg_diag=lg_diag):
                        t.wait_ge(lk[pk].h, lk_n)
                        t.wait_ge(lv[pk].h, lv_n)
                        t.wait_ge(lg.h, lg_diag)
                        kt = ksb[:, (pk * 2 + pr) * 512 + sub * 128:
                                 (pk * 2 + pr) * 512 + (sub + 1) * 128]
                        qt = q_sb[:, par * 2 * TOK + pr * TOK + qg * 512:
                                  par * 2 * TOK + pr * TOK + (qg + 1) * 512]
                        t.wait_ge(*score_done[scA])
                        t.matmul(SCB[scA][:, 0:512], kt[0:64, :], qt[0:64, :],
                                 start=True, stop=True)
                        t.wait_ge(*score_done[scB])
                        ins = t.matmul(SCB[scB][:, 0:512], kt[64:128, :], qt[64:128, :],
                                       start=True, stop=True)
                        tm.inc(ins)
                        kv_used[pk] = tm.n
                    TE(_)
                    tmS[n] = tm.n

                def emit_exp(n, qg=qg, hg=hg):
                    kc, sub, pr = units[n]
                    slot = kc * 4 + sub
                    scA, scB = ubank[n]

                    def _(s, qg=qg, slot=slot, pr=pr, scA=scA, scB=scB, n=n):
                        s.wait_ge(tm.h, tmS[n])
                        for ih, sc in ((2 * pr, scA), (2 * pr + 1, scB)):
                            s.wait_ge(tm.h, pv_done[sc])
                            col = btab[:, par * 4 * NSLOT + ih * NSLOT + QGOFF[qg] + slot:
                                       par * 4 * NSLOT + ih * NSLOT + QGOFF[qg] + slot + 1]
                            sm.inc(s.activation(psb[:, sc * 512:(sc + 1) * 512],
                                                SCB[sc][:, 0:512], AF.Exp,
                                                scale=0.125, bias=col))
                            score_done[sc] = (sm.h, sm.n)
                    SC(_)
                    smE[n] = sm.n

                def emit_PV(n, nkv=nkv):
                    kc, sub, pr = units[n]
                    slot = kc * 4 + sub
                    pk = kc % 2
                    last = (slot == nkv * 4 - 1)
                    scA, scB = ubank[n]

                    def _(t, slot=slot, pr=pr, scA=scA, scB=scB, pk=pk, last=last, n=n):
                        t.wait_ge(sm.h, smE[n])
                        for ih, sc in ((2 * pr, scA), (2 * pr + 1, scB)):
                            ins = t.matmul(bank(ih)[0:65, :],
                                           vsb[:, pk * 1040 + (slot % 4) * 260 + ih * 65:
                                               pk * 1040 + (slot % 4) * 260 + ih * 65 + 65],
                                           psb[:, sc * 512:(sc + 1) * 512],
                                           start=(slot == 0), stop=last)
                            tm.inc(ins)
                            pv_done[sc] = tm.n
                        kv_used[pk] = tm.n
                    TE(_)

                emit_loads(0)
                emit_S(0)
                for n in range(NB):
                    emit_exp(n)
                    if n + 1 < NB:
                        if units[n + 1][0] != units[n][0]:
                            emit_loads(units[n + 1][0])
                        emit_S(n + 1)
                    emit_PV(n)

                for i4 in range(4):
                    def _(v, i4=i4):
                        v.wait_ge(tm.h, tm.n)
                        vchain(v, v.tensor_mul(blw[:, i4 * 512:(i4 + 1) * 512][0:65, :],
                                               bank(i4)[0:65, :],
                                               gt[:, i4 * 512:(i4 + 1) * 512][0:65, :]))
                    VE(_)

                dunits = [(toff, pr) for toff in range(4) for pr in range(2)]
                dbank = []
                for _u in range(8):
                    sci_ctr[0] += 2
                    dbank.append((sci_ctr[0] % 4, (sci_ctr[0] + 1) % 4))
                tmS2 = {}
                smE2 = {}
                vmD = {}

                def emit_S2(u, qg=qg, lg_diag=lg_diag):
                    toff, pr = dunits[u]
                    scA, scB = dbank[u]

                    def _(t, qg=qg, toff=toff, pr=pr, scA=scA, scB=scB, lg_diag=lg_diag, u=u):
                        t.wait_ge(lg.h, lg_diag)
                        if u == 0:
                            t.wait_ge(vm.h, vm.n)
                        kt = kdsb[:, par * 2 * TOK + pr * TOK + qg * 512 + toff * 128:
                                  par * 2 * TOK + pr * TOK + qg * 512 + (toff + 1) * 128]
                        qt = q_sb[:, par * 2 * TOK + pr * TOK + qg * 512:
                                  par * 2 * TOK + pr * TOK + (qg + 1) * 512]
                        t.wait_ge(*score_done[scA])
                        t.matmul(SCB[scA][:, 0:512], kt[0:64, :], qt[0:64, :],
                                 start=True, stop=True)
                        t.wait_ge(*score_done[scB])
                        ins = t.matmul(SCB[scB][:, 0:512], kt[64:128, :], qt[64:128, :],
                                       start=True, stop=True)
                        tm.inc(ins)
                    TE(_)
                    tmS2[u] = tm.n

                def emit_dexp(u, j2, hg=hg):
                    toff, pr = dunits[u]
                    ih = 2 * pr + j2
                    h = 4 * hg + ih
                    sc_i = dbank[u][j2]
                    pr_i = (toff * 4 + ih) % 2

                    def _(s, h=h, sc_i=sc_i, pr_i=pr_i, u=u):
                        s.wait_ge(tm.h, tmS2[u])
                        s.wait_ge(vm.h, praw_free[pr_i])
                        sm.inc(s.activation(praw[:, pr_i * 512:(pr_i + 1) * 512],
                                            SCB[sc_i][:, 0:512], AF.Exp,
                                            scale=0.125, bias=dcol[:, h:h + 1]))
                        score_done[sc_i] = (sm.h, sm.n)
                    SC(_)
                    smE2[(u, j2)] = sm.n

                def emit_dmul(u, j2):
                    toff, pr = dunits[u]
                    ih = 2 * pr + j2
                    pr_i = (toff * 4 + ih) % 2

                    def _(v, ih=ih, toff=toff, pr_i=pr_i, u=u, j2=j2):
                        v.wait_ge(sm.h, smE2[(u, j2)])
                        v.wait_ge(tm.h, pdg_free[pr_i])
                        vchain(v, v.tensor_mul(pdg[:, pr_i * 512:(pr_i + 1) * 512],
                                               praw[:, pr_i * 512:(pr_i + 1) * 512],
                                               fv[:, (ih * 4 + toff) * 512:(ih * 4 + toff + 1) * 512]))
                        praw_free[pr_i] = vm.n
                    VE(_)
                    vmD[(u, j2)] = vm.n

                def emit_dPV(u, j2, qg=qg):
                    toff, pr = dunits[u]
                    ih = 2 * pr + j2
                    pr_i = (toff * 4 + ih) % 2

                    def _(t, qg=qg, toff=toff, ih=ih, pr_i=pr_i, u=u, j2=j2):
                        t.wait_ge(vm.h, vmD[(u, j2)])
                        blk = qg * 4 + toff
                        ins = t.matmul(bank(ih)[0:65, :],
                                       vdsb[:, par * 4160 + blk * 260 + ih * 65:
                                           par * 4160 + blk * 260 + ih * 65 + 65],
                                       pdg[:, pr_i * 512:(pr_i + 1) * 512],
                                       start=(toff == 0), stop=(toff == 3))
                        tm.inc(ins)
                        pdg_free[pr_i] = tm.n
                    TE(_)
                    if toff == 3:
                        pvh[ih] = tm.n

                emit_S2(0)
                for u in range(8):
                    emit_dexp(u, 0)
                    emit_dmul(u, 0)
                    if u + 1 < 8:
                        emit_S2(u + 1)
                    emit_dPV(u, 0)
                    emit_dexp(u, 1)
                    emit_dmul(u, 1)
                    emit_dPV(u, 1)

                tmPV_all = tm.n
                vmM = {}
                smR = {}
                tmB = {}
                QBs = [SB, QB]
                for i4 in range(4):
                    h = heads[i4]
                    mpar = i4 % 2

                    def _(v, i4=i4, tmPV_all=tmPV_all):
                        v.wait_ge(tm.h, pvh[i4])
                        vchain(v, v.tensor_add(mrg4[:, i4 * 512:(i4 + 1) * 512][0:65, :],
                                               blw[:, i4 * 512:(i4 + 1) * 512][0:65, :],
                                               bank(i4)[0:65, :]))
                    VE(_)
                    vmM[i4] = vm.n

                    def _(v, i4=i4, mpar=mpar):
                        v.wait_ge(tm.h, rscr_free[mpar])
                        vchain(v, v.reciprocal(rscr[0:1, mpar * 512:(mpar + 1) * 512],
                                               mrg4[64:65, i4 * 512:(i4 + 1) * 512]))
                    VE(_)
                    smR[i4] = vm.n

                    def _(t, i4=i4, mpar=mpar):
                        t.wait_ge(vm.h, smR[i4])
                        t.wait_ge(*score_done[2 + mpar])
                        ins = t.matmul(QBs[mpar][0:64, :], onesf[0:1, 0:64],
                                       rscr[0:1, mpar * 512:(mpar + 1) * 512],
                                       start=True, stop=True)
                        tm.inc(ins)
                        rscr_free[mpar] = tm.n
                    TE(_)
                    tmB[i4] = tm.n

                    def _(v, i4=i4, h=h, qg=qg, mpar=mpar):
                        v.wait_ge(tm.h, tmB[i4])
                        at = att[:, (h // 2) * TOK + qg * 512:(h // 2) * TOK + (qg + 1) * 512]
                        vchain(v, v.tensor_mul(at[(h % 2) * 64:(h % 2) * 64 + 64, :],
                                               mrg4[0:64, i4 * 512:(i4 + 1) * 512],
                                               QBs[mpar][0:64, :]))
                        score_done[2 + mpar] = (vm.h, vm.n)
                    VE(_)

            hgtm[par] = tm.n
            hgsm[par] = sm.n

        es2.close()
        rows = es.enter_context(nc.sbuf_tensor("d_rows", [1, 4096], F32))
        wo_sb = es3.enter_context(nc.sbuf_tensor("d_wo_sb", [128, 8 * 1024], BF16, side="right"))

        def _(g):
            g.wait_ge(tm.h, tm.n)
            g.wait_ge(vm.h, vm.n)
            for dt in range(8):
                lg.inc(g.dma_start(out=wo_sb[:, dt * 1024:(dt + 1) * 1024],
                                   in_=Wo1[dt * 128:(dt + 1) * 128, :]), 16)
        GP(_)
        lg_wo = lg.n

        EVACD = [YT, YT2]
        YWD = [YW, YW2]
        sgp = [sg, sg2]
        shp = [sh, sh2]
        tmP3 = {}
        ldn3 = {}
        smEv3 = {}
        vmAd3 = {}
        smSq3 = {}
        ldfree3 = {0: 0, 1: 0}
        evfree3 = {0: 0, 1: 0}
        stY3 = {}
        rows_done3 = 0
        for nc4 in range(4):
            for et in range(8):
                gi = nc4 * 8 + et
                lp = gi % 2

                def _(t, nc4=nc4, et=et, gi=gi):
                    if gi == 0:
                        t.wait_ge(lg.h, lg_wo)
                        t.wait_ge(vm.h, vm.n)
                    if gi >= 4:
                        t.wait_ge(vm.h, bank_free[gi % 4])
                    ins = None
                    for dt in range(8):
                        ins = t.matmul(bank(gi),
                                       wo_sb[:, dt * 1024 + et * 128: dt * 1024 + (et + 1) * 128],
                                       att[:, dt * TOK + nc4 * 512: dt * TOK + (nc4 + 1) * 512],
                                       start=(dt == 0), stop=(dt == 7))
                    tm.inc(ins)
                TE(_)
                tmP3[gi] = tm.n

                def _(sync, nc4=nc4, et=et, lp=lp):
                    sync.wait_ge(vm.h, ldfree3[lp])
                    sync.wait_ge(ld.h, ld.n)
                    ld.inc(sync.dma_start(out=ldscr[:, lp * 512:(lp + 1) * 512],
                                          in_=x2T_in[et * 128:(et + 1) * 128,
                                                     nc4 * 512:(nc4 + 1) * 512]), 16)
                SY(_)
                ldn3[gi] = ld.n

                def _(s, gi=gi, lp=lp):
                    s.wait_ge(tm.h, tmP3[gi])
                    s.wait_ge(vm.h, evfree3[lp])
                    sm.inc(s.activation(EVACD[lp], bank(gi), AF.Copy))
                SC(_)
                smEv3[gi] = sm.n

                def _(v, nc4=nc4, et=et, gi=gi, lp=lp):
                    v.wait_ge(sm.h, smEv3[gi])
                    v.wait_ge(ld.h, ldn3[gi])
                    v.wait_ge(*stY3.get(gi - 2, (sh.h, 0)))
                    v.wait_ge(tm.h, tm.n)
                    vchain(v, v.tensor_add(YWD[lp], EVACD[lp], ldscr[:, lp * 512:(lp + 1) * 512]))
                    vchain(v, v.tensor_copy(ybf2[:, 0:512], YWD[lp]))
                    bank_free[gi % 4] = vm.n
                    ldfree3[lp] = vm.n
                    evfree3[lp] = vm.n
                VE(_)
                vmAd3[gi] = vm.n

                def _(s, gi=gi):
                    s.wait_ge(vm.h, vmAd3[gi])
                    s.wait_ge(tm.h, tm.n)
                    sm.inc(s.activation(sqscr[:, 0:512], ybf2[:, 0:512], AF.Square))
                SC(_)
                smSq3[gi] = sm.n

                def _(t, nc4=nc4, et=et, gi=gi, rd=rows_done3):
                    t.wait_ge(sm.h, smSq3[gi])
                    if et == 0:
                        t.wait_ge(vm.h, rd)
                    t.matmul(SB[0:1, 0:512], onesb[:], ybf2[:, 0:512],
                             start=(et == 0), stop=(et == 7))
                    ins = t.matmul(QB[0:1, 0:512], onesb[:], sqscr[:, 0:512],
                                   start=(et == 0), stop=(et == 7))
                    tm.inc(ins)
                TE(_)

                def _(s, nc4=nc4, et=et, gi=gi, lp=lp):
                    s.wait_ge(vm.h, vmAd3[gi])
                    s.wait_ge(shp[lp].h, shp[lp].n)
                    shp[lp].inc(s.dma_start(out=y3_dram[et * 128:(et + 1) * 128,
                                                        nc4 * 512:(nc4 + 1) * 512],
                                            in_=YWD[lp]), 16)
                SC(_)
                stY3[gi] = (shp[lp].h, shp[lp].n)

            def _(v, nc4=nc4):
                v.wait_ge(tm.h, tm.n)
                mu = rows[0:1, nc4 * 512:(nc4 + 1) * 512]
                vchain(v, v.tensor_scalar(mu, SB[0:1, 0:512], 1.0 / 1024.0, None, ALU.mult))
                vchain(v, v.tensor_scalar(rscr[0:1, 0:512], QB[0:1, 0:512], 1.0 / 1024.0, None, ALU.mult))
                vchain(v, v.tensor_mul(rscr[0:1, 512:1024], mu, mu))
                vchain(v, v.tensor_sub(rscr[0:1, 0:512], rscr[0:1, 0:512], rscr[0:1, 512:1024]))
            VE(_)

            def _(s):
                s.wait_ge(vm.h, vm.n)
                schain(s, s.activation(rscr[0:1, 512:1024], rscr[0:1, 0:512], AF.Sqrt,
                                       bias=epsc[0:1, 0:1]))
            SC(_)

            def _(v, nc4=nc4):
                v.wait_ge(sm.h, sm.n)
                vchain(v, v.reciprocal(rows[0:1, 2048 + nc4 * 512: 2048 + (nc4 + 1) * 512],
                                       rscr[0:1, 512:1024]))
            VE(_)
            rows_done3 = vm.n

        vmAp3 = {}
        ldnB3 = {}
        stX3 = {}
        p2_vmD = vm.n
        p2_sgD = [(sh.h, sh.n), (sh2.h, sh2.n)]
        for nc4 in range(4):
            def _(t, nc4=nc4):
                t.wait_ge(vm.h, vm.n)
                t.matmul(SB[:, 0:512], onesf[:], rows[0:1, nc4 * 512:(nc4 + 1) * 512],
                         start=True, stop=True)
                ins = t.matmul(QB[:, 0:512], onesf[:],
                               rows[0:1, 2048 + nc4 * 512: 2048 + (nc4 + 1) * 512],
                               start=True, stop=True)
                tm.inc(ins)
            TE(_)
            tmUB3 = tm.n

            for et in range(8):
                gi = nc4 * 8 + et
                lp = gi % 2

                def _(sync, nc4=nc4, et=et, gi=gi, lp=lp):
                    sync.wait_ge(vm.h, vmAp3.get(gi - 2, p2_vmD))
                    sync.wait_ge(*stY3[gi])
                    sync.wait_ge(ld.h, ld.n)
                    ld.inc(sync.dma_start(out=ldscr[:, lp * 512:(lp + 1) * 512],
                                          in_=y3_dram[et * 128:(et + 1) * 128,
                                                      nc4 * 512:(nc4 + 1) * 512]), 16)
                SY(_)
                ldnB3[gi] = ld.n

                def _(v, nc4=nc4, et=et, gi=gi, lp=lp, tmUB3=tmUB3):
                    v.wait_ge(ld.h, ldnB3[gi])
                    v.wait_ge(tm.h, tmUB3)
                    for _hh, _nn in ([stX3[gi - 2]] if gi >= 2 else p2_sgD):
                        v.wait_ge(_hh, _nn)
                    vchain(v, v.tensor_sub(EVACD[lp], ldscr[:, lp * 512:(lp + 1) * 512], SB[:, 0:512]))
                    vchain(v, v.tensor_mul(YWD[lp], EVACD[lp], QB[:, 0:512]))
                    vchain(v, v.tensor_copy(xbfp[:, lp * 512:(lp + 1) * 512], YWD[lp]))
                VE(_)
                vmAp3[gi] = vm.n

                def _(g, nc4=nc4, et=et, gi=gi, lp=lp):
                    g.wait_ge(vm.h, vmAp3[gi])
                    g.wait_ge(sgp[lp].h, sgp[lp].n)
                    sgp[lp].inc(g.dma_start(out=x3_dram[et * 128:(et + 1) * 128,
                                                        nc4 * 512:(nc4 + 1) * 512],
                                            in_=YWD[lp]), 16)
                    sgp[lp].inc(g.dma_start(out=x3bf_dram[et * 128:(et + 1) * 128,
                                                          nc4 * 512:(nc4 + 1) * 512],
                                            in_=xbfp[:, lp * 512:(lp + 1) * 512]), 16)
                GP(_)
                stX3[gi] = (sgp[lp].h, sgp[lp].n)
        ln3_sg = [(sg.h, sg.n), (sg2.h, sg2.n)]
        ln3_vm = vm.n
        p3_tm = tm.n
        vmS4 = {}
        smS4 = {}
        ldn4 = {}
        stY4 = {}
        stats4n = {}
        rows4_done = 0

        es3.close()
        ffscr = es.enter_context(nc.sbuf_tensor("d_ffscr", [128, 1024], BF16, side="right"))
        AH = ffscr[:, 0:512]
        BH = ffscr[:, 512:1024]
        h_sb = es.enter_context(nc.sbuf_tensor("d_h_sb", [128, 32 * 1024], BF16, side="right"))
        g_sb = es.enter_context(nc.sbuf_tensor("d_g_sb", [128, 32 * 1024], BF16, side="right"))
        xq_sb = es.enter_context(nc.sbuf_tensor("d_xq_sb", [128, 8 * 1024], BF16, side="right"))
        wfb = es.enter_context(nc.sbuf_tensor("d_wfb", [128, 2 * 2 * 8 * 128], BF16, side="right"))
        whb = es.enter_context(nc.sbuf_tensor("d_whb", [128, 2 * 8 * 128], BF16, side="right"))
        w3b = es.enter_context(nc.sbuf_tensor("d_w3b", [128, 2 * 4 * 512], BF16, side="right"))

        wf_used = {0: 0, 1: 0}
        lwf_tot = {}
        wh_used = {0: 0, 1: 0, 2: 0, 3: 0}
        lwh_tot = {0: 0, 1: 0, 2: 0, 3: 0}
        w3_used = {0: 0, 1: 0}
        hfree = {0: 0, 1: 0}
        gfree = {0: 0, 1: 0}

        s1_tm = {0: 0}
        lgx = {}
        for th_ in range(2):
            tb = th_ * 1024

            def _(g, th_=th_, tb=tb):
                if th_ == 0:
                    for _hh, _nn in ln3_sg:
                        g.wait_ge(_hh, _nn)
                else:
                    g.wait_ge(tm.h, s1_tm[0])
                g.wait_ge(lg.h, lg.n)
                for dt in range(8):
                    lg.inc(g.dma_start(out=xq_sb[:, dt * 1024:(dt + 1) * 1024],
                                       in_=x3bf_dram[dt * 128:(dt + 1) * 128, tb:tb + 1024]), 16)
            GP(_)
            lgx[th_] = lg.n

            for fo in range(32):
                fset = fo % 2
                for wi, WT in enumerate((w1T, w2T)):
                    def _(g, th_=th_, fo=fo, wi=wi, WT=WT, fset=fset):
                        if th_ * 32 + fo >= 2:
                            g.wait_ge(tm.h, wf_used[fset])
                        g.wait_ge(lw[wi].h, lw[wi].n)
                        for hx in range(2):
                            lw[wi].inc(g.dma_start(
                                out=wfb[:, (fset * 2 + wi) * 1024 + hx * 512:
                                    (fset * 2 + wi) * 1024 + (hx + 1) * 512],
                                in_=WT[:, fo * 1024 + hx * 512: fo * 1024 + (hx + 1) * 512]), 16)
                    GP(_)
                    lwf_tot[wi] = lw[wi].n

                def _(t, th_=th_, fo=fo, lwa=lwf_tot[0], lwb=lwf_tot[1], tb=tb, fset=fset):
                    t.wait_ge(lw[0].h, lwa)
                    t.wait_ge(lw[1].h, lwb)
                    if fo == 0:
                        t.wait_ge(lg.h, lgx[th_])
                        t.wait_ge(vm.h, vm.n)
                        t.wait_ge(sm.h, sm.n)
                    elif th_ * 32 + fo >= 2:
                        t.wait_ge(vm.h, hfree[fo % 2])
                    bsel = (fo % 2) * 4
                    ins = None
                    for wi in range(2):
                        for half in range(2):
                            bk = bsel + wi * 2 + half
                            for dt in range(8):
                                ins = t.matmul(
                                    ps[:, bk * 512:(bk + 1) * 512],
                                    wfb[:, (fset * 2 + wi) * 1024 + dt * 128:
                                        (fset * 2 + wi) * 1024 + (dt + 1) * 128],
                                    xq_sb[:, dt * 1024 + half * 512: dt * 1024 + (half + 1) * 512],
                                    start=(dt == 0), stop=(dt == 7))
                    tm.inc(ins)
                    wf_used[fset] = tm.n
                TE(_)

                for half in range(2):
                    def _(s, fo=fo, half=half):
                        s.wait_ge(tm.h, tm.n)
                        s.wait_ge(vm.h, vm.n)
                        bsel = (fo % 2) * 4
                        schain(s, s.activation(AH, ps[:, (bsel + half) * 512:(bsel + half + 1) * 512],
                                               AF.Silu, scale=S1C[:, fo:fo + 1]))
                    SC(_)

                    def _(v, fo=fo, half=half):
                        v.wait_ge(sm.h, sm.n)
                        bsel = (fo % 2) * 4
                        vchain(v, v.tensor_scalar(BH, ps[:, (bsel + 2 + half) * 512:(bsel + 3 + half) * 512],
                                                  S2C[:, fo:fo + 1], None, ALU.mult))
                        vchain(v, v.tensor_mul(h_sb[:, fo * 1024 + half * 512: fo * 1024 + (half + 1) * 512],
                                               AH, BH))
                        hfree[fo % 2] = vm.n
                    VE(_)

            s1_tm[0] = tm.n
            for si, (HT, SHC, SRC, DST) in enumerate(
                    ((h1T, SH1C, h_sb, g_sb), (h2T, SH2C, g_sb, h_sb))):
                for o2 in range(32):
                    for kih in range(4):
                        g2k = ((th_ * 2 + si) * 32 + o2) * 4 + kih
                        p4 = g2k % 2
                        p2 = g2k % 2

                        def _(g, o2=o2, kih=kih, HT=HT, p2=p2, p4=p4, g2k=g2k):
                            if g2k >= 2:
                                g.wait_ge(tm.h, wh_used[p4])
                            g.wait_ge(lw[p2].h, lw[p2].n)
                            for hx in range(2):
                                lw[p2].inc(g.dma_start(
                                    out=whb[:, p4 * 1024 + hx * 512: p4 * 1024 + (hx + 1) * 512],
                                    in_=HT[:, (o2 * 4 + kih) * 1024 + hx * 512:
                                        (o2 * 4 + kih) * 1024 + (hx + 1) * 512]), 16)
                        GP(_)
                        lwh_tot[p4] = lw[p2].n

                        def _(t, si=si, o2=o2, kih=kih, lwh=lwh_tot[p4], th_=th_, p2=p2, p4=p4):
                            t.wait_ge(lw[p2].h, lwh)
                            if kih == 0:
                                if th_ * 64 + si * 32 + o2 >= 2:
                                    t.wait_ge(sm.h, gfree[o2 % 2])
                                if o2 == 0:
                                    t.wait_ge(vm.h, vm.n)
                                    t.wait_ge(sm.h, sm.n)
                            bb = (o2 % 2) * 2
                            ins = None
                            for ki8 in range(8):
                                ki = kih * 8 + ki8
                                for half in range(2):
                                    ins = t.matmul(
                                        ps[:, (bb + half) * 512:(bb + half + 1) * 512],
                                        whb[:, p4 * 1024 + ki8 * 128: p4 * 1024 + (ki8 + 1) * 128],
                                        SRC[:, ki * 1024 + half * 512: ki * 1024 + (half + 1) * 512],
                                        start=(ki == 0), stop=(ki == 31))
                            tm.inc(ins)
                            wh_used[p4] = tm.n
                        TE(_)

                    def _(s, si=si, o2=o2, SHC=SHC, DST=DST):
                        s.wait_ge(tm.h, tm.n)
                        bb = (o2 % 2) * 2
                        sm.inc(s.activation(DST[:, o2 * 1024: o2 * 1024 + 512],
                                            ps[:, bb * 512:(bb + 1) * 512],
                                            AF.Gelu, scale=SHC[:, o2:o2 + 1]))
                        schain(s, s.activation(DST[:, o2 * 1024 + 512: o2 * 1024 + 1024],
                                               ps[:, (bb + 1) * 512:(bb + 2) * 512],
                                               AF.Gelu, scale=SHC[:, o2:o2 + 1]))
                        gfree[o2 % 2] = sm.n
                    SC(_)

            for hh in range(2):
                cb = tb + hh * 512
                tsl = th_ * 2 + hh
                for dog in range(2):
                    for kiq in range(8):
                        w3i = ((th_ * 2 + hh) * 2 + dog) * 8 + kiq
                        s3 = w3i % 2

                        def _(g, dog=dog, kiq=kiq, s3=s3, w3i=w3i):
                            if w3i >= 2:
                                g.wait_ge(tm.h, w3_used[s3])
                            g.wait_ge(lg.h, lg.n)
                            for hx in range(2):
                                lg.inc(g.dma_start(
                                    out=w3b[:, s3 * 2048 + hx * 1024: s3 * 2048 + (hx + 1) * 1024],
                                    in_=w3T[:, (dog * 8 + kiq) * 2048 + hx * 1024:
                                        (dog * 8 + kiq) * 2048 + (hx + 1) * 1024]), 16)
                        GP(_)
                        lg_w3 = lg.n

                        def _(t, dog=dog, kiq=kiq, lg_w3=lg_w3, cb=cb, tb=tb, s3=s3):
                            t.wait_ge(lg.h, lg_w3)
                            if kiq == 0:
                                t.wait_ge(sm.h, sm.n)
                                t.wait_ge(vm.h, vm.n)
                            ins = None
                            for ki4 in range(4):
                                kig = kiq * 4 + ki4
                                for do4 in range(4):
                                    ins = t.matmul(bank(do4),
                                                   w3b[:, s3 * 2048 + ki4 * 512 + do4 * 128:
                                                       s3 * 2048 + ki4 * 512 + (do4 + 1) * 128],
                                                   h_sb[:, kig * 1024 + (cb - tb): kig * 1024 + (cb - tb) + 512],
                                                   start=(kig == 0), stop=(kig == 31))
                            tm.inc(ins)
                            w3_used[s3] = tm.n
                        TE(_)

                    s4_tm = tm.n
                    for do4 in range(4):
                        do = dog * 4 + do4
                        di = ((th_ * 2 + hh) * 2 + dog) * 4 + do4
                        lp = di % 2

                        def _(sync, do=do, cb=cb, di=di, lp=lp):
                            sync.wait_ge(vm.h, vmS4.get(di - 2, ln3_vm))
                            sync.wait_ge(*stX3[(cb // 512) * 8 + do])
                            sync.wait_ge(ld.h, ld.n)
                            ld.inc(sync.dma_start(out=ldscr[:, lp * 512:(lp + 1) * 512],
                                                  in_=x3_dram[do * 128:(do + 1) * 128, cb: cb + 512]), 16)
                        SY(_)
                        ldn4[di] = ld.n

                        def _(v, do=do, do4=do4, di=di, lp=lp, s4_tm=s4_tm):
                            v.wait_ge(ld.h, ldn4[di])
                            v.wait_ge(tm.h, max(s4_tm, stats4n.get(di - 2, p3_tm)))
                            for _hh, _nn in ([stY4[di - 2]] if di >= 2 else ln3_sg):
                                v.wait_ge(_hh, _nn)
                            vchain(v, v.scalar_tensor_tensor(YWD[lp], bank(do4), S3C[:, do:do + 1],
                                                             ldscr[:, lp * 512:(lp + 1) * 512],
                                                             ALU.mult, ALU.add))
                            vchain(v, v.tensor_copy(ybf2[:, lp * 512:(lp + 1) * 512], YWD[lp]))
                        VE(_)
                        vmS4[di] = vm.n

                        def _(s, do=do, di=di, lp=lp):
                            s.wait_ge(vm.h, vmS4[di])
                            s.wait_ge(tm.h, stats4n.get(di - 2, p3_tm))
                            sm.inc(s.activation(sqscr[:, lp * 512:(lp + 1) * 512],
                                                ybf2[:, lp * 512:(lp + 1) * 512], AF.Square))
                        SC(_)
                        smS4[di] = sm.n

                        def _(t, do=do, di=di, lp=lp, rd=rows4_done):
                            t.wait_ge(sm.h, smS4[di])
                            if do == 0:
                                t.wait_ge(vm.h, rd)
                            t.matmul(SB[0:1, 0:512], onesb[:], ybf2[:, lp * 512:(lp + 1) * 512],
                                     start=(do == 0), stop=(do == 7))
                            ins = t.matmul(QB[0:1, 0:512], onesb[:], sqscr[:, lp * 512:(lp + 1) * 512],
                                           start=(do == 0), stop=(do == 7))
                            tm.inc(ins)
                        TE(_)
                        stats4n[di] = tm.n

                        def _(s, do=do, cb=cb, di=di, lp=lp):
                            s.wait_ge(vm.h, vmS4[di])
                            s.wait_ge(shp[lp].h, shp[lp].n)
                            shp[lp].inc(s.dma_start(out=y4_dram[do * 128:(do + 1) * 128, cb: cb + 512],
                                                    in_=YWD[lp]), 16)
                        SC(_)
                        stY4[di] = (shp[lp].h, shp[lp].n)

                def _(v, tsl=tsl):
                    v.wait_ge(tm.h, tm.n)
                    mu = rows[0:1, tsl * 512:(tsl + 1) * 512]
                    vchain(v, v.tensor_scalar(mu, SB[0:1, 0:512], 1.0 / 1024.0, None, ALU.mult))
                    vchain(v, v.tensor_scalar(rscr[0:1, 0:512], QB[0:1, 0:512], 1.0 / 1024.0, None, ALU.mult))
                    vchain(v, v.tensor_mul(rscr[0:1, 512:1024], mu, mu))
                    vchain(v, v.tensor_sub(rscr[0:1, 0:512], rscr[0:1, 0:512], rscr[0:1, 512:1024]))
                VE(_)

                def _(s):
                    s.wait_ge(vm.h, vm.n)
                    schain(s, s.activation(rscr[0:1, 512:1024], rscr[0:1, 0:512], AF.Sqrt,
                                           bias=epsc[0:1, 0:1]))
                SC(_)

                def _(v, tsl=tsl):
                    v.wait_ge(sm.h, sm.n)
                    vchain(v, v.reciprocal(rows[0:1, 2048 + tsl * 512: 2048 + (tsl + 1) * 512],
                                           rscr[0:1, 512:1024]))
                VE(_)
                rows4_done = vm.n

        vmP4 = {}
        ldnP4 = {}
        stP4 = {}
        stp4 = [st, st2]
        s4_vm_end = vm.n
        s4_sg_end = [(sh.h, sh.n), (sh2.h, sh2.n)]
        for nc4 in range(4):
            def _(t, nc4=nc4):
                t.wait_ge(vm.h, vm.n)
                t.matmul(SB[:, 0:512], onesf[:], rows[0:1, nc4 * 512:(nc4 + 1) * 512],
                         start=True, stop=True)
                ins = t.matmul(QB[:, 0:512], onesf[:],
                               rows[0:1, 2048 + nc4 * 512: 2048 + (nc4 + 1) * 512],
                               start=True, stop=True)
                tm.inc(ins)
            TE(_)
            tmUB4 = tm.n

            for et in range(8):
                gi = nc4 * 8 + et
                lp = gi % 2

                def _(sync, nc4=nc4, et=et, gi=gi, lp=lp):
                    sync.wait_ge(vm.h, vmP4.get(gi - 2, s4_vm_end))
                    sync.wait_ge(*stY4[nc4 * 8 + et])
                    sync.wait_ge(ld.h, ld.n)
                    ld.inc(sync.dma_start(out=ldscr[:, lp * 512:(lp + 1) * 512],
                                          in_=y4_dram[et * 128:(et + 1) * 128,
                                                      nc4 * 512:(nc4 + 1) * 512]), 16)
                SY(_)
                ldnP4[gi] = ld.n

                def _(v, nc4=nc4, et=et, gi=gi, lp=lp, tmUB4=tmUB4):
                    v.wait_ge(ld.h, ldnP4[gi])
                    v.wait_ge(tm.h, tmUB4)
                    if gi < 2:
                        for _hh, _nn in s4_sg_end:
                            v.wait_ge(_hh, _nn)
                    else:
                        v.wait_ge(*stP4[gi - 2])
                    vchain(v, v.tensor_sub(EVACD[lp], ldscr[:, lp * 512:(lp + 1) * 512], SB[:, 0:512]))
                    vchain(v, v.tensor_mul(YWD[lp], EVACD[lp], QB[:, 0:512]))
                VE(_)
                vmP4[gi] = vm.n

                def _(g, nc4=nc4, et=et, gi=gi, lp=lp):
                    g.wait_ge(vm.h, vmP4[gi])
                    g.wait_ge(sgp[lp].h, sgp[lp].n)
                    sgp[lp].inc(g.dma_start(out=x4T[et * 128:(et + 1) * 128,
                                                    nc4 * 512:(nc4 + 1) * 512],
                                            in_=YWD[lp]), 16)
                    stP4[gi] = (sgp[lp].h, sgp[lp].n)
                GP(_)

        def _(sync):
            sync.wait_ge(sg.h, sg.n)
            sync.wait_ge(sg2.h, sg2.n)
        SY(_)

    es.close()
    return EQ


def build_launch_D():
    nc = bass.Bass("TRN2", target_bir_lowering=False)
    io = {}
    _declare_io(nc, io, D_INPUTS + [("kf_in", [D, T], "BF16"), ("vt_in", [T, NH * 65], "BF16")],
                D_OUTPUTS)
    io["kf_ap"] = lambda rows0, kc: io["kf_in"][rows0:rows0 + 128, kc * 512:(kc + 1) * 512]
    io["vt_ap"] = lambda kc, hg: io["vt_in"].rearrange("(a p) n -> p a n", p=128)[
        :, kc * 4:(kc + 1) * 4, hg * 260:(hg + 1) * 260]
    io["kdiag_ap"] = lambda hg, ct: io["kdiag"][(2 * hg + ct) * 128:(2 * hg + ct + 1) * 128, :]
    io["vdiag_ap"] = lambda hg: io["vdiag"].rearrange("(a p) n -> p a n", p=128)[
        :, :, hg * 260:(hg + 1) * 260]
    EQ = emit_D(nc, io)
    _flush(nc, EQ)
    return nc



RG_BATCH = [[0, 1, 2, 3], [4, 5, 6, 7]]


def _inject_waits(EQ, waits):
    for q in EQ.values():
        q.calls[0:0] = [("wait_ge", (h, n), {}, None) for h, n in waits]


def build_fused():
    nc = bass.Bass("TRN2", target_bir_lowering=False)
    es = ExitStack()
    io = {}
    _declare_io(nc, io, A_INPUTS + [("td2", [128, 2], "F32"), ("tf2", [128, 2], "F32"),
                                    ("gat_idx", [128, 32], "I32")]
                + [x for x in C_INPUTS if x[0] not in ("xT", "sr_in", "wkv_in")]
                + [x for x in D_INPUTS if x[0] not in ("q_in", "kdiag", "vdiag", "x2T_in")],
                D_OUTPUTS)

    kvin = [nc.dram_tensor(f"kvin_{i}", [2048, 1024], F32, kind="Internal") for i in range(2)]
    kvag = [nc.dram_tensor(f"kvag_{i}", [4 * 2048, 1024], F32, kind="Internal") for i in range(2)]
    sr_dram = nc.dram_tensor("sr_dram", [D, TOK], F32, kind="Internal")
    x_own = nc.dram_tensor("x_own", [D, TOK], F32, kind="Internal")
    wkvo = [nc.dram_tensor(f"wkvo_{i}", [1024, 1024], F32, kind="Internal") for i in range(2)]
    wkvag = [nc.dram_tensor(f"wkvag_{i}", [4 * 1024, 1024], F32, kind="Internal") for i in range(2)]
    q_dram = nc.dram_tensor("q_dram", [D, TOK], BF16, kind="Internal")
    kqo = [nc.dram_tensor(f"kqo_{i}", [D // 2, TOK], BF16, kind="Internal") for i in range(2)]
    vto = [nc.dram_tensor(f"vto_{i}", [TOK, 520], BF16, kind="Internal") for i in range(2)]
    x2_dram = nc.dram_tensor("x2_dram", [D, TOK], F32, kind="Internal")
    kq_all = [nc.dram_tensor(f"kq_all_{i}", [4 * D // 2, TOK], BF16, kind="Internal") for i in range(2)]
    vt_all = [nc.dram_tensor(f"vt_all_{i}", [4 * TOK, 520], BF16, kind="Internal") for i in range(2)]

    cc = Sem(nc, es, "cc")
    gidx = es.enter_context(nc.sbuf_tensor("gidx", [128, 32], I32))

    def phase_sems(mark):
        return [(s.h, s.n) for s in Sem._all[mark:] if s.n > 0]

    ioA = {n: io[n] for n, _, _ in A_INPUTS}

    def kv_dst(pi, et):
        assert pi == 2
        return sr_dram.ap()[et * 128:(et + 1) * 128, :]

    def kv_sth(pi, et, half):
        j, sub = et // 2, (et % 2) * 128
        base = 512 * j + (0 if pi == 0 else 256) + sub
        return kvin[half].ap()[base:base + 128, :]
    ioA["kv_dst"] = kv_dst
    ioA["kv_sth"] = kv_sth
    markA = len(Sem._all)
    EQ_A, stA = emit_A(nc, ioA)
    gq = EQ_A["gpsimd"]
    for dst, src in ((gidx[:, :], io["gat_idx"]),
                     (x_own.ap()[:, 0:1024], io["xT"][:, 1:1025]),
                     (x_own.ap()[:, 1024:2048], io["xT"][:, 1026:2050])):
        fi = FakeIns()
        gq.calls.append(("dma_start", (), {"out": dst, "in_": src}, fi))
        fi.incs.append((cc.h, 16))
        cc.n += 16
    gq.wait_ge(stA.h, 512)
    for half in range(2):
        fi3 = FakeIns()
        gq.calls.append(("collective_compute", ("AllGather", ALU.bypass),
                         {"replica_groups": RG_BATCH, "ins": [kvin[half].ap()],
                          "outs": [kvag[half].ap()]}, fi3))
        fi3.incs.append((cc.h, 1))
        cc.n += 1
        if half == 0:
            cc_kvh = cc.n
    cc_kvt = cc.n
    wA = phase_sems(markA) + [(cc.h, cc_kvh)]

    ioB = {"td2": io["td2"], "tf2": io["tf2"]}

    def kv_load(g, dst, kv, dt2, c, half):
        g_ = 2 * c + half
        s, lc = _g2src(g_)
        col = s * 4 + kv * 2 + dt2
        if lc:
            g.wait_ge(cc.h, cc_kvt)
        return g.indirect_dma_start(dst, None, kvag[lc // 1024].ap(),
                                    IndirectOffsetOnAxis(gidx[:, col:col + 1], 0))

    def wkv_dst(dt2, c, half):
        g_ = 2 * c + half
        return wkvo[g_ // 4].ap()[256 * (g_ % 4) + dt2 * 128: 256 * (g_ % 4) + dt2 * 128 + 128, :]
    ioB["kv_load"] = kv_load
    ioB["wkv_dst"] = wkv_dst
    ccs = {}

    def wkv_ag_hook(g):
        ins = g.collective_compute("AllGather", ALU.bypass,
                                   replica_groups=RG_BATCH, ins=[wkvo[0].ap()],
                                   outs=[wkvag[0].ap()])
        cc.inc(ins, 1)
        ccs["a"] = cc.n
    ioB["wkv_ag_hook"] = wkv_ag_hook
    markB = len(Sem._all)
    EQ_B, stB = emit_B(nc, ioB)
    _inject_waits(EQ_B, wA)
    gq = EQ_B["gpsimd"]
    gq.wait_ge(stB.h, stB.n)
    fi = FakeIns()
    gq.calls.append(("collective_compute", ("AllGather", ALU.bypass),
                     {"replica_groups": RG_BATCH, "ins": [wkvo[1].ap()],
                      "outs": [wkvag[1].ap()]}, fi))
    fi.incs.append((cc.h, 1))
    cc.n += 1
    ccs["b"] = cc.n
    wB = phase_sems(markB) + [(cc.h, ccs["a"])]

    ioC = {n: io[n] for n, _, _ in C_INPUTS if n in io}
    ioC["xT"] = x_own.ap()
    ioC["sr_in"] = sr_dram.ap()
    ioC["wkv_in"] = wkvag[0].ap()

    def wkv_load(g, dst, ft, th):
        g.wait_ge(cc.h, ccs["a"] if th == 0 else ccs["b"])
        return g.indirect_dma_start(dst, None, wkvag[th].ap(),
                                    IndirectOffsetOnAxis(gidx[:, 16 + ft * 2 + th:17 + ft * 2 + th], 0))
    ioC["wkv_load"] = wkv_load
    ioC["x2T"] = x2_dram.ap()
    ioC["q_fm"] = q_dram.ap()
    ioC["k_fm"] = kqo[0].ap()
    ioC["v_tm"] = vto[0].ap()
    ioC["kq_st"] = lambda et: kqo[et // 4].ap()[(et % 4) * 128:(et % 4 + 1) * 128, :]
    ioC["vt_st"] = lambda blk, half: vto[half].ap()[blk * 128:(blk + 1) * 128, :]
    markC = len(Sem._all)
    EQ_C = emit_C(nc, ioC)
    _inject_waits(EQ_C, wB)
    gq = EQ_C["gpsimd"]
    for h, n in phase_sems(markC):
        gq.wait_ge(h, n)
    for half in (0, 1):
        for src_t, dst_t in ((kqo[half], kq_all[half]), (vto[half], vt_all[half])):
            fi = FakeIns()
            gq.calls.append(("collective_compute", ("AllGather", ALU.bypass),
                             {"replica_groups": RG_BATCH, "ins": [src_t.ap()],
                              "outs": [dst_t.ap()]}, fi))
            fi.incs.append((cc.h, 1))
            cc.n += 1
        if half == 0:
            cc_half = cc.n
    wC_nocc = phase_sems(markC)
    wC = wC_nocc + [(cc.h, cc_half)]
    cc_full = cc.n

    ioD = {n: io[n] for n, _, _ in D_INPUTS if n in io}
    ioD["q_in"] = q_dram.ap()
    ioD["kdiag"] = kqo[0].ap()
    ioD["vdiag"] = vto[0].ap()
    ioD["x2T_in"] = x2_dram.ap()
    ioD["x4T"] = io["x4T"]

    def kf_ap(rows0, kc):
        g_ = kc // 2
        s, lc0 = _g2src(g_)
        lc = lc0 + (kc % 2) * 512
        hv, r = divmod(rows0, 512)
        return kq_all[hv].ap()[512 * s + r: 512 * s + r + 128, lc:lc + 512]

    def vt_ap(kc, hg):
        g_ = kc // 2
        s, lc0 = _g2src(g_)
        r0 = 2048 * s + lc0 + (kc % 2) * 512
        return vt_all[hg // 2].ap().rearrange("(a p) n -> p a n", p=128)[
            :, r0 // 128:(r0 // 128) + 4, (hg % 2) * 260:(hg % 2 + 1) * 260]
    ioD["kf_ap"] = kf_ap
    ioD["vt_ap"] = vt_ap
    ioD["kdiag_ap"] = lambda hg, ct: kqo[(2 * hg + ct) // 4].ap()[
        ((2 * hg + ct) % 4) * 128:((2 * hg + ct) % 4 + 1) * 128, :]
    ioD["vdiag_ap"] = lambda hg: vto[hg // 2].ap().rearrange("(a p) n -> p a n", p=128)[
        :, :, (hg % 2) * 260:(hg % 2 + 1) * 260]
    ioD["agw"] = lambda sync: sync.wait_ge(cc.h, cc_full)
    EQ_D = emit_D(nc, ioD)
    _inject_waits({k: v for k, v in EQ_D.items() if k != "gpsimd"}, wC)
    _inject_waits({"gpsimd": EQ_D["gpsimd"]}, wC_nocc)

    from concourse.aot_env import allow_ici_all_to_all
    with allow_ici_all_to_all():
        _flush(nc, [EQ_A, EQ_B, EQ_C, EQ_D])
    es.close()
    return nc


def _cc(a):
    return np.ascontiguousarray(a)


def _s1w(w):
    wT = np.asarray(w, np.float32).T
    return _cc(wT.reshape(8, 128, 32, 128).transpose(1, 2, 0, 3).reshape(128, 8 * FF))


def _s23w(w):
    wT = np.asarray(w, np.float32).T
    return _cc(wT.reshape(4, 8, 128, 32, 128).transpose(2, 3, 0, 1, 4).reshape(128, 32 * FF))


def _s4w(w):
    wT = np.asarray(w, np.float32).T
    return _cc(wT.reshape(8, 4, 128, 2, 512).transpose(2, 3, 0, 1, 4).reshape(128, 8 * FF))


def fused_maps(inp):
    x = np.asarray(inp["L0_attn_x"], dtype=np.float32)
    xT = _cc(x.transpose(0, 2, 1))
    tmk = _cc(np.asarray(inp["L0_attn_time_mix_k"], np.float32).reshape(8, 128).T)
    tmv = _cc(np.asarray(inp["L0_attn_time_mix_v"], np.float32).reshape(8, 128).T)
    tmr = _cc(np.asarray(inp["L0_attn_time_mix_r"], np.float32).reshape(8, 128).T)
    WkT = _cc(np.asarray(inp["L0_attn_Wk"], np.float32).T)
    WvT = _cc(np.asarray(inp["L0_attn_Wv"], np.float32).T)
    WrT = _cc(np.asarray(inp["L0_attn_Wr"], np.float32).T)
    td = np.asarray(inp["L0_attn_time_decay"], np.float32)
    tf = np.asarray(inp["L0_attn_time_first"], np.float32)

    c32 = lambda v: _cc(np.asarray(v, np.float32).reshape(32, 128).T)
    jp = np.arange(128)
    shared = {
        "tmk": tmk, "tmv": tmv, "tmr": tmr, "WkT": WkT, "WvT": WvT, "WrT": WrT,
        "WoT": _cc(np.asarray(inp["L0_attn_Wo"], np.float32).T),
        "W1": _cc(np.asarray(inp["L0_ffn_W1"], np.float32)),
        "b1": c32(inp["L0_ffn_b1"]),
        "th": c32(inp["L0_ffn_quantum_phases"]),
        "qwr": c32(inp["L0_ffn_quantum_weights_real"]),
        "qwi": c32(inp["L0_ffn_quantum_weights_imag"]),
        "W2": _cc(np.asarray(inp["L0_ffn_W2"], np.float32)),
        "b2": _cc(np.asarray(inp["L0_ffn_b2"], np.float32).reshape(8, 128).T),
        "Wq": _cc(np.asarray(inp["L1_attn_Wq"], np.float32)),
        "Wk1": _cc(np.asarray(inp["L1_attn_Wk"], np.float32)),
        "Wv1": _cc(np.asarray(inp["L1_attn_Wv"], np.float32)),
        "ident_in": np.eye(128, dtype=np.float32),
        "Wo1": _cc(np.asarray(inp["L1_attn_Wo"], np.float32)),
        "w1T": _s1w(inp["L1_ffn_w1_w"]),
        "w2T": _s1w(inp["L1_ffn_w2_w"]),
        "h1T": _s23w(inp["L1_ffn_h1_w"]),
        "h2T": _s23w(inp["L1_ffn_h2_w"]),
        "w3T": _s4w(inp["L1_ffn_w3_w"]),
        "s1c_in": c32(inp["L1_ffn_w1_s"]), "s2c_in": c32(inp["L1_ffn_w2_s"]),
        "sh1_in": c32(inp["L1_ffn_h1_s"]), "sh2_in": c32(inp["L1_ffn_h2_s"]),
        "s3c_in": _cc(np.asarray(inp["L1_ffn_w3_s"], np.float32).reshape(8, 128).T),
        "dcol_in": np.stack([SLOPES[h] * (jp - 127) + DIAG_OFF - MS for h in range(NH)],
                            axis=1).astype(np.float32),
        "fvb_in": np.stack([-SLOPES[h] * jp - DIAG_OFF for h in range(NH)],
                           axis=1).astype(np.float32),
    }

    maps = []
    for c in range(NCORES):
        b, h0, t0 = core_chunks(c)
        j = c % 4
        m = dict(shared)
        xin = np.zeros((D, 2050), dtype=np.float32)
        if h0 > 0:
            xin[:, 0] = xT[b, :, h0 - 1]
        xin[:, 1:1025] = xT[b, :, h0:h0 + 1024]
        xin[:, 1025] = xT[b, :, t0 - 1]
        xin[:, 1026:2050] = xT[b, :, t0:t0 + 1024]
        m["xT"] = xin
        fsl = slice(256 * j, 256 * j + 256)
        m["td2"] = _cc(td[fsl].reshape(2, 128).T)
        m["tf2"] = _cc(tf[fsl].reshape(2, 128).T)
        gi = np.zeros((128, 32), dtype=np.int32)
        for s in range(4):
            for kv in range(2):
                for dt2 in range(2):
                    gi[:, s * 4 + kv * 2 + dt2] = 2048 * s + 512 * j + 256 * kv + 128 * dt2 + jp
        for ft in range(8):
            s2, dt2 = divmod(ft, 2)
            for th in range(2):
                g_ = j if th == 0 else 7 - j
                gi[:, 16 + ft * 2 + th] = 1024 * s2 + 256 * (g_ % 4) + 128 * dt2 + jp
        m["gat_idx"] = gi
        qgpos = qg_positions(c)
        bias_tbl = np.zeros((128, NSLOT * NH), dtype=np.float32)
        for qg in range(4):
            q0 = qgpos[qg]
            for slot in range(SWEEPS[qg] // 128):
                j0 = slot * 128
                for h in range(NH):
                    col = h * NSLOT + QGOFF[qg] + slot
                    if j0 < q0:
                        bias_tbl[:, col] = SLOPES[h] * (j0 + jp - q0) - MS
                    else:
                        bias_tbl[:, col] = -120.0
        m["bias_tbl"] = bias_tbl
        maps.append(m)
    return maps


def kernel(**inp):
    maps = fused_maps(inp)
    res = run_bass_kernel_spmd(build_fused(), maps, core_ids=list(range(NCORES))).results
    out = np.zeros((B, T, D), dtype=np.float32)
    for c in range(NCORES):
        b, h0, t0 = core_chunks(c)
        a = np.asarray(res[c]["x4T"], dtype=np.float32)
        out[b, h0:h0 + 1024] = a[:, 0:1024].T
        out[b, t0:t0 + 1024] = a[:, 1024:2048].T
    return out


def _kernel_multilaunch_dead(**inp):
    resA = None

    def reasm_fm(res, key, dtype=np.float32):
        out = np.zeros((B, D, T), dtype=dtype)
        for c in range(NCORES):
            b, h0, t0 = core_chunks(c)
            a = np.asarray(res[c][key])
            out[b, :, h0:h0 + 1024] = a[:, 0:1024]
            out[b, :, t0:t0 + 1024] = a[:, 1024:2048]
        return out

    kfull = reasm_fm(resA, "k_out")
    vfull = reasm_fm(resA, "v_out")

    td = np.asarray(inp["L0_attn_time_decay"], np.float32)
    tf = np.asarray(inp["L0_attn_time_first"], np.float32)
    maps = []
    for c in range(NCORES):
        sl = slice(128 * c, 128 * (c + 1))
        maps.append({"k_in": _cc(kfull[:, sl, :]), "v_in": _cc(vfull[:, sl, :]),
                     "td": _cc(td[sl].reshape(128, 1)), "tf": _cc(tf[sl].reshape(128, 1))})
    resB = run_bass_kernel_spmd(build_launch_B(), maps, core_ids=list(range(NCORES))).results
    wkv_all = np.zeros((B, D, T), dtype=np.float32)
    for c in range(NCORES):
        wkv_all[:, 128 * c:128 * (c + 1), :] = np.asarray(resB[c]["wkv_out"])

    c32 = lambda v: _cc(np.asarray(v, np.float32).reshape(32, 128).T)
    sharedC = {
        "WoT": _cc(np.asarray(inp["L0_attn_Wo"], np.float32).T),
        "W1": _cc(np.asarray(inp["L0_ffn_W1"], np.float32)),
        "b1": c32(inp["L0_ffn_b1"]),
        "th": c32(inp["L0_ffn_quantum_phases"]),
        "qwr": c32(inp["L0_ffn_quantum_weights_real"]),
        "qwi": c32(inp["L0_ffn_quantum_weights_imag"]),
        "W2": _cc(np.asarray(inp["L0_ffn_W2"], np.float32)),
        "b2": _cc(np.asarray(inp["L0_ffn_b2"], np.float32).reshape(8, 128).T),
        "Wq": _cc(np.asarray(inp["L1_attn_Wq"], np.float32)),
        "Wk1": _cc(np.asarray(inp["L1_attn_Wk"], np.float32)),
        "Wv1": _cc(np.asarray(inp["L1_attn_Wv"], np.float32)),
        "ident_in": np.eye(128, dtype=np.float32),
    }
    maps = []
    for c in range(NCORES):
        b, h0, t0 = core_chunks(c)
        m = dict(sharedC)
        m["xT"] = _cc(np.concatenate([xT[b, :, h0:h0 + 1024], xT[b, :, t0:t0 + 1024]], axis=1))
        m["sr_in"] = np.asarray(resA[c]["sr_out"])
        m["wkv_in"] = _cc(np.concatenate([wkv_all[b, :, h0:h0 + 1024], wkv_all[b, :, t0:t0 + 1024]], axis=1))
        maps.append(m)
    resC = run_bass_kernel_spmd(build_launch_C(), maps, core_ids=list(range(NCORES))).results

    kf_b = np.zeros((B, D, T), dtype=BF16NP)
    vt_b = np.zeros((B, T, NH * 65), dtype=BF16NP)
    for c in range(NCORES):
        b, h0, t0 = core_chunks(c)
        kq = np.asarray(resC[c]["k_fm"])
        kf_b[b, :, h0:h0 + 1024] = kq[:, 0:1024]
        kf_b[b, :, t0:t0 + 1024] = kq[:, 1024:2048]
        vt = np.asarray(resC[c]["v_tm"])
        vt_b[b, h0:h0 + 1024] = vt[0:1024]
        vt_b[b, t0:t0 + 1024] = vt[1024:2048]

    jp = np.arange(128)
    sharedD = {
        "Wo1": _cc(np.asarray(inp["L1_attn_Wo"], np.float32)),
        "w1T": _s1w(inp["L1_ffn_w1_w"]),
        "w2T": _s1w(inp["L1_ffn_w2_w"]),
        "h1T": _s23w(inp["L1_ffn_h1_w"]),
        "h2T": _s23w(inp["L1_ffn_h2_w"]),
        "w3T": _s4w(inp["L1_ffn_w3_w"]),
        "s1c_in": c32(inp["L1_ffn_w1_s"]), "s2c_in": c32(inp["L1_ffn_w2_s"]),
        "sh1_in": c32(inp["L1_ffn_h1_s"]), "sh2_in": c32(inp["L1_ffn_h2_s"]),
        "s3c_in": _cc(np.asarray(inp["L1_ffn_w3_s"], np.float32).reshape(8, 128).T),
        "dcol_in": np.stack([SLOPES[h] * (jp - 127) + DIAG_OFF - MS for h in range(NH)],
                            axis=1).astype(np.float32),
        "fvb_in": np.stack([-SLOPES[h] * jp - DIAG_OFF for h in range(NH)],
                           axis=1).astype(np.float32),
    }
    maps = []
    for c in range(NCORES):
        b, h0, t0 = core_chunks(c)
        m = dict(sharedD)
        m["q_in"] = np.asarray(resC[c]["q_fm"])
        m["kf_in"] = kf_b[b]
        m["vt_in"] = vt_b[b]
        m["kdiag"] = np.asarray(resC[c]["k_fm"])
        m["vdiag"] = np.asarray(resC[c]["v_tm"])
        m["x2T_in"] = np.asarray(resC[c]["x2T"])
        qgpos = qg_positions(c)
        bias_tbl = np.zeros((128, NSLOT * NH), dtype=np.float32)
        for qg in range(4):
            q0 = qgpos[qg]
            for slot in range(SWEEPS[qg] // 128):
                j0 = slot * 128
                for h in range(NH):
                    col = h * NSLOT + QGOFF[qg] + slot
                    if j0 < q0:
                        bias_tbl[:, col] = SLOPES[h] * (j0 + jp - q0) - MS
                    else:
                        bias_tbl[:, col] = -120.0
        m["bias_tbl"] = bias_tbl
        maps.append(m)
    resD = run_bass_kernel_spmd(build_launch_D(), maps, core_ids=list(range(NCORES))).results

    out = np.zeros((B, T, D), dtype=np.float32)
    for c in range(NCORES):
        b, h0, t0 = core_chunks(c)
        a = np.asarray(resD[c]["x4T"], dtype=np.float32)
        out[b, h0:h0 + 1024] = a[:, 0:1024].T
        out[b, t0:t0 + 1024] = a[:, 1024:2048].T
    return out


# version 26
# speedup vs baseline: 1.0149x; 1.0080x over previous
import numpy as np
import ml_dtypes
from contextlib import ExitStack

import concourse.bass as bass
import concourse.mybir as mybir
from concourse.bass import IndirectOffsetOnAxis
from concourse.bass_utils import run_bass_kernel_spmd

F32 = mybir.dt.float32
BF16 = mybir.dt.bfloat16
I32 = mybir.dt.int32
AF = mybir.ActivationFunctionType
ALU = mybir.AluOpType
BF16NP = ml_dtypes.bfloat16

NCORES = 8
B, T, D, FF = 2, 8192, 1024, 4096
TOK = 2048
NH, HD = 16, 64
EPS_LN = 1e-5
MS = 34.0
DIAG_OFF = 45.0
TWO_PI = 6.283185307179586

SLOPES = [2 ** (-0.5 * (h + 1)) for h in range(NH)]


def core_chunks(c):
    s = c % 4
    return c // 4, 1024 * s, 8192 - 1024 * (s + 1)


class Sem:

    _ctr = [0]
    _all = []

    def __init__(self, nc, es, name):
        Sem._ctr[0] += 1
        self.h = es.enter_context(nc.semaphore(f"{name}_{Sem._ctr[0]}"))
        self.n = 0
        Sem._all.append(self)

    def inc(self, instr, step=1):
        instr.then_inc(self.h, step)
        self.n += step
        return self.n


class FakeIns:

    __slots__ = ("incs",)

    def __init__(self):
        self.incs = []

    def then_inc(self, sem, n=1):
        self.incs.append((sem, n))
        return self


class EngQ:

    def __init__(self):
        self.calls = []

    def __getattr__(self, name):
        def m(*a, **k):
            fi = FakeIns()
            self.calls.append((name, a, k, fi))
            return fi
        return m

    def wait_ge(self, sem, val):
        self.calls.append(("wait_ge", (sem, val), {}, None))

    def replay(self, e):
        for name, a, k, fi in self.calls:
            ins = getattr(e, name)(*a, **k)
            if fi is not None:
                for sem, n in fi.incs:
                    ins.then_inc(sem, n)



A_INPUTS = [("xT", [D, 2050], "F32"), ("tmk", [128, 8], "F32"), ("tmv", [128, 8], "F32"),
            ("tmr", [128, 8], "F32"), ("WkT", [D, D], "F32"), ("WvT", [D, D], "F32"),
            ("WrT", [D, D], "F32")]


def emit_A(nc, io):
    es = ExitStack()
    xT, tmk, tmv, tmr, WkT, WvT, WrT = (io[n] for n, _, _ in A_INPUTS)
    kv_dst = io["kv_dst"]

    x_sb = es.enter_context(nc.sbuf_tensor("a_x_sb", [128, 2 * 2050], F32, side="right"))
    mix = {m: es.enter_context(nc.sbuf_tensor(f"a_mix_{m}", [128, 8 * TOK], BF16, side="right"))
           for m in "kvr"}
    tm_sb = es.enter_context(nc.sbuf_tensor("a_tm_sb", [128, 3 * 8], F32, side="right"))
    dscr = es.enter_context(nc.sbuf_tensor("a_dscr", [128, 1024], F32, side="right"))
    w_sb = es.enter_context(nc.sbuf_tensor("a_w_sb", [128, 2 * 8 * 1024], BF16, side="right"))
    stage = es.enter_context(nc.sbuf_tensor("a_stage", [128, 2 * TOK], F32, side="right"))
    ps = es.enter_context(nc.psum_tensor("a_ps", [128, 4 * 512], F32))

    ld = Sem(nc, es, "a_ld")
    ldx = [Sem(nc, es, "a_ldx0"), Sem(nc, es, "a_ldx1")]
    lw = [Sem(nc, es, "a_lw0"), Sem(nc, es, "a_lw1")]
    vm = Sem(nc, es, "a_vm")
    tsm = Sem(nc, es, "a_tsm")
    ssm = Sem(nc, es, "a_ssm")
    st = Sem(nc, es, "a_st")

    EQ = {n_: EngQ() for n_ in ("sync", "gpsimd", "vector", "scalar", "tensor")}
    SY = lambda f: f(EQ["sync"])
    GP = lambda f: f(EQ["gpsimd"])
    VE = lambda f: f(EQ["vector"])
    SC = lambda f: f(EQ["scalar"])
    TE = lambda f: f(EQ["tensor"])

    def _(sync):
        for i, tmx in enumerate((tmk, tmv, tmr)):
            ld.inc(sync.dma_start(out=tm_sb[:, i * 8:(i + 1) * 8], in_=tmx), 16)
            sync.wait_ge(ld.h, ld.n)
        for ft in range(8):
            if ft >= 2:
                sync.wait_ge(vm.h, 8 * (ft - 1))
            p = ft % 2
            ldx[p].inc(sync.dma_start(out=x_sb[:, p * 2050:(p + 1) * 2050],
                                      in_=xT[ft * 128:(ft + 1) * 128, :]), 16)
    SY(_)

    def _(g):
        for pi, W in enumerate((WkT, WvT, WrT)):
            if pi == 2:
                g.wait_ge(tsm.h, 32)
            p = pi % 2
            g.wait_ge(lw[p].h, lw[p].n)
            base = p * 8192
            for dt in range(8):
                lw[p].inc(g.dma_start(out=w_sb[:, base + dt * 1024: base + (dt + 1) * 1024],
                                      in_=W[dt * 128:(dt + 1) * 128, :]), 16)
    GP(_)

    def _(v):
        v.wait_ge(ld.h, 48)
        for ft in range(8):
            p = ft % 2
            v.wait_ge(ldx[p].h, 16 * (ft // 2 + 1))
            xb = x_sb[:, p * 2050:(p + 1) * 2050]
            for ch in range(2):
                cur = xb[:, 1 + ch * 1025: 1 + ch * 1025 + 1024]
                last = xb[:, ch * 1025: ch * 1025 + 1024]
                vm.inc(v.tensor_sub(dscr[:], cur, last))
                v.wait_ge(vm.h, vm.n)
                for i in range(3):
                    m = "kvr"[i]
                    vm.inc(v.scalar_tensor_tensor(
                        mix[m][:, ft * TOK + ch * 1024: ft * TOK + ch * 1024 + 1024],
                        dscr[:], tm_sb[:, i * 8 + ft: i * 8 + ft + 1], last,
                        ALU.mult, ALU.add))
                v.wait_ge(vm.h, vm.n)
    VE(_)

    def _(t):
        t.wait_ge(vm.h, 64)
        for pi in range(3):
            p = pi % 2
            t.wait_ge(lw[p].h, 128 if pi < 2 else 256)
            wbase = p * 8192
            for et in range(8):
                for nc4 in range(4):
                    g_ = pi * 32 + et * 4 + nc4
                    if g_ >= 4:
                        t.wait_ge(ssm.h, g_ - 3)
                    ins = None
                    for dt in range(8):
                        ins = t.matmul(
                            ps[:, (g_ % 4) * 512:(g_ % 4 + 1) * 512],
                            w_sb[:, wbase + dt * 1024 + et * 128: wbase + dt * 1024 + (et + 1) * 128],
                            mix["kvr"[pi]][:, dt * TOK + nc4 * 512: dt * TOK + (nc4 + 1) * 512],
                            start=(dt == 0), stop=(dt == 7))
                    tsm.inc(ins)
    TE(_)

    def _(s):
        def stcnt(m):
            return 32 * min(m + 1, 16) + 16 * max(0, m - 15)
        for pi in range(3):
            for et in range(8):
                ei = pi * 8 + et
                for nc4 in range(4):
                    g_ = pi * 32 + et * 4 + nc4
                    s.wait_ge(tsm.h, g_ + 1)
                    if nc4 == 0 and ei >= 2:
                        s.wait_ge(st.h, stcnt(ei - 2))
                    ssm.inc(s.activation(
                        stage[:, (ei % 2) * TOK + nc4 * 512: (ei % 2) * TOK + (nc4 + 1) * 512],
                        ps[:, (g_ % 4) * 512:(g_ % 4 + 1) * 512],
                        AF.Sigmoid if pi == 2 else AF.Copy))
    SC(_)

    def _(sync):
        for pi in range(3):
            for et in range(8):
                ei = pi * 8 + et
                sync.wait_ge(ssm.h, (ei + 1) * 4)
                if pi == 2:
                    st.inc(sync.dma_start(out=kv_dst(pi, et),
                                          in_=stage[:, (ei % 2) * TOK:(ei % 2) * TOK + TOK]), 16)
                else:
                    for half in range(2):
                        st.inc(sync.dma_start(out=io["kv_sth"](pi, et, half),
                                              in_=stage[:, (ei % 2) * TOK + half * 1024:
                                                        (ei % 2) * TOK + (half + 1) * 1024]), 16)
                sync.wait_ge(st.h, st.n)
    SY(_)

    es.close()
    return EQ, st


def build_launch_A():
    nc = bass.Bass("TRN2", target_bir_lowering=False)
    io = {}
    _declare_io(nc, io, A_INPUTS,
                [("kv_out", [2048, TOK], "F32"), ("sr_out", [D, TOK], "F32")])
    def kv_dst(pi, et):
        if pi == 2:
            return io["sr_out"][et * 128:(et + 1) * 128, :]
        j, sub = et // 2, (et % 2) * 128
        base = 512 * j + (0 if pi == 0 else 256) + sub
        return io["kv_out"][base:base + 128, :]
    io["kv_dst"] = kv_dst
    io["kv_sth"] = lambda pi, et, half: kv_dst(pi, et)[:, half * 1024:(half + 1) * 1024]
    EQ, _st = emit_A(nc, io)
    _flush(nc, EQ)
    return nc


def emit_B(nc, io):
    es = ExitStack()
    CH = 2048
    NCH = T // CH
    OPS = 10
    kv_load = io["kv_load"]
    wkv_dst = io["wkv_dst"]
    td2, tf2 = io["td2"], io["tf2"]

    kc = es.enter_context(nc.sbuf_tensor("b_kc", [128, 2 * CH], F32, side="right"))
    vc = es.enter_context(nc.sbuf_tensor("b_vc", [128, 2 * CH], F32, side="right"))
    ek = es.enter_context(nc.sbuf_tensor("b_ek", [128, CH], F32, side="right"))
    euk = es.enter_context(nc.sbuf_tensor("b_euk", [128, CH], F32, side="right"))
    da = es.enter_context(nc.sbuf_tensor("b_da", [128, CH], F32, side="right"))
    abuf = es.enter_context(nc.sbuf_tensor("b_abuf", [128, 2 * (CH + 1)], F32, side="right"))
    bbuf = es.enter_context(nc.sbuf_tensor("b_bbuf", [128, 2 * (CH + 1)], F32, side="right"))
    num = es.enter_context(nc.sbuf_tensor("b_num", [128, CH], F32, side="right"))
    den = es.enter_context(nc.sbuf_tensor("b_den", [128, CH], F32, side="right"))
    lam = es.enter_context(nc.sbuf_tensor("b_lam", [128, 2 * CH], F32, side="right"))
    wkv_sb = es.enter_context(nc.sbuf_tensor("b_wkv", [128, 2 * CH], F32, side="right"))
    vec = es.enter_context(nc.sbuf_tensor("b_vec", [128, 8], F32, side="right"))

    ldv = Sem(nc, es, "b_ldv")
    ldc = [Sem(nc, es, "b_ldc0"), Sem(nc, es, "b_ldc1")]
    vm = Sem(nc, es, "b_vm")
    sm = Sem(nc, es, "b_sm")
    st = Sem(nc, es, "b_st")

    EQ = {n_: EngQ() for n_ in ("sync", "gpsimd", "vector", "scalar", "tensor")}
    ORDER = [(i % 2, i // 2) for i in range(2 * NCH)]

    def _(sync):
        ldv.inc(sync.dma_start(out=vec[:, 0:2], in_=td2), 16)
        ldv.inc(sync.dma_start(out=vec[:, 2:4], in_=tf2), 16)
    _(EQ["sync"])

    def _loads(g, i):
        dt2, c = ORDER[i]
        p = i % 2
        if i >= 2:
            g.wait_ge(sm.h, 2 + (i - 1) * 2)
            g.wait_ge(ldc[p].h, ldc[p].n)
        for half in range(2):
            ldc[p].inc(kv_load(g, kc[:, p * CH + half * 1024: p * CH + (half + 1) * 1024],
                               0, dt2, c, half), 16)
            ldc[p].inc(kv_load(g, vc[:, p * CH + half * 1024: p * CH + (half + 1) * 1024],
                               1, dt2, c, half), 16)

    def _(s):
        s.wait_ge(ldv.h, 32)
        s.activation(vec[:, 4:6], vec[:, 0:2], AF.Exp).then_inc(sm.h, 1)
        s.wait_ge(sm.h, 1)
        s.activation(vec[:, 6:8], vec[:, 4:6], AF.Exp, scale=-1.0).then_inc(sm.h, 1)
        sm.n = 2
        for i, (dt2, c) in enumerate(ORDER):
                p = i % 2
                s.wait_ge(ldc[p].h, 64 * (i // 2 + 1))
                if i >= 1:
                    s.wait_ge(vm.h, 4 + (i - 1) * OPS + 8)
                sm.inc(s.activation(ek[:], kc[:, p * CH:(p + 1) * CH], AF.Exp))
                sm.inc(s.activation(euk[:], kc[:, p * CH:(p + 1) * CH], AF.Exp,
                                    bias=vec[:, 2 + dt2:3 + dt2]))
    _(EQ["scalar"])

    def _(v):
        v.wait_ge(sm.h, 2)
        vm.inc(v.memset(lam[:], 1.0))
        v.wait_ge(vm.h, vm.n)
        for dt2 in range(2):
            vm.inc(v.tensor_scalar(lam[:, dt2 * CH:(dt2 + 1) * CH], lam[:, dt2 * CH:(dt2 + 1) * CH],
                                   vec[:, 6 + dt2:7 + dt2], None, ALU.mult))
        v.wait_ge(vm.h, vm.n)
        vm.inc(v.memset(da[:], 0.0))
        for i, (dt2, c) in enumerate(ORDER):
                bf = (i % 2) * CH
                ab = abuf[:, dt2 * (CH + 1):(dt2 + 1) * (CH + 1)]
                bb = bbuf[:, dt2 * (CH + 1):(dt2 + 1) * (CH + 1)]
                if c == 0:
                    vm.inc(v.memset(ab[:, 0:1], 0.0))
                    vm.inc(v.memset(bb[:, 0:1], 0.0))
                else:
                    vm.inc(v.tensor_copy(ab[:, 0:1], ab[:, CH:CH + 1]))
                    vm.inc(v.tensor_copy(bb[:, 0:1], bb[:, CH:CH + 1]))
                v.wait_ge(sm.h, 2 + (i + 1) * 2)
                v.wait_ge(vm.h, vm.n)
                vm.inc(v.tensor_mul(da[:], ek[:], vc[:, bf:bf + CH]))
                v.wait_ge(vm.h, vm.n)
                vm.inc(v.tensor_tensor_scan(ab[:, 1:CH + 1], lam[:, dt2 * CH:(dt2 + 1) * CH], da[:],
                                            ab[:, 0:1], ALU.mult, ALU.add))
                vm.inc(v.tensor_tensor_scan(bb[:, 1:CH + 1], lam[:, dt2 * CH:(dt2 + 1) * CH], ek[:],
                                            bb[:, 0:1], ALU.mult, ALU.add))
                v.wait_ge(vm.h, vm.n)
                vm.inc(v.tensor_mul(num[:], euk[:], vc[:, bf:bf + CH]))
                v.wait_ge(vm.h, vm.n)
                vm.inc(v.tensor_add(num[:], num[:], ab[:, 0:CH]))
                vm.inc(v.tensor_add(den[:], euk[:], bb[:, 0:CH]))
                v.wait_ge(vm.h, vm.n)
                vm.inc(v.reciprocal(den[:], den[:]))
                v.wait_ge(vm.h, vm.n)
                if i >= 2:
                    v.wait_ge(st.h, (i - 1) * 32)
                vm.inc(v.tensor_mul(wkv_sb[:, bf:bf + CH], num[:], den[:]))
    _(EQ["vector"])

    def _(g):
        _loads(g, 0)
        _loads(g, 1)
        for i, (dt2, c) in enumerate(ORDER):
                g.wait_ge(vm.h, 4 + (i + 1) * OPS)
                for half in range(2):
                    st.inc(g.dma_start(out=wkv_dst(dt2, c, half),
                                       in_=wkv_sb[:, (i % 2) * CH + half * 1024:(i % 2) * CH + (half + 1) * 1024]), 16)
                g.wait_ge(st.h, st.n)
                if i == 3 and "wkv_ag_hook" in io:
                    io["wkv_ag_hook"](g)
                if i + 2 < 2 * NCH:
                    _loads(g, i + 2)
    _(EQ["gpsimd"])

    es.close()
    return EQ, st


def _g2src(g_):
    if g_ < 4:
        return g_, 0
    return 7 - g_, 1024


def build_launch_B():
    nc = bass.Bass("TRN2", target_bir_lowering=False)
    io = {}
    _declare_io(nc, io, [("kv_all", [2048, 2048], "F32"), ("td2", [128, 2], "F32"),
                         ("tf2", [128, 2], "F32")],
                [("wkv_out", [1024, 2048], "F32")])
    def kv_load(g, dst, kv, dt2, c, half):
        g_ = 2 * c + half
        s, lc = _g2src(g_)
        base = 512 * s + (0 if kv == 0 else 256) + dt2 * 128
        return g.dma_start(out=dst, in_=io["kv_all"][base:base + 128, lc:lc + 1024])
    def wkv_dst(dt2, c, half):
        g_ = 2 * c + half
        s, lc = _g2src(g_)
        return io["wkv_out"][256 * s + dt2 * 128: 256 * s + dt2 * 128 + 128, lc:lc + 1024]
    io["kv_load"] = kv_load
    io["wkv_dst"] = wkv_dst
    EQ, _st = emit_B(nc, io)
    _flush(nc, EQ)
    return nc


C_INPUTS = [("xT", [D, TOK], "F32"), ("sr_in", [D, TOK], "F32"), ("wkv_in", [D, TOK], "F32"),
            ("WoT", [D, D], "F32"), ("W1", [D, FF], "F32"), ("b1", [128, 32], "F32"),
            ("th", [128, 32], "F32"), ("qwr", [128, 32], "F32"), ("qwi", [128, 32], "F32"),
            ("W2", [FF, D], "F32"), ("b2", [128, 8], "F32"), ("Wq", [D, D], "F32"),
            ("Wk1", [D, D], "F32"), ("Wv1", [D, D], "F32"), ("ident_in", [128, 128], "F32")]
C_OUTPUTS = [("x2T", [D, TOK], "F32"), ("q_fm", [D, TOK], "BF16"),
             ("k_fm", [D, TOK], "BF16"), ("v_tm", [TOK, NH * 65], "BF16")]


def emit_C(nc, io):
    es = ExitStack()
    (xT, sr_in, wkv_in, WoT, W1, b1, th, qwr, qwi, W2, b2, Wq, Wk1, Wv1, ident_in) = (
        io[n] for n, _, _ in C_INPUTS)
    x2T, q_fm, k_fm, v_tm = (io[n] for n, _, _ in C_OUTPUTS)

    y_dram = nc.dram_tensor("y_dram", [D, TOK], F32, kind="Internal").ap()
    x1_dram = nc.dram_tensor("x1_dram", [D, TOK], F32, kind="Internal").ap()
    y2_dram = nc.dram_tensor("y2_dram", [D, TOK], F32, kind="Internal").ap()

    xbf = es.enter_context(nc.sbuf_tensor("c_xbf", [128, 8 * TOK], BF16))
    B1 = es.enter_context(nc.sbuf_tensor("c_B1", [128, 16 * 1040], BF16))
    B2 = es.enter_context(nc.sbuf_tensor("c_B2", [128, 16 * 1024], BF16))
    wsq = es.enter_context(nc.sbuf_tensor("c_wsq", [128, 8 * 1024], BF16))
    w1b = es.enter_context(nc.sbuf_tensor("c_w1b", [128, 2 * 8 * 512], BF16))
    w2b = es.enter_context(nc.sbuf_tensor("c_w2b", [128, 16 * 512], BF16))
    fscr = es.enter_context(nc.sbuf_tensor("c_fscr", [128, 6 * 512], F32))
    ldscr = es.enter_context(nc.sbuf_tensor("c_ldscr", [128, 4096], F32))
    iscr = es.enter_context(nc.sbuf_tensor("c_iscr", [128, 512], I32))
    sqscr = es.enter_context(nc.sbuf_tensor("c_sqscr", [128, 512], BF16))
    ybf2 = es.enter_context(nc.sbuf_tensor("c_ybf2", [128, 512], BF16))
    vcols = es.enter_context(nc.sbuf_tensor("c_vcols", [128, 320], F32))
    vscr = es.enter_context(nc.sbuf_tensor("c_vscr", [128, 192], F32))
    rows = es.enter_context(nc.sbuf_tensor("c_rows", [1, 4096], F32))
    rscr = es.enter_context(nc.sbuf_tensor("c_rscr", [1, 2 * 512], F32))
    onesb = es.enter_context(nc.sbuf_tensor("c_onesb", [128, 1], BF16))
    onesf = es.enter_context(nc.sbuf_tensor("c_onesf", [1, 128], F32))
    epsc = es.enter_context(nc.sbuf_tensor("c_epsc", [128, 1], F32))
    ident = es.enter_context(nc.sbuf_tensor("c_ident", [128, 128], BF16))

    ps = es.enter_context(nc.psum_tensor("c_ps", [128, 7 * 512], F32))
    ps_tr = es.enter_context(nc.psum_tensor("c_ps_tr", [128, 128], BF16))

    ld = Sem(nc, es, "ld")
    lw = Sem(nc, es, "lw")
    lwb = Sem(nc, es, "lwb")
    lg = Sem(nc, es, "lg")
    lw1 = [Sem(nc, es, "lw1a"), Sem(nc, es, "lw1b")]
    vm = Sem(nc, es, "vm")
    sm = Sem(nc, es, "sm")
    tm = Sem(nc, es, "tm")
    st = Sem(nc, es, "st")
    sg = Sem(nc, es, "sg")
    sg2 = Sem(nc, es, "sg2")
    sh = Sem(nc, es, "c_sh")
    sh2 = Sem(nc, es, "c_sh2")

    ZARG, MARG, SSIN, YT, YW, SP5 = (fscr[:, i * 512:(i + 1) * 512] for i in range(6))
    B1C, THC, QWRC, QWIC = (vcols[:, i * 32:(i + 1) * 32] for i in range(4))
    B2C = vcols[:, 128:136]
    ZCC = vcols[:, 160:192]
    CPC = vcols[:, 192:224]
    CpCc = vcols[:, 256:288]
    CpSc = vcols[:, 288:320]

    def bank(i):
        return ps[:, (i % 4) * 512:(i % 4 + 1) * 512]

    SB = ps[:, 4 * 512:5 * 512]
    QB = ps[:, 5 * 512:6 * 512]
    UB = ps[:, 6 * 512:7 * 512]

    bank_free_c = {0: 0, 1: 0, 2: 0, 3: 0}
    w1_used = {0: 0, 1: 0}
    w2_used = {0: 0}
    lw1_tot = {0: 0, 1: 0}

    EQ = {n_: EngQ() for n_ in ("sync", "gpsimd", "vector", "scalar", "tensor")}
    SY = lambda f: f(EQ["sync"])
    GP = lambda f: f(EQ["gpsimd"])
    VE = lambda f: f(EQ["vector"])
    SC = lambda f: f(EQ["scalar"])
    TE = lambda f: f(EQ["tensor"])
    if True:
        def vchain(v, ins):
            vm.inc(ins)
            v.wait_ge(vm.h, vm.n)

        def schain(s, ins):
            sm.inc(ins)
            s.wait_ge(sm.h, sm.n)

        def _(sync):
            for t_, dst in ((b1, B1C), (th, THC), (qwr, QWRC), (qwi, QWIC), (b2, B2C)):
                ld.inc(sync.dma_start(out=dst, in_=t_), 16)
                sync.wait_ge(ld.h, ld.n)
        SY(_)

        def _(g):
            lg.inc(g.dma_start(out=ident[:], in_=ident_in), 16)
            g.wait_ge(lg.h, lg.n)
            for dt in range(8):
                lg.inc(g.dma_start(out=wsq[:, dt * 1024:(dt + 1) * 1024],
                                   in_=WoT[dt * 128:(dt + 1) * 128, :]), 16)
            g.wait_ge(lg.h, lg.n)
        GP(_)
        lg_wo = lg.n

        def _(v):
            vchain(v, v.memset(onesb[:], 1.0))
            vchain(v, v.memset(onesf[:], 1.0))
            vchain(v, v.memset(epsc[:], EPS_LN))
            v.wait_ge(ld.h, ld.n)
            vchain(v, v.tensor_scalar(iscr[:, 0:32], THC, 1.0 / TWO_PI, None, ALU.mult))
            vchain(v, v.tensor_copy(vscr[:, 0:32], iscr[:, 0:32]))
            vchain(v, v.scalar_tensor_tensor(vscr[:, 0:32], vscr[:, 0:32], -TWO_PI, THC,
                                             ALU.mult, ALU.add))
            vchain(v, v.tensor_scalar(vscr[:, 96:128], vscr[:, 0:32], 3.141592653589793, -TWO_PI,
                                      ALU.is_gt, ALU.mult))
            vchain(v, v.tensor_add(vscr[:, 0:32], vscr[:, 0:32], vscr[:, 96:128]))
            vchain(v, v.tensor_scalar(vscr[:, 96:128], vscr[:, 0:32], -3.141592653589793, TWO_PI,
                                      ALU.is_lt, ALU.mult))
            vchain(v, v.tensor_add(vscr[:, 0:32], vscr[:, 0:32], vscr[:, 96:128]))
            vchain(v, v.scalar_tensor_tensor(ZCC, B1C, 0.1, vscr[:, 0:32], ALU.mult, ALU.add))
        VE(_)

        def _(s):
            s.wait_ge(vm.h, vm.n)
            schain(s, s.activation(vscr[:, 32:64], vscr[:, 0:32], AF.Sin))
            schain(s, s.activation(vscr[:, 64:96], vscr[:, 0:32], AF.Sin, scale=0.5))
            schain(s, s.activation(vscr[:, 96:128], vscr[:, 64:96], AF.Square))
        SC(_)

        def _(v):
            v.wait_ge(sm.h, sm.n)
            vchain(v, v.tensor_scalar(vscr[:, 128:160], vscr[:, 96:128], -2.0, 1.0,
                                      ALU.mult, ALU.add))
            vchain(v, v.tensor_mul(vscr[:, 160:192], vscr[:, 128:160], QWRC))
            vchain(v, v.tensor_mul(vscr[:, 96:128], vscr[:, 32:64], QWIC))
            vchain(v, v.tensor_add(vscr[:, 160:192], vscr[:, 160:192], vscr[:, 96:128]))
            vchain(v, v.tensor_mul(vscr[:, 160:192], vscr[:, 160:192], QWIC))
            vchain(v, v.tensor_scalar(CPC, vscr[:, 160:192], 0.1, None, ALU.mult))
            vchain(v, v.tensor_scalar(vscr[:, 0:32], ZCC, 3.141592653589793, -TWO_PI,
                                      ALU.is_gt, ALU.mult))
            vchain(v, v.tensor_add(vscr[:, 32:64], ZCC, vscr[:, 0:32]))
            vchain(v, v.tensor_scalar(vscr[:, 0:32], vscr[:, 32:64], -3.141592653589793, TWO_PI,
                                      ALU.is_lt, ALU.mult))
            vchain(v, v.tensor_add(vscr[:, 64:96], vscr[:, 32:64], vscr[:, 0:32]))
        VE(_)

        def _(s):
            s.wait_ge(vm.h, vm.n)
            schain(s, s.activation(vscr[:, 96:128], vscr[:, 64:96], AF.Sin))
            schain(s, s.activation(vscr[:, 128:160], vscr[:, 64:96], AF.Sin, scale=0.5))
            schain(s, s.activation(vscr[:, 160:192], vscr[:, 128:160], AF.Square))
        SC(_)

        def _(v):
            v.wait_ge(sm.h, sm.n)
            vchain(v, v.tensor_scalar(vscr[:, 0:32], vscr[:, 160:192], -2.0, 1.0,
                                      ALU.mult, ALU.add))
            vchain(v, v.tensor_mul(CpCc, CPC, vscr[:, 0:32]))
            vchain(v, v.tensor_mul(CpSc, CPC, vscr[:, 96:128]))
        VE(_)

        wkv_load = io["wkv_load"]
        lwp = [lw, lwb]
        vm_rw = {}
        ldn_rw = {}
        lwn_rw = {}
        for u, (ft, th) in enumerate([(f_, t_) for t_ in range(2) for f_ in range(8)]):
                sl = u % 2

                def _(sync, ft=ft, th=th, u=u, sl=sl):
                    sync.wait_ge(vm.h, vm_rw.get(u - 2, 0))
                    sync.wait_ge(ld.h, ld.n)
                    ld.inc(sync.dma_start(out=ldscr[:, sl * 1024:(sl + 1) * 1024],
                                          in_=sr_in[ft * 128:(ft + 1) * 128,
                                                    th * 1024:(th + 1) * 1024]), 16)
                SY(_)
                ldn_rw[u] = ld.n

                def _(g, ft=ft, th=th, u=u, sl=sl):
                    g.wait_ge(vm.h, vm_rw.get(u - 2, 0))
                    g.wait_ge(lwp[sl].h, lwp[sl].n)
                    lwp[sl].inc(wkv_load(g, ldscr[:, 2048 + sl * 1024:2048 + (sl + 1) * 1024],
                                         ft, th), 16)
                GP(_)
                lwn_rw[u] = lwp[sl].n

                def _(v, ft=ft, th=th, u=u, sl=sl):
                    v.wait_ge(ld.h, ldn_rw[u])
                    v.wait_ge(lwp[sl].h, lwn_rw[u])
                    vm.inc(v.tensor_mul(B1[:, ft * 2048 + th * 1024: ft * 2048 + (th + 1) * 1024],
                                        ldscr[:, sl * 1024:(sl + 1) * 1024],
                                        ldscr[:, 2048 + sl * 1024:2048 + (sl + 1) * 1024]))
                VE(_)
                vm_rw[u] = vm.n

        tmP = {}
        ldn1 = {}
        smEv = {}
        vmAd = {}
        smSq = {}
        ldfree = {0: vm.n, 1: vm.n}
        evfree = {0: 0, 1: 0}
        sgp = [sg, sg2]
        shp = [sh, sh2]
        stY = {}
        rows_done = 0
        EVAC = [YT, SP5]
        YWP = [YW, ZARG]
        S1P = [ZARG, SSIN]
        S2P = [MARG, SP5]
        s1free = {0: 0, 1: 0}
        for nc4 in range(4):
            for et in range(8):
                gi = nc4 * 8 + et
                lp = gi % 2

                def _(t, nc4=nc4, et=et, gi=gi):
                    if gi == 0:
                        t.wait_ge(vm.h, vm.n)
                        t.wait_ge(lg.h, lg_wo)
                    if gi >= 4:
                        t.wait_ge(vm.h, bank_free_c[gi % 4])
                    ins = None
                    for dt in range(8):
                        ins = t.matmul(bank(gi),
                                       wsq[:, dt * 1024 + et * 128: dt * 1024 + (et + 1) * 128],
                                       B1[:, dt * 2048 + nc4 * 512: dt * 2048 + (nc4 + 1) * 512],
                                       start=(dt == 0), stop=(dt == 7))
                    tm.inc(ins)
                TE(_)
                tmP[gi] = tm.n

                def _(sync, nc4=nc4, et=et, gi=gi, lp=lp):
                    sync.wait_ge(vm.h, ldfree[lp])
                    sync.wait_ge(ld.h, ld.n)
                    ld.inc(sync.dma_start(out=ldscr[:, lp * 512:(lp + 1) * 512],
                                          in_=xT[et * 128:(et + 1) * 128, nc4 * 512:(nc4 + 1) * 512]), 16)
                SY(_)
                ldn1[gi] = ld.n

                def _(s, gi=gi, lp=lp):
                    s.wait_ge(tm.h, tmP[gi])
                    s.wait_ge(vm.h, evfree[lp])
                    sm.inc(s.activation(EVAC[lp], bank(gi), AF.Copy))
                SC(_)
                smEv[gi] = sm.n

                def _(v, nc4=nc4, et=et, gi=gi, lp=lp):
                    v.wait_ge(sm.h, smEv[gi])
                    v.wait_ge(ld.h, ldn1[gi])
                    v.wait_ge(*stY.get(gi - 2, (sh.h, 0)))
                    vchain(v, v.tensor_add(YWP[lp], EVAC[lp], ldscr[:, lp * 512:(lp + 1) * 512]))
                    vchain(v, v.tensor_copy(xbf[:, et * TOK + nc4 * 512: et * TOK + (nc4 + 1) * 512],
                                            YWP[lp]))
                    bank_free_c[gi % 4] = vm.n
                    ldfree[lp] = vm.n
                    evfree[lp] = vm.n
                VE(_)
                vmAd[gi] = vm.n

                def _(s, nc4=nc4, et=et, gi=gi):
                    s.wait_ge(vm.h, vmAd[gi])
                    s.wait_ge(tm.h, tm.n)
                    sm.inc(s.activation(sqscr[:], xbf[:, et * TOK + nc4 * 512: et * TOK + (nc4 + 1) * 512],
                                        AF.Square))
                SC(_)
                smSq[gi] = sm.n

                def _(t, nc4=nc4, et=et, gi=gi, rd=rows_done):
                    t.wait_ge(sm.h, smSq[gi])
                    if et == 0:
                        t.wait_ge(vm.h, rd)
                    t.matmul(SB[0:1, 0:512], onesb[:],
                             xbf[:, et * TOK + nc4 * 512: et * TOK + (nc4 + 1) * 512],
                             start=(et == 0), stop=(et == 7))
                    ins = t.matmul(QB[0:1, 0:512], onesb[:], sqscr[:],
                                   start=(et == 0), stop=(et == 7))
                    tm.inc(ins)
                TE(_)

                def _(s, nc4=nc4, et=et, gi=gi, lp=lp):
                    s.wait_ge(vm.h, vmAd[gi])
                    s.wait_ge(shp[lp].h, shp[lp].n)
                    shp[lp].inc(s.dma_start(out=y_dram[et * 128:(et + 1) * 128, nc4 * 512:(nc4 + 1) * 512],
                                            in_=YWP[lp]), 16)
                SC(_)
                stY[gi] = (shp[lp].h, shp[lp].n)

            def _(v, nc4=nc4):
                v.wait_ge(tm.h, tm.n)
                mu = rows[0:1, nc4 * 512:(nc4 + 1) * 512]
                vchain(v, v.tensor_scalar(mu, SB[0:1, 0:512], 1.0 / 1024.0, None, ALU.mult))
                vchain(v, v.tensor_scalar(rscr[0:1, 0:512], QB[0:1, 0:512], 1.0 / 1024.0, None, ALU.mult))
                vchain(v, v.tensor_mul(rscr[0:1, 512:1024], mu, mu))
                vchain(v, v.tensor_sub(rscr[0:1, 0:512], rscr[0:1, 0:512], rscr[0:1, 512:1024]))
            VE(_)

            def _(s, nc4=nc4):
                s.wait_ge(vm.h, vm.n)
                schain(s, s.activation(rscr[0:1, 512:1024], rscr[0:1, 0:512], AF.Sqrt, bias=epsc[0:1, 0:1]))
            SC(_)

            def _(v, nc4=nc4):
                v.wait_ge(sm.h, sm.n)
                vchain(v, v.reciprocal(rows[0:1, 2048 + nc4 * 512: 2048 + (nc4 + 1) * 512],
                                       rscr[0:1, 512:1024]))
            VE(_)
            rows_done = vm.n

        vmAp = {}
        ldnB = {}
        stX = {}
        p1_vm = vm.n
        p1_st = [(sh.h, sh.n), (sh2.h, sh2.n)]
        for nc4 in range(4):
            def _(t, nc4=nc4):
                t.wait_ge(vm.h, vm.n)
                t.matmul(UB[:, 0:512], onesf[:], rows[0:1, nc4 * 512:(nc4 + 1) * 512],
                         start=True, stop=True)
                ins = t.matmul(QB[:, 0:512], onesf[:], rows[0:1, 2048 + nc4 * 512: 2048 + (nc4 + 1) * 512],
                               start=True, stop=True)
                tm.inc(ins)
            TE(_)
            tmUB = tm.n

            for et in range(8):
                gi = nc4 * 8 + et
                lp = gi % 2

                def _(sync, nc4=nc4, et=et, gi=gi, lp=lp):
                    sync.wait_ge(vm.h, vmAp.get(gi - 2, p1_vm))
                    sync.wait_ge(*stY[gi])
                    sync.wait_ge(ld.h, ld.n)
                    ld.inc(sync.dma_start(out=ldscr[:, lp * 512:(lp + 1) * 512],
                                          in_=y_dram[et * 128:(et + 1) * 128, nc4 * 512:(nc4 + 1) * 512]), 16)
                SY(_)
                ldnB[gi] = ld.n

                def _(v, nc4=nc4, et=et, gi=gi, lp=lp, tmUB=tmUB):
                    v.wait_ge(ld.h, ldnB[gi])
                    v.wait_ge(tm.h, tmUB)
                    for _hh, _nn in ([stX[gi - 2]] if gi >= 2 else p1_st):
                        v.wait_ge(_hh, _nn)
                    vchain(v, v.tensor_sub(EVAC[lp], ldscr[:, lp * 512:(lp + 1) * 512], UB[:, 0:512]))
                    vchain(v, v.tensor_mul(YWP[lp], EVAC[lp], QB[:, 0:512]))
                    vchain(v, v.tensor_copy(xbf[:, et * TOK + nc4 * 512: et * TOK + (nc4 + 1) * 512],
                                            YWP[lp]))
                VE(_)
                vmAp[gi] = vm.n

                def _(s, nc4=nc4, et=et, gi=gi, lp=lp):
                    s.wait_ge(vm.h, vmAp[gi])
                    s.wait_ge(shp[lp].h, shp[lp].n)
                    shp[lp].inc(s.dma_start(out=x1_dram[et * 128:(et + 1) * 128, nc4 * 512:(nc4 + 1) * 512],
                                            in_=YWP[lp]), 16)
                SC(_)
                stX[gi] = (shp[lp].h, shp[lp].n)
        p1b_st = [(sh.h, sh.n), (sh2.h, sh2.n)]

        for t512 in range(4):
            for fog in range(8):
                p = fog % 2

                def _(g, t512=t512, fog=fog, p=p):
                    if t512 * 8 + fog >= 2:
                        g.wait_ge(tm.h, w1_used[p])
                    lw1[p].inc(g.dma_start(
                        out=w1b[:, p * 4096:(p + 1) * 4096].rearrange("q (a n) -> q a n", a=8),
                        in_=W1.rearrange("(a q) n -> q a n", q=128)[:, :, fog * 512:(fog + 1) * 512]), 16)
                GP(_)
                lw1_tot[p] = lw1[p].n

                for fo4 in range(4):
                    fo = fog * 4 + fo4

                    def _(t, t512=t512, fo=fo, fo4=fo4, p=p, lw1_w=lw1_tot[p]):
                        t.wait_ge(lw1[p].h, lw1_w)
                        if t512 * 32 + fo >= 4:
                            t.wait_ge(vm.h, bank_free_c[fo % 4])
                        else:
                            t.wait_ge(sm.h, sm.n)
                            t.wait_ge(vm.h, vm.n)
                        ins = None
                        for dt in range(8):
                            ins = t.matmul(bank(fo),
                                           w1b[:, p * 4096 + dt * 512 + fo4 * 128: p * 4096 + dt * 512 + (fo4 + 1) * 128],
                                           xbf[:, dt * TOK + t512 * 512:(dt) * TOK + (t512 + 1) * 512],
                                           start=(dt == 0), stop=(dt == 7))
                        tm.inc(ins)
                        w1_used[p] = tm.n
                    TE(_)

                    lpq = fo % 2

                    def _(s, fo=fo, t512=t512, lpq=lpq):
                        s.wait_ge(tm.h, tm.n)
                        s.wait_ge(vm.h, s1free[lpq])
                        if t512 == 0 and fo == 0:
                            for _hh, _nn in p1b_st:
                                s.wait_ge(_hh, _nn)
                        sm.inc(s.activation(S1P[lpq], bank(fo), AF.Sin, scale=0.1))
                        sm.inc(s.activation(S2P[lpq], bank(fo), AF.Sin, scale=0.1,
                                            bias=np.float32(1.5707963267948966)))
                    SC(_)
                    smW1e = sm.n

                    def _(v, fo=fo, t512=t512, lpq=lpq, smW1e=smW1e):
                        v.wait_ge(sm.h, smW1e)
                        vchain(v, v.scalar_tensor_tensor(S1P[lpq], S1P[lpq], CpCc[:, fo:fo + 1],
                                                         bank(fo), ALU.mult, ALU.add))
                        bank_free_c[fo % 4] = vm.n
                        vchain(v, v.scalar_tensor_tensor(S2P[lpq], S2P[lpq], CpSc[:, fo:fo + 1],
                                                         S1P[lpq], ALU.mult, ALU.add))
                        vchain(v, v.tensor_scalar(B2[:, fo * 512:(fo + 1) * 512], S2P[lpq],
                                                  B1C[:, fo:fo + 1], 0.0, ALU.add, ALU.max))
                        s1free[lpq] = vm.n
                    VE(_)

            for dog in range(2):
                for fth in range(2):
                    def _(g, t512=t512, dog=dog, fth=fth):
                        if not (t512 == 0 and dog == 0 and fth == 0):
                            g.wait_ge(tm.h, w2_used[0])
                        lg.inc(g.dma_start(
                            out=w2b[:].rearrange("q (a n) -> q a n", a=16),
                            in_=W2.rearrange("(a q) n -> q a n", q=128)[:, fth * 16:(fth + 1) * 16, dog * 512:(dog + 1) * 512]), 16)
                    GP(_)
                    lg_w2 = lg.n

                    def _(t, t512=t512, dog=dog, fth=fth, lg_w2=lg_w2):
                        t.wait_ge(lg.h, lg_w2)
                        if fth == 0:
                            t.wait_ge(vm.h, vm.n)
                        ins = None
                        for ft16 in range(16):
                            ftg = fth * 16 + ft16
                            for do4 in range(4):
                                ins = t.matmul(bank(do4),
                                               w2b[:, ft16 * 512 + do4 * 128: ft16 * 512 + (do4 + 1) * 128],
                                               B2[:, ftg * 512:(ftg + 1) * 512],
                                               start=(ftg == 0), stop=(ftg == 31))
                        tm.inc(ins)
                        w2_used[0] = tm.n
                    TE(_)

                for do4 in range(4):
                    do = dog * 4 + do4

                    def _(sync, t512=t512, do=do):
                        sync.wait_ge(vm.h, vm.n)
                        sync.wait_ge(*stX[t512 * 8 + do])
                        ld.inc(sync.dma_start(out=ldscr[:, 0:512],
                                              in_=x1_dram[do * 128:(do + 1) * 128, t512 * 512:(t512 + 1) * 512]), 16)
                        sync.wait_ge(ld.h, ld.n)
                    SY(_)

                    def _(v, do=do, do4=do4, t512=t512):
                        v.wait_ge(ld.h, ld.n)
                        v.wait_ge(tm.h, tm.n)
                        if t512 == 0 and do == 0:
                            for _hh, _nn in p1b_st:
                                v.wait_ge(_hh, _nn)
                        vchain(v, v.scalar_tensor_tensor(YW, bank(do4), B2C[:, do:do + 1],
                                                         ldscr[:, 0:512], ALU.add, ALU.add))
                        vchain(v, v.tensor_copy(ybf2[:], YW))
                    VE(_)

                    def _(s, do=do):
                        s.wait_ge(vm.h, vm.n)
                        schain(s, s.activation(sqscr[:], ybf2[:], AF.Square))
                    SC(_)

                    def _(t, t512=t512, do=do):
                        t.wait_ge(sm.h, sm.n)
                        if do == 0:
                            t.wait_ge(vm.h, vm.n)
                        t.matmul(SB[0:1, 0:512], onesb[:], ybf2[:],
                                 start=(do == 0), stop=(do == 7))
                        ins = t.matmul(QB[0:1, 0:512], onesb[:], sqscr[:],
                                       start=(do == 0), stop=(do == 7))
                        tm.inc(ins)
                    TE(_)

                    def _(sync, t512=t512, do=do):
                        sync.wait_ge(vm.h, vm.n)
                        st.inc(sync.dma_start(out=y2_dram[do * 128:(do + 1) * 128, t512 * 512:(t512 + 1) * 512],
                                              in_=YW), 16)
                        sync.wait_ge(st.h, st.n)
                    SY(_)

            def _(v, t512=t512):
                v.wait_ge(tm.h, tm.n)
                mu = rows[0:1, t512 * 512:(t512 + 1) * 512]
                vchain(v, v.tensor_scalar(mu, SB[0:1, 0:512], 1.0 / 1024.0, None, ALU.mult))
                vchain(v, v.tensor_scalar(rscr[0:1, 0:512], QB[0:1, 0:512], 1.0 / 1024.0, None, ALU.mult))
                vchain(v, v.tensor_mul(rscr[0:1, 512:1024], mu, mu))
                vchain(v, v.tensor_sub(rscr[0:1, 0:512], rscr[0:1, 0:512], rscr[0:1, 512:1024]))
            VE(_)

            def _(s, t512=t512):
                s.wait_ge(vm.h, vm.n)
                schain(s, s.activation(rscr[0:1, 512:1024], rscr[0:1, 0:512], AF.Sqrt, bias=epsc[0:1, 0:1]))
            SC(_)

            def _(v, t512=t512):
                v.wait_ge(sm.h, sm.n)
                vchain(v, v.reciprocal(rows[0:1, 2048 + t512 * 512: 2048 + (t512 + 1) * 512],
                                       rscr[0:1, 512:1024]))
            VE(_)

        vmAp2 = {}
        ldnB2 = {}
        stX2 = {}
        p2_vm = vm.n
        p2_sg = [(sh.h, sh.n), (sh2.h, sh2.n)]
        for nc4 in range(4):
            def _(t, nc4=nc4):
                t.wait_ge(vm.h, vm.n)
                t.matmul(UB[:, 0:512], onesf[:], rows[0:1, nc4 * 512:(nc4 + 1) * 512],
                         start=True, stop=True)
                ins = t.matmul(QB[:, 0:512], onesf[:], rows[0:1, 2048 + nc4 * 512: 2048 + (nc4 + 1) * 512],
                               start=True, stop=True)
                tm.inc(ins)
            TE(_)
            tmUB2 = tm.n

            for et in range(8):
                gi = nc4 * 8 + et
                lp = gi % 2

                def _(sync, nc4=nc4, et=et, gi=gi, lp=lp):
                    sync.wait_ge(vm.h, vmAp2.get(gi - 2, p2_vm))
                    sync.wait_ge(ld.h, ld.n)
                    ld.inc(sync.dma_start(out=ldscr[:, lp * 512:(lp + 1) * 512],
                                          in_=y2_dram[et * 128:(et + 1) * 128, nc4 * 512:(nc4 + 1) * 512]), 16)
                SY(_)
                ldnB2[gi] = ld.n

                def _(v, nc4=nc4, et=et, gi=gi, lp=lp, tmUB2=tmUB2):
                    v.wait_ge(ld.h, ldnB2[gi])
                    v.wait_ge(tm.h, tmUB2)
                    for _hh, _nn in ([stX2[gi - 2]] if gi >= 2 else p2_sg):
                        v.wait_ge(_hh, _nn)
                    vchain(v, v.tensor_sub(EVAC[lp], ldscr[:, lp * 512:(lp + 1) * 512], UB[:, 0:512]))
                    vchain(v, v.tensor_mul(YWP[lp], EVAC[lp], QB[:, 0:512]))
                    vchain(v, v.tensor_copy(xbf[:, et * TOK + nc4 * 512: et * TOK + (nc4 + 1) * 512],
                                            YWP[lp]))
                VE(_)
                vmAp2[gi] = vm.n

                def _(s, nc4=nc4, et=et, gi=gi, lp=lp):
                    s.wait_ge(vm.h, vmAp2[gi])
                    s.wait_ge(shp[lp].h, shp[lp].n)
                    shp[lp].inc(s.dma_start(out=x2T[et * 128:(et + 1) * 128, nc4 * 512:(nc4 + 1) * 512],
                                            in_=YWP[lp]), 16)
                SC(_)
                stX2[gi] = (shp[lp].h, shp[lp].n)

        def _(v):
            v.wait_ge(vm.h, vm.n)
            ap = B1[:].rearrange("q (t h c) -> q t h c", t=16, h=16, c=65)
            vchain(v, v.memset(ap[:, :, :, 64:65], 1.0))
        VE(_)

        for pi, W in enumerate((Wq, Wk1, Wv1)):
            def _(g, pi=pi, W=W):
                g.wait_ge(tm.h, tm.n)
                for dt in range(8):
                    lg.inc(g.dma_start(out=wsq[:, dt * 1024:(dt + 1) * 1024],
                                       in_=W[dt * 128:(dt + 1) * 128, :]), 16)
            GP(_)
            lg_w = lg.n

            for et in range(8):
                for nc4 in range(4):
                    gi = et * 4 + nc4

                    def _(t, pi=pi, et=et, nc4=nc4, gi=gi, lg_w=lg_w):
                        t.wait_ge(lg.h, lg_w)
                        t.wait_ge(sm.h, sm.n)
                        t.wait_ge(vm.h, vm.n)
                        ins = None
                        for dt in range(8):
                            ins = t.matmul(bank(gi),
                                           wsq[:, dt * 1024 + et * 128: dt * 1024 + (et + 1) * 128],
                                           xbf[:, dt * TOK + nc4 * 512: dt * TOK + (nc4 + 1) * 512],
                                           start=(dt == 0), stop=(dt == 7))
                        tm.inc(ins)
                    TE(_)

                    if pi < 2:
                        def _(s, pi=pi, et=et, nc4=nc4, gi=gi):
                            s.wait_ge(tm.h, tm.n)
                            if nc4 == 0:
                                s.wait_ge(st.h, st.n)
                            schain(s, s.activation(
                                B2[:, (et % 2) * 2048 + nc4 * 512:(et % 2) * 2048 + (nc4 + 1) * 512],
                                bank(gi), AF.Copy))
                        SC(_)

                        if nc4 == 3:
                            def _(sync, pi=pi, et=et):
                                sync.wait_ge(sm.h, sm.n)
                                dst = (q_fm[et * 128:(et + 1) * 128, :] if pi == 0
                                       else io["kq_st"](et))
                                st.inc(sync.dma_start(out=dst,
                                                      in_=B2[:, (et % 2) * 2048:(et % 2) * 2048 + 2048]), 16)
                                sync.wait_ge(st.h, st.n)
                            SY(_)
                    else:
                        def _(s, et=et, nc4=nc4, gi=gi):
                            s.wait_ge(tm.h, tm.n)
                            schain(s, s.activation(B2[:, nc4 * 512:(nc4 + 1) * 512], bank(gi), AF.Copy))
                        SC(_)

                        for tt in range(4):
                            def _(t, et=et, nc4=nc4, tt=tt):
                                t.wait_ge(sm.h, sm.n)
                                t.wait_ge(vm.h, vm.n)
                                ins = t.transpose(ps_tr[:, 0:128],
                                                  B2[:, nc4 * 512 + tt * 128: nc4 * 512 + (tt + 1) * 128],
                                                  ident[:])
                                tm.inc(ins)
                            TE(_)

                            def _(v, et=et, nc4=nc4, tt=tt):
                                v.wait_ge(tm.h, tm.n)
                                blk = nc4 * 4 + tt
                                vchain(v, v.tensor_copy(
                                    B1[:, blk * 1040 + (2 * et) * 65: blk * 1040 + (2 * et) * 65 + 64],
                                    ps_tr[:, 0:64]))
                                vchain(v, v.tensor_copy(
                                    B1[:, blk * 1040 + (2 * et + 1) * 65: blk * 1040 + (2 * et + 1) * 65 + 64],
                                    ps_tr[:, 64:128]))
                            VE(_)

        for blk in range(16):
            def _(g, blk=blk):
                g.wait_ge(vm.h, vm.n)
                for half in range(2):
                    sg.inc(g.dma_start(out=io["vt_st"](blk, half),
                                       in_=B1[:, blk * 1040 + half * 520: blk * 1040 + (half + 1) * 520]), 16)
                g.wait_ge(sg.h, sg.n)
            GP(_)

    es.close()
    return EQ


def _flush(nc, EQs):
    if not isinstance(EQs, list):
        EQs = [EQs]
    with nc.Block() as block:
        block.sync(lambda e: [q["sync"].replay(e) for q in EQs])
        block.gpsimd(lambda e: [q["gpsimd"].replay(e) for q in EQs])
        block.vector(lambda e: [q["vector"].replay(e) for q in EQs])
        block.scalar(lambda e: [q["scalar"].replay(e) for q in EQs])
        block.tensor(lambda e: [q["tensor"].replay(e) for q in EQs])


def _declare_io(nc, io, inputs, outputs):
    DT = {"F32": F32, "BF16": BF16, "I32": I32}
    for n, shp, dt in inputs:
        io[n] = nc.dram_tensor(n, shp, DT[dt], kind="ExternalInput").ap()
    for n, shp, dt in outputs:
        io[n] = nc.dram_tensor(n, shp, DT[dt], kind="ExternalOutput").ap()


def build_launch_C():
    nc = bass.Bass("TRN2", target_bir_lowering=False)
    io = {}
    _declare_io(nc, io, C_INPUTS, C_OUTPUTS)
    io["wkv_load"] = lambda g, dst, ft, th: g.dma_start(
        out=dst, in_=io["wkv_in"][ft * 128:(ft + 1) * 128, th * 1024:(th + 1) * 1024])
    io["kq_st"] = lambda et: io["k_fm"][et * 128:(et + 1) * 128, :]
    io["vt_st"] = lambda blk, half: io["v_tm"][blk * 128:(blk + 1) * 128,
                                               half * 520:(half + 1) * 520]
    EQ = emit_C(nc, io)
    _flush(nc, EQ)
    return nc


SWEEPS = [3584, 4096, 7680, 8192]
QGOFF = [0, 28, 60, 120]
NSLOT = 184


def qg_positions(core):
    b, h0, t0 = core_chunks(core)
    return sorted([h0, h0 + 512, t0, t0 + 512])


D_INPUTS = [("q_in", [D, TOK], "BF16"), ("kdiag", [D, TOK], "BF16"),
            ("vdiag", [TOK, NH * 65], "BF16"), ("bias_tbl", [128, NH * NSLOT], "F32"),
            ("dcol_in", [128, NH], "F32"), ("fvb_in", [128, NH], "F32"),
            ("x2T_in", [D, TOK], "F32"), ("Wo1", [D, D], "F32"),
            ("w1T", [128, 8 * FF], "F32"), ("w2T", [128, 8 * FF], "F32"),
            ("h1T", [128, 32 * FF], "F32"), ("h2T", [128, 32 * FF], "F32"),
            ("w3T", [128, 8 * FF], "F32"),
            ("s1c_in", [128, 32], "F32"), ("s2c_in", [128, 32], "F32"),
            ("sh1_in", [128, 32], "F32"), ("sh2_in", [128, 32], "F32"),
            ("s3c_in", [128, 8], "F32")]
D_OUTPUTS = [("x4T", [D, TOK], "F32")]


def emit_D(nc, io):
    es = ExitStack()
    (q_in, kdiag, vdiag, bias_tbl, dcol_in, fvb_in, x2T_in, Wo1, w1T, w2T,
     h1T, h2T, w3T, s1c_in, s2c_in, sh1_in, sh2_in, s3c_in) = (io[n] for n, _, _ in D_INPUTS)
    x4T = io["x4T"]
    kf_ap = io["kf_ap"]
    vt_ap = io["vt_ap"]

    y3_dram = nc.dram_tensor("y3_dram", [D, TOK], F32, kind="Internal").ap()
    x3bf_dram = nc.dram_tensor("x3bf_dram", [D, TOK], BF16, kind="Internal").ap()
    x3_dram = nc.dram_tensor("x3_dram", [D, TOK], F32, kind="Internal").ap()
    y4_dram = nc.dram_tensor("y4_dram", [D, TOK], F32, kind="Internal").ap()

    xbfp = es.enter_context(nc.sbuf_tensor("d_xbfp", [128, 2 * 512], BF16))
    fscr = es.enter_context(nc.sbuf_tensor("d_fscr", [128, 4 * 512], F32))
    ldscr = es.enter_context(nc.sbuf_tensor("d_ldscr", [128, 2 * 512], F32))
    sqscr = es.enter_context(nc.sbuf_tensor("d_sqscr", [128, 2 * 512], BF16))
    ybf2 = es.enter_context(nc.sbuf_tensor("d_ybf2", [128, 2 * 512], BF16))
    rscr = es.enter_context(nc.sbuf_tensor("d_rscr", [1, 2 * 512], F32))
    onesb = es.enter_context(nc.sbuf_tensor("d_onesb", [128, 1], BF16))
    onesf = es.enter_context(nc.sbuf_tensor("d_onesf", [1, 128], F32))
    epsc = es.enter_context(nc.sbuf_tensor("d_epsc", [128, 1], F32))
    scols = es.enter_context(nc.sbuf_tensor("d_scols", [128, 136], F32))

    ps = es.enter_context(nc.psum_tensor("d_ps", [128, 8 * 512], F32))

    ld = Sem(nc, es, "ld")
    lg = Sem(nc, es, "lg")
    lk = [Sem(nc, es, "lka"), Sem(nc, es, "lkb")]
    lv = [Sem(nc, es, "lva"), Sem(nc, es, "lvb")]
    lw = [Sem(nc, es, "lwa"), Sem(nc, es, "lwb")]
    vm = Sem(nc, es, "vm")
    sm = Sem(nc, es, "sm")
    tm = Sem(nc, es, "tm")
    st = Sem(nc, es, "st")
    gm = Sem(nc, es, "gm")
    sg = Sem(nc, es, "d_sg")
    sg2 = Sem(nc, es, "d_sg2")
    st2 = Sem(nc, es, "d_st2")
    sh = Sem(nc, es, "d_sh")
    sh2 = Sem(nc, es, "d_sh2")

    S1C = scols[:, 0:32]
    S2C = scols[:, 32:64]
    SH1C = scols[:, 64:96]
    SH2C = scols[:, 96:128]
    S3C = scols[:, 128:136]

    YT, YW = fscr[:, 0:512], fscr[:, 512:1024]
    YT2, YW2 = fscr[:, 1024:1536], fscr[:, 1536:2048]

    def bank(i):
        return ps[:, (i % 4) * 512:(i % 4 + 1) * 512]

    SCB = [ps[:, (4 + i) * 512:(5 + i) * 512] for i in range(4)]
    SB = ps[:, 6 * 512:7 * 512]
    QB = ps[:, 7 * 512:8 * 512]

    es2 = ExitStack()
    es3 = ExitStack()
    att = es3.enter_context(nc.sbuf_tensor("d_att", [128, 8 * TOK], BF16, side="right"))
    q_sb = es2.enter_context(nc.sbuf_tensor("d_q_sb", [128, 2 * 2 * TOK], BF16, side="right"))
    fv = es2.enter_context(nc.sbuf_tensor("d_fv", [128, 16 * 512], BF16, side="right"))
    gt = es2.enter_context(nc.sbuf_tensor("d_gt", [128, 4 * 512], F32, side="right"))
    msk = es2.enter_context(nc.sbuf_tensor("d_msk", [128, 512], F32, side="right"))
    dti = es2.enter_context(nc.sbuf_tensor("d_dti", [128, 5 * 512], F32, side="right"))
    iscr2 = es2.enter_context(nc.sbuf_tensor("d_iscr2", [128, 512], I32, side="right"))
    btab = es2.enter_context(nc.sbuf_tensor("d_btab", [128, 2 * 4 * NSLOT], F32, side="right"))
    dcol = es2.enter_context(nc.sbuf_tensor("d_dcol", [128, NH], F32, side="right"))
    fvb = es2.enter_context(nc.sbuf_tensor("d_fvb", [128, NH], F32, side="right"))
    ksb = es2.enter_context(nc.sbuf_tensor("d_ksb", [128, 2 * 2 * 512], BF16, side="right"))
    vsb = es2.enter_context(nc.sbuf_tensor("d_vsb", [128, 2 * 4 * 260], BF16, side="right"))
    kdsb = es2.enter_context(nc.sbuf_tensor("d_kdsb", [128, 2 * 2 * TOK], BF16, side="right"))
    vdsb = es2.enter_context(nc.sbuf_tensor("d_vdsb", [128, 2 * 16 * 260], BF16, side="right"))
    psb = es2.enter_context(nc.sbuf_tensor("d_psb", [128, 4 * 512], BF16, side="right"))
    pdg = es2.enter_context(nc.sbuf_tensor("d_pdg", [128, 2 * 512], BF16, side="right"))
    praw = es2.enter_context(nc.sbuf_tensor("d_praw", [128, 2 * 512], F32, side="right"))
    blw = es2.enter_context(nc.sbuf_tensor("d_blw", [128, 4 * 512], BF16, side="right"))
    mrg4 = es2.enter_context(nc.sbuf_tensor("d_mrg4", [128, 4 * 512], F32, side="right"))

    kv_used = {0: 0, 1: 0}
    kv_seen = {0: False, 1: False}
    score_done = {}
    pv_done = {0: 0, 1: 0, 2: 0, 3: 0}
    praw_free = {0: 0, 1: 0}
    pdg_free = {0: 0, 1: 0}
    pvh = {}
    rscr_free = {0: 0, 1: 0}
    hgtm = {0: 0, 1: 0}
    hgsm = {0: 0, 1: 0}
    bank_free = {0: 0, 1: 0, 2: 0, 3: 0}
    sci_ctr = [0]

    for _i in range(4):
        score_done[_i] = (sm.h, 0)

    EQ = {n_: EngQ() for n_ in ("sync", "gpsimd", "vector", "scalar", "tensor")}
    SY = lambda f: f(EQ["sync"])
    GP = lambda f: f(EQ["gpsimd"])
    VE = lambda f: f(EQ["vector"])
    SC = lambda f: f(EQ["scalar"])
    TE = lambda f: f(EQ["tensor"])
    if True:
        def vchain(v, ins):
            vm.inc(ins)
            v.wait_ge(vm.h, vm.n)

        def schain(s, ins):
            sm.inc(ins)
            s.wait_ge(sm.h, sm.n)

        def _(sync):
            for t_, dst in ((dcol_in, dcol[:]), (fvb_in, fvb[:]),
                            (s1c_in, S1C), (s2c_in, S2C), (sh1_in, SH1C),
                            (sh2_in, SH2C), (s3c_in, S3C)):
                ld.inc(sync.dma_start(out=dst, in_=t_), 16)
                sync.wait_ge(ld.h, ld.n)
        SY(_)

        for toff in range(5):
            def _(g, toff=toff):
                g.wait_ge(vm.h, vm.n)
                if toff < 4:
                    gm.inc(g.iota(iscr2[:], pattern=[[-1, 512]], base=128 * toff + 127,
                                  channel_multiplier=1))
                else:
                    gm.inc(g.iota(iscr2[:], pattern=[[1, 512]], base=0,
                                  channel_multiplier=0))
            GP(_)

            def _(v, toff=toff):
                v.wait_ge(gm.h, gm.n)
                vchain(v, v.tensor_copy(dti[:, toff * 512:(toff + 1) * 512], iscr2[:]))
            VE(_)

        def _(v):
            vchain(v, v.memset(onesb[:], 1.0))
            vchain(v, v.memset(onesf[:], 1.0))
            vchain(v, v.memset(epsc[:], EPS_LN))
        VE(_)

        for hg in range(4):
            heads = [4 * hg + i for i in range(4)]

            for i4, h in enumerate(heads):
                def _(v, i4=i4, h=h):
                    v.wait_ge(ld.h, ld.n)
                    for toff in range(4):
                        fslc = fv[:, (i4 * 4 + toff) * 512:(i4 * 4 + toff + 1) * 512]
                        vchain(v, v.tensor_scalar(fslc, dti[:, toff * 512:(toff + 1) * 512],
                                                  float(np.float32(SLOPES[h])), fvb[:, h:h + 1],
                                                  ALU.mult, ALU.add))
                        vchain(v, v.tensor_scalar(fslc, fslc, 50.0, None, ALU.min))
                VE(_)

                def _(s, i4=i4, h=h):
                    s.wait_ge(vm.h, vm.n)
                    for toff in range(4):
                        fslc = fv[:, (i4 * 4 + toff) * 512:(i4 * 4 + toff + 1) * 512]
                        sm.inc(s.activation(fslc, fslc, AF.Exp))
                    schain(s, s.activation(gt[:, i4 * 512:(i4 + 1) * 512],
                                           dti[:, 4 * 512:5 * 512], AF.Exp,
                                           scale=float(np.float32(-SLOPES[h]))))
                SC(_)

                def _(v, i4=i4):
                    v.wait_ge(sm.h, sm.n)
                    for toff in range(4):
                        fslc = fv[:, (i4 * 4 + toff) * 512:(i4 * 4 + toff + 1) * 512]
                        vchain(v, v.tensor_scalar(msk[:], dti[:, toff * 512:(toff + 1) * 512],
                                                  127.5, None, ALU.is_lt))
                        vchain(v, v.tensor_mul(fslc, fslc, msk[:]))
                VE(_)

            par = hg % 2

            def _(g, hg=hg, par=par):
                g.wait_ge(tm.h, hgtm[par])
                g.wait_ge(sm.h, hgsm[par])
                g.wait_ge(lg.h, lg.n)
                for ct in range(2):
                    lg.inc(g.dma_start(out=kdsb[:, par * 2 * TOK + ct * TOK: par * 2 * TOK + (ct + 1) * TOK],
                                       in_=io["kdiag_ap"](hg, ct)), 16)
                    lg.inc(g.dma_start(out=q_sb[:, par * 2 * TOK + ct * TOK: par * 2 * TOK + (ct + 1) * TOK],
                                       in_=q_in[(2 * hg + ct) * 128:(2 * hg + ct + 1) * 128, :]), 16)
                lg.inc(g.dma_start(out=vdsb[:, par * 4160:(par + 1) * 4160].rearrange("p (a n) -> p a n", a=16),
                                   in_=io["vdiag_ap"](hg)), 16)
                lg.inc(g.dma_start(out=btab[:, par * 4 * NSLOT:(par + 1) * 4 * NSLOT],
                                   in_=bias_tbl[:, 4 * hg * NSLOT:(4 * hg + 4) * NSLOT]), 16)
            GP(_)
            lg_diag = lg.n

            for qg in range(4):
                nkv = SWEEPS[qg] // 512

                units = [(kc, sub, pr) for kc in range(nkv)
                         for sub in range(4) for pr in range(2)]
                NB = len(units)
                ubank = []
                for _u in range(NB):
                    sci_ctr[0] += 2
                    ubank.append((sci_ctr[0] % 4, (sci_ctr[0] + 1) % 4))
                kcln = {}
                tmS = {}
                smE = {}

                def emit_loads(kc, hg=hg, qg=qg):
                    pk = kc % 2

                    def _(sync, hg=hg, qg=qg, kc=kc, pk=pk):
                        if hg == 2 and qg == 0 and kc == 0 and "agw" in io:
                            io["agw"](sync)
                        if kv_seen[pk]:
                            sync.wait_ge(tm.h, kv_used[pk])
                            sync.wait_ge(lk[pk].h, lk[pk].n)
                            sync.wait_ge(lv[pk].h, lv[pk].n)
                        kv_seen[pk] = True
                        for ct in range(2):
                            lk[pk].inc(sync.dma_start(
                                out=ksb[:, (pk * 2 + ct) * 512:(pk * 2 + ct + 1) * 512],
                                in_=kf_ap((2 * hg + ct) * 128, kc)), 16)
                        lv[pk].inc(sync.dma_start(
                            out=vsb[:, pk * 1040:(pk + 1) * 1040].rearrange("p (a n) -> p a n", a=4),
                            in_=vt_ap(kc, hg)), 16)
                    SY(_)
                    kcln[kc] = (lk[pk].n, lv[pk].n)

                def emit_S(n, qg=qg):
                    kc, sub, pr = units[n]
                    pk = kc % 2
                    scA, scB = ubank[n]
                    lk_n, lv_n = kcln[kc]

                    def _(t, qg=qg, sub=sub, pr=pr, scA=scA, scB=scB, pk=pk,
                          lk_n=lk_n, lv_n=lv_n, lg_diag=lg_diag):
                        t.wait_ge(lk[pk].h, lk_n)
                        t.wait_ge(lv[pk].h, lv_n)
                        t.wait_ge(lg.h, lg_diag)
                        kt = ksb[:, (pk * 2 + pr) * 512 + sub * 128:
                                 (pk * 2 + pr) * 512 + (sub + 1) * 128]
                        qt = q_sb[:, par * 2 * TOK + pr * TOK + qg * 512:
                                  par * 2 * TOK + pr * TOK + (qg + 1) * 512]
                        t.wait_ge(*score_done[scA])
                        t.matmul(SCB[scA][:, 0:512], kt[0:64, :], qt[0:64, :],
                                 start=True, stop=True)
                        t.wait_ge(*score_done[scB])
                        ins = t.matmul(SCB[scB][:, 0:512], kt[64:128, :], qt[64:128, :],
                                       start=True, stop=True)
                        tm.inc(ins)
                        kv_used[pk] = tm.n
                    TE(_)
                    tmS[n] = tm.n

                def emit_exp(n, qg=qg, hg=hg):
                    kc, sub, pr = units[n]
                    slot = kc * 4 + sub
                    scA, scB = ubank[n]

                    def _(s, qg=qg, slot=slot, pr=pr, scA=scA, scB=scB, n=n):
                        s.wait_ge(tm.h, tmS[n])
                        for ih, sc in ((2 * pr, scA), (2 * pr + 1, scB)):
                            s.wait_ge(tm.h, pv_done[sc])
                            col = btab[:, par * 4 * NSLOT + ih * NSLOT + QGOFF[qg] + slot:
                                       par * 4 * NSLOT + ih * NSLOT + QGOFF[qg] + slot + 1]
                            sm.inc(s.activation(psb[:, sc * 512:(sc + 1) * 512],
                                                SCB[sc][:, 0:512], AF.Exp,
                                                scale=0.125, bias=col))
                            score_done[sc] = (sm.h, sm.n)
                    SC(_)
                    smE[n] = sm.n

                def emit_PV(n, nkv=nkv):
                    kc, sub, pr = units[n]
                    slot = kc * 4 + sub
                    pk = kc % 2
                    last = (slot == nkv * 4 - 1)
                    scA, scB = ubank[n]

                    def _(t, slot=slot, pr=pr, scA=scA, scB=scB, pk=pk, last=last, n=n):
                        t.wait_ge(sm.h, smE[n])
                        for ih, sc in ((2 * pr, scA), (2 * pr + 1, scB)):
                            ins = t.matmul(bank(ih)[0:65, :],
                                           vsb[:, pk * 1040 + (slot % 4) * 260 + ih * 65:
                                               pk * 1040 + (slot % 4) * 260 + ih * 65 + 65],
                                           psb[:, sc * 512:(sc + 1) * 512],
                                           start=(slot == 0), stop=last)
                            tm.inc(ins)
                            pv_done[sc] = tm.n
                        kv_used[pk] = tm.n
                    TE(_)

                emit_loads(0)
                emit_S(0)
                for n in range(NB):
                    emit_exp(n)
                    if n + 1 < NB:
                        if units[n + 1][0] != units[n][0]:
                            emit_loads(units[n + 1][0])
                        emit_S(n + 1)
                    emit_PV(n)

                for i4 in range(4):
                    def _(v, i4=i4):
                        v.wait_ge(tm.h, tm.n)
                        vchain(v, v.tensor_mul(blw[:, i4 * 512:(i4 + 1) * 512][0:65, :],
                                               bank(i4)[0:65, :],
                                               gt[:, i4 * 512:(i4 + 1) * 512][0:65, :]))
                    VE(_)

                dunits = [(toff, pr) for toff in range(4) for pr in range(2)]
                dbank = []
                for _u in range(8):
                    sci_ctr[0] += 2
                    dbank.append((sci_ctr[0] % 4, (sci_ctr[0] + 1) % 4))
                tmS2 = {}
                smE2 = {}
                vmD = {}

                def emit_S2(u, qg=qg, lg_diag=lg_diag):
                    toff, pr = dunits[u]
                    scA, scB = dbank[u]

                    def _(t, qg=qg, toff=toff, pr=pr, scA=scA, scB=scB, lg_diag=lg_diag, u=u):
                        t.wait_ge(lg.h, lg_diag)
                        if u == 0:
                            t.wait_ge(vm.h, vm.n)
                        kt = kdsb[:, par * 2 * TOK + pr * TOK + qg * 512 + toff * 128:
                                  par * 2 * TOK + pr * TOK + qg * 512 + (toff + 1) * 128]
                        qt = q_sb[:, par * 2 * TOK + pr * TOK + qg * 512:
                                  par * 2 * TOK + pr * TOK + (qg + 1) * 512]
                        t.wait_ge(*score_done[scA])
                        t.matmul(SCB[scA][:, 0:512], kt[0:64, :], qt[0:64, :],
                                 start=True, stop=True)
                        t.wait_ge(*score_done[scB])
                        ins = t.matmul(SCB[scB][:, 0:512], kt[64:128, :], qt[64:128, :],
                                       start=True, stop=True)
                        tm.inc(ins)
                    TE(_)
                    tmS2[u] = tm.n

                def emit_dexp(u, j2, hg=hg):
                    toff, pr = dunits[u]
                    ih = 2 * pr + j2
                    h = 4 * hg + ih
                    sc_i = dbank[u][j2]
                    pr_i = (toff * 4 + ih) % 2

                    def _(s, h=h, sc_i=sc_i, pr_i=pr_i, u=u):
                        s.wait_ge(tm.h, tmS2[u])
                        s.wait_ge(vm.h, praw_free[pr_i])
                        sm.inc(s.activation(praw[:, pr_i * 512:(pr_i + 1) * 512],
                                            SCB[sc_i][:, 0:512], AF.Exp,
                                            scale=0.125, bias=dcol[:, h:h + 1]))
                        score_done[sc_i] = (sm.h, sm.n)
                    SC(_)
                    smE2[(u, j2)] = sm.n

                def emit_dmul(u, j2):
                    toff, pr = dunits[u]
                    ih = 2 * pr + j2
                    pr_i = (toff * 4 + ih) % 2

                    def _(v, ih=ih, toff=toff, pr_i=pr_i, u=u, j2=j2):
                        v.wait_ge(sm.h, smE2[(u, j2)])
                        v.wait_ge(tm.h, pdg_free[pr_i])
                        vchain(v, v.tensor_mul(pdg[:, pr_i * 512:(pr_i + 1) * 512],
                                               praw[:, pr_i * 512:(pr_i + 1) * 512],
                                               fv[:, (ih * 4 + toff) * 512:(ih * 4 + toff + 1) * 512]))
                        praw_free[pr_i] = vm.n
                    VE(_)
                    vmD[(u, j2)] = vm.n

                def emit_dPV(u, j2, qg=qg):
                    toff, pr = dunits[u]
                    ih = 2 * pr + j2
                    pr_i = (toff * 4 + ih) % 2

                    def _(t, qg=qg, toff=toff, ih=ih, pr_i=pr_i, u=u, j2=j2):
                        t.wait_ge(vm.h, vmD[(u, j2)])
                        blk = qg * 4 + toff
                        ins = t.matmul(bank(ih)[0:65, :],
                                       vdsb[:, par * 4160 + blk * 260 + ih * 65:
                                           par * 4160 + blk * 260 + ih * 65 + 65],
                                       pdg[:, pr_i * 512:(pr_i + 1) * 512],
                                       start=(toff == 0), stop=(toff == 3))
                        tm.inc(ins)
                        pdg_free[pr_i] = tm.n
                    TE(_)
                    if toff == 3:
                        pvh[ih] = tm.n

                emit_S2(0)
                for u in range(8):
                    emit_dexp(u, 0)
                    emit_dmul(u, 0)
                    if u + 1 < 8:
                        emit_S2(u + 1)
                    emit_dPV(u, 0)
                    emit_dexp(u, 1)
                    emit_dmul(u, 1)
                    emit_dPV(u, 1)

                tmPV_all = tm.n
                vmM = {}
                smR = {}
                tmB = {}
                QBs = [SB, QB]
                for i4 in range(4):
                    h = heads[i4]
                    mpar = i4 % 2

                    def _(v, i4=i4, tmPV_all=tmPV_all):
                        v.wait_ge(tm.h, pvh[i4])
                        vchain(v, v.tensor_add(mrg4[:, i4 * 512:(i4 + 1) * 512][0:65, :],
                                               blw[:, i4 * 512:(i4 + 1) * 512][0:65, :],
                                               bank(i4)[0:65, :]))
                    VE(_)
                    vmM[i4] = vm.n

                    def _(v, i4=i4, mpar=mpar):
                        v.wait_ge(tm.h, rscr_free[mpar])
                        vchain(v, v.reciprocal(rscr[0:1, mpar * 512:(mpar + 1) * 512],
                                               mrg4[64:65, i4 * 512:(i4 + 1) * 512]))
                    VE(_)
                    smR[i4] = vm.n

                    def _(t, i4=i4, mpar=mpar):
                        t.wait_ge(vm.h, smR[i4])
                        t.wait_ge(*score_done[2 + mpar])
                        ins = t.matmul(QBs[mpar][0:64, :], onesf[0:1, 0:64],
                                       rscr[0:1, mpar * 512:(mpar + 1) * 512],
                                       start=True, stop=True)
                        tm.inc(ins)
                        rscr_free[mpar] = tm.n
                    TE(_)
                    tmB[i4] = tm.n

                    def _(v, i4=i4, h=h, qg=qg, mpar=mpar):
                        v.wait_ge(tm.h, tmB[i4])
                        at = att[:, (h // 2) * TOK + qg * 512:(h // 2) * TOK + (qg + 1) * 512]
                        vchain(v, v.tensor_mul(at[(h % 2) * 64:(h % 2) * 64 + 64, :],
                                               mrg4[0:64, i4 * 512:(i4 + 1) * 512],
                                               QBs[mpar][0:64, :]))
                        score_done[2 + mpar] = (vm.h, vm.n)
                    VE(_)

            hgtm[par] = tm.n
            hgsm[par] = sm.n

        es2.close()
        rows = es.enter_context(nc.sbuf_tensor("d_rows", [1, 4096], F32))
        wo_sb = es3.enter_context(nc.sbuf_tensor("d_wo_sb", [128, 8 * 1024], BF16, side="right"))

        def _(g):
            g.wait_ge(tm.h, tm.n)
            g.wait_ge(vm.h, vm.n)
            for dt in range(8):
                lg.inc(g.dma_start(out=wo_sb[:, dt * 1024:(dt + 1) * 1024],
                                   in_=Wo1[dt * 128:(dt + 1) * 128, :]), 16)
        GP(_)
        lg_wo = lg.n

        EVACD = [YT, YT2]
        YWD = [YW, YW2]
        sgp = [sg, sg2]
        shp = [sh, sh2]
        tmP3 = {}
        ldn3 = {}
        smEv3 = {}
        vmAd3 = {}
        smSq3 = {}
        ldfree3 = {0: 0, 1: 0}
        evfree3 = {0: 0, 1: 0}
        stY3 = {}
        rows_done3 = 0
        for nc4 in range(4):
            for et in range(8):
                gi = nc4 * 8 + et
                lp = gi % 2

                def _(t, nc4=nc4, et=et, gi=gi):
                    if gi == 0:
                        t.wait_ge(lg.h, lg_wo)
                        t.wait_ge(vm.h, vm.n)
                    if gi >= 4:
                        t.wait_ge(vm.h, bank_free[gi % 4])
                    ins = None
                    for dt in range(8):
                        ins = t.matmul(bank(gi),
                                       wo_sb[:, dt * 1024 + et * 128: dt * 1024 + (et + 1) * 128],
                                       att[:, dt * TOK + nc4 * 512: dt * TOK + (nc4 + 1) * 512],
                                       start=(dt == 0), stop=(dt == 7))
                    tm.inc(ins)
                TE(_)
                tmP3[gi] = tm.n

                def _(sync, nc4=nc4, et=et, lp=lp):
                    sync.wait_ge(vm.h, ldfree3[lp])
                    sync.wait_ge(ld.h, ld.n)
                    ld.inc(sync.dma_start(out=ldscr[:, lp * 512:(lp + 1) * 512],
                                          in_=x2T_in[et * 128:(et + 1) * 128,
                                                     nc4 * 512:(nc4 + 1) * 512]), 16)
                SY(_)
                ldn3[gi] = ld.n

                def _(s, gi=gi, lp=lp):
                    s.wait_ge(tm.h, tmP3[gi])
                    s.wait_ge(vm.h, evfree3[lp])
                    sm.inc(s.activation(EVACD[lp], bank(gi), AF.Copy))
                SC(_)
                smEv3[gi] = sm.n

                def _(v, nc4=nc4, et=et, gi=gi, lp=lp):
                    v.wait_ge(sm.h, smEv3[gi])
                    v.wait_ge(ld.h, ldn3[gi])
                    v.wait_ge(*stY3.get(gi - 2, (sh.h, 0)))
                    v.wait_ge(tm.h, tm.n)
                    vchain(v, v.tensor_add(YWD[lp], EVACD[lp], ldscr[:, lp * 512:(lp + 1) * 512]))
                    vchain(v, v.tensor_copy(ybf2[:, 0:512], YWD[lp]))
                    bank_free[gi % 4] = vm.n
                    ldfree3[lp] = vm.n
                    evfree3[lp] = vm.n
                VE(_)
                vmAd3[gi] = vm.n

                def _(s, gi=gi):
                    s.wait_ge(vm.h, vmAd3[gi])
                    s.wait_ge(tm.h, tm.n)
                    sm.inc(s.activation(sqscr[:, 0:512], ybf2[:, 0:512], AF.Square))
                SC(_)
                smSq3[gi] = sm.n

                def _(t, nc4=nc4, et=et, gi=gi, rd=rows_done3):
                    t.wait_ge(sm.h, smSq3[gi])
                    if et == 0:
                        t.wait_ge(vm.h, rd)
                    t.matmul(SB[0:1, 0:512], onesb[:], ybf2[:, 0:512],
                             start=(et == 0), stop=(et == 7))
                    ins = t.matmul(QB[0:1, 0:512], onesb[:], sqscr[:, 0:512],
                                   start=(et == 0), stop=(et == 7))
                    tm.inc(ins)
                TE(_)

                def _(s, nc4=nc4, et=et, gi=gi, lp=lp):
                    s.wait_ge(vm.h, vmAd3[gi])
                    s.wait_ge(shp[lp].h, shp[lp].n)
                    shp[lp].inc(s.dma_start(out=y3_dram[et * 128:(et + 1) * 128,
                                                        nc4 * 512:(nc4 + 1) * 512],
                                            in_=YWD[lp]), 16)
                SC(_)
                stY3[gi] = (shp[lp].h, shp[lp].n)

            def _(v, nc4=nc4):
                v.wait_ge(tm.h, tm.n)
                mu = rows[0:1, nc4 * 512:(nc4 + 1) * 512]
                vchain(v, v.tensor_scalar(mu, SB[0:1, 0:512], 1.0 / 1024.0, None, ALU.mult))
                vchain(v, v.tensor_scalar(rscr[0:1, 0:512], QB[0:1, 0:512], 1.0 / 1024.0, None, ALU.mult))
                vchain(v, v.tensor_mul(rscr[0:1, 512:1024], mu, mu))
                vchain(v, v.tensor_sub(rscr[0:1, 0:512], rscr[0:1, 0:512], rscr[0:1, 512:1024]))
            VE(_)

            def _(s):
                s.wait_ge(vm.h, vm.n)
                schain(s, s.activation(rscr[0:1, 512:1024], rscr[0:1, 0:512], AF.Sqrt,
                                       bias=epsc[0:1, 0:1]))
            SC(_)

            def _(v, nc4=nc4):
                v.wait_ge(sm.h, sm.n)
                vchain(v, v.reciprocal(rows[0:1, 2048 + nc4 * 512: 2048 + (nc4 + 1) * 512],
                                       rscr[0:1, 512:1024]))
            VE(_)
            rows_done3 = vm.n

        vmAp3 = {}
        ldnB3 = {}
        stX3f = {}
        stX3b = {}
        p2_vmD = vm.n
        p2_sgD = [(sh.h, sh.n), (sh2.h, sh2.n)]
        for nc4 in range(4):
            def _(t, nc4=nc4):
                t.wait_ge(vm.h, vm.n)
                t.matmul(SB[:, 0:512], onesf[:], rows[0:1, nc4 * 512:(nc4 + 1) * 512],
                         start=True, stop=True)
                ins = t.matmul(QB[:, 0:512], onesf[:],
                               rows[0:1, 2048 + nc4 * 512: 2048 + (nc4 + 1) * 512],
                               start=True, stop=True)
                tm.inc(ins)
            TE(_)
            tmUB3 = tm.n

            for et in range(8):
                gi = nc4 * 8 + et
                lp = gi % 2

                def _(sync, nc4=nc4, et=et, gi=gi, lp=lp):
                    sync.wait_ge(vm.h, vmAp3.get(gi - 2, p2_vmD))
                    sync.wait_ge(*stY3[gi])
                    sync.wait_ge(ld.h, ld.n)
                    ld.inc(sync.dma_start(out=ldscr[:, lp * 512:(lp + 1) * 512],
                                          in_=y3_dram[et * 128:(et + 1) * 128,
                                                      nc4 * 512:(nc4 + 1) * 512]), 16)
                SY(_)
                ldnB3[gi] = ld.n

                def _(v, nc4=nc4, et=et, gi=gi, lp=lp, tmUB3=tmUB3):
                    v.wait_ge(ld.h, ldnB3[gi])
                    v.wait_ge(tm.h, tmUB3)
                    for _hh, _nn in ([stX3f[gi - 2], stX3b[gi - 2]] if gi >= 2 else p2_sgD):
                        v.wait_ge(_hh, _nn)
                    vchain(v, v.tensor_sub(EVACD[lp], ldscr[:, lp * 512:(lp + 1) * 512], SB[:, 0:512]))
                    vchain(v, v.tensor_mul(YWD[lp], EVACD[lp], QB[:, 0:512]))
                    vchain(v, v.tensor_copy(xbfp[:, lp * 512:(lp + 1) * 512], YWD[lp]))
                VE(_)
                vmAp3[gi] = vm.n

                def _(s, nc4=nc4, et=et, gi=gi, lp=lp):
                    s.wait_ge(vm.h, vmAp3[gi])
                    s.wait_ge(shp[lp].h, shp[lp].n)
                    shp[lp].inc(s.dma_start(out=x3_dram[et * 128:(et + 1) * 128,
                                                        nc4 * 512:(nc4 + 1) * 512],
                                            in_=YWD[lp]), 16)
                SC(_)
                stX3f[gi] = (shp[lp].h, shp[lp].n)

                def _(g, nc4=nc4, et=et, gi=gi, lp=lp):
                    g.wait_ge(vm.h, vmAp3[gi])
                    g.wait_ge(sgp[lp].h, sgp[lp].n)
                    g.dma_start(out=x3bf_dram[et * 128:(et + 1) * 128,
                                              nc4 * 512:(nc4 + 1) * 512],
                                in_=xbfp[:, lp * 512:(lp + 1) * 512]).then_inc(sgp[lp].h, 16)
                GP(_)
                sgp[lp].n += 16
                stX3b[gi] = (sgp[lp].h, sgp[lp].n)
        ln3_sg = [(sg.h, sg.n), (sg2.h, sg2.n), (sh.h, sh.n), (sh2.h, sh2.n)]
        ln3_vm = vm.n
        p3_tm = tm.n
        vmS4 = {}
        smS4 = {}
        ldn4 = {}
        stY4 = {}
        stats4n = {}
        rows4_done = 0

        es3.close()
        ffscr = es.enter_context(nc.sbuf_tensor("d_ffscr", [128, 1024], BF16, side="right"))
        AH = ffscr[:, 0:512]
        BH = ffscr[:, 512:1024]
        h_sb = es.enter_context(nc.sbuf_tensor("d_h_sb", [128, 32 * 1024], BF16, side="right"))
        g_sb = es.enter_context(nc.sbuf_tensor("d_g_sb", [128, 32 * 1024], BF16, side="right"))
        xq_sb = es.enter_context(nc.sbuf_tensor("d_xq_sb", [128, 8 * 1024], BF16, side="right"))
        wfb = es.enter_context(nc.sbuf_tensor("d_wfb", [128, 2 * 2 * 8 * 128], BF16, side="right"))
        whb = es.enter_context(nc.sbuf_tensor("d_whb", [128, 2 * 8 * 128], BF16, side="right"))
        w3b = es.enter_context(nc.sbuf_tensor("d_w3b", [128, 2 * 4 * 512], BF16, side="right"))

        wf_used = {0: 0, 1: 0}
        lwf_tot = {}
        wh_used = {0: 0, 1: 0, 2: 0, 3: 0}
        lwh_tot = {0: 0, 1: 0, 2: 0, 3: 0}
        w3_used = {0: 0, 1: 0}
        hfree = {0: 0, 1: 0}
        gfree = {0: 0, 1: 0}

        s1_tm = {0: 0}
        lgx = {}
        for th_ in range(2):
            tb = th_ * 1024

            def _(g, th_=th_, tb=tb):
                if th_ == 0:
                    for _hh, _nn in ln3_sg:
                        g.wait_ge(_hh, _nn)
                else:
                    g.wait_ge(tm.h, s1_tm[0])
                g.wait_ge(lg.h, lg.n)
                for dt in range(8):
                    lg.inc(g.dma_start(out=xq_sb[:, dt * 1024:(dt + 1) * 1024],
                                       in_=x3bf_dram[dt * 128:(dt + 1) * 128, tb:tb + 1024]), 16)
            GP(_)
            lgx[th_] = lg.n

            for fo in range(32):
                fset = fo % 2
                for wi, WT in enumerate((w1T, w2T)):
                    def _(g, th_=th_, fo=fo, wi=wi, WT=WT, fset=fset):
                        if th_ * 32 + fo >= 2:
                            g.wait_ge(tm.h, wf_used[fset])
                        g.wait_ge(lw[wi].h, lw[wi].n)
                        for hx in range(2):
                            lw[wi].inc(g.dma_start(
                                out=wfb[:, (fset * 2 + wi) * 1024 + hx * 512:
                                    (fset * 2 + wi) * 1024 + (hx + 1) * 512],
                                in_=WT[:, fo * 1024 + hx * 512: fo * 1024 + (hx + 1) * 512]), 16)
                    GP(_)
                    lwf_tot[wi] = lw[wi].n

                def _(t, th_=th_, fo=fo, lwa=lwf_tot[0], lwb=lwf_tot[1], tb=tb, fset=fset):
                    t.wait_ge(lw[0].h, lwa)
                    t.wait_ge(lw[1].h, lwb)
                    if fo == 0:
                        t.wait_ge(lg.h, lgx[th_])
                        t.wait_ge(vm.h, vm.n)
                        t.wait_ge(sm.h, sm.n)
                    elif th_ * 32 + fo >= 2:
                        t.wait_ge(vm.h, hfree[fo % 2])
                    bsel = (fo % 2) * 4
                    ins = None
                    for wi in range(2):
                        for half in range(2):
                            bk = bsel + wi * 2 + half
                            for dt in range(8):
                                ins = t.matmul(
                                    ps[:, bk * 512:(bk + 1) * 512],
                                    wfb[:, (fset * 2 + wi) * 1024 + dt * 128:
                                        (fset * 2 + wi) * 1024 + (dt + 1) * 128],
                                    xq_sb[:, dt * 1024 + half * 512: dt * 1024 + (half + 1) * 512],
                                    start=(dt == 0), stop=(dt == 7))
                    tm.inc(ins)
                    wf_used[fset] = tm.n
                TE(_)

                for half in range(2):
                    def _(s, fo=fo, half=half):
                        s.wait_ge(tm.h, tm.n)
                        s.wait_ge(vm.h, vm.n)
                        bsel = (fo % 2) * 4
                        schain(s, s.activation(AH, ps[:, (bsel + half) * 512:(bsel + half + 1) * 512],
                                               AF.Silu, scale=S1C[:, fo:fo + 1]))
                    SC(_)

                    def _(v, fo=fo, half=half):
                        v.wait_ge(sm.h, sm.n)
                        bsel = (fo % 2) * 4
                        vchain(v, v.tensor_scalar(BH, ps[:, (bsel + 2 + half) * 512:(bsel + 3 + half) * 512],
                                                  S2C[:, fo:fo + 1], None, ALU.mult))
                        vchain(v, v.tensor_mul(h_sb[:, fo * 1024 + half * 512: fo * 1024 + (half + 1) * 512],
                                               AH, BH))
                        hfree[fo % 2] = vm.n
                    VE(_)

            s1_tm[0] = tm.n
            for si, (HT, SHC, SRC, DST) in enumerate(
                    ((h1T, SH1C, h_sb, g_sb), (h2T, SH2C, g_sb, h_sb))):
                for o2 in range(32):
                    for kih in range(4):
                        g2k = ((th_ * 2 + si) * 32 + o2) * 4 + kih
                        p4 = g2k % 2
                        p2 = g2k % 2

                        def _(g, o2=o2, kih=kih, HT=HT, p2=p2, p4=p4, g2k=g2k):
                            if g2k >= 2:
                                g.wait_ge(tm.h, wh_used[p4])
                            g.wait_ge(lw[p2].h, lw[p2].n)
                            for hx in range(2):
                                lw[p2].inc(g.dma_start(
                                    out=whb[:, p4 * 1024 + hx * 512: p4 * 1024 + (hx + 1) * 512],
                                    in_=HT[:, (o2 * 4 + kih) * 1024 + hx * 512:
                                        (o2 * 4 + kih) * 1024 + (hx + 1) * 512]), 16)
                        GP(_)
                        lwh_tot[p4] = lw[p2].n

                        def _(t, si=si, o2=o2, kih=kih, lwh=lwh_tot[p4], th_=th_, p2=p2, p4=p4):
                            t.wait_ge(lw[p2].h, lwh)
                            if kih == 0:
                                if th_ * 64 + si * 32 + o2 >= 2:
                                    t.wait_ge(sm.h, gfree[o2 % 2])
                                if o2 == 0:
                                    t.wait_ge(vm.h, vm.n)
                                    t.wait_ge(sm.h, sm.n)
                            bb = (o2 % 2) * 2
                            ins = None
                            for ki8 in range(8):
                                ki = kih * 8 + ki8
                                for half in range(2):
                                    ins = t.matmul(
                                        ps[:, (bb + half) * 512:(bb + half + 1) * 512],
                                        whb[:, p4 * 1024 + ki8 * 128: p4 * 1024 + (ki8 + 1) * 128],
                                        SRC[:, ki * 1024 + half * 512: ki * 1024 + (half + 1) * 512],
                                        start=(ki == 0), stop=(ki == 31))
                            tm.inc(ins)
                            wh_used[p4] = tm.n
                        TE(_)

                    def _(s, si=si, o2=o2, SHC=SHC, DST=DST):
                        s.wait_ge(tm.h, tm.n)
                        bb = (o2 % 2) * 2
                        sm.inc(s.activation(DST[:, o2 * 1024: o2 * 1024 + 512],
                                            ps[:, bb * 512:(bb + 1) * 512],
                                            AF.Gelu, scale=SHC[:, o2:o2 + 1]))
                        schain(s, s.activation(DST[:, o2 * 1024 + 512: o2 * 1024 + 1024],
                                               ps[:, (bb + 1) * 512:(bb + 2) * 512],
                                               AF.Gelu, scale=SHC[:, o2:o2 + 1]))
                        gfree[o2 % 2] = sm.n
                    SC(_)

            for hh in range(2):
                cb = tb + hh * 512
                tsl = th_ * 2 + hh
                for dog in range(2):
                    for kiq in range(8):
                        w3i = ((th_ * 2 + hh) * 2 + dog) * 8 + kiq
                        s3 = w3i % 2

                        def _(g, dog=dog, kiq=kiq, s3=s3, w3i=w3i):
                            if w3i >= 2:
                                g.wait_ge(tm.h, w3_used[s3])
                            g.wait_ge(lg.h, lg.n)
                            for hx in range(2):
                                lg.inc(g.dma_start(
                                    out=w3b[:, s3 * 2048 + hx * 1024: s3 * 2048 + (hx + 1) * 1024],
                                    in_=w3T[:, (dog * 8 + kiq) * 2048 + hx * 1024:
                                        (dog * 8 + kiq) * 2048 + (hx + 1) * 1024]), 16)
                        GP(_)
                        lg_w3 = lg.n

                        def _(t, dog=dog, kiq=kiq, lg_w3=lg_w3, cb=cb, tb=tb, s3=s3):
                            t.wait_ge(lg.h, lg_w3)
                            if kiq == 0:
                                t.wait_ge(sm.h, sm.n)
                                t.wait_ge(vm.h, vm.n)
                            ins = None
                            for ki4 in range(4):
                                kig = kiq * 4 + ki4
                                for do4 in range(4):
                                    ins = t.matmul(bank(do4),
                                                   w3b[:, s3 * 2048 + ki4 * 512 + do4 * 128:
                                                       s3 * 2048 + ki4 * 512 + (do4 + 1) * 128],
                                                   h_sb[:, kig * 1024 + (cb - tb): kig * 1024 + (cb - tb) + 512],
                                                   start=(kig == 0), stop=(kig == 31))
                            tm.inc(ins)
                            w3_used[s3] = tm.n
                        TE(_)

                    s4_tm = tm.n
                    for do4 in range(4):
                        do = dog * 4 + do4
                        di = ((th_ * 2 + hh) * 2 + dog) * 4 + do4
                        lp = di % 2

                        def _(sync, do=do, cb=cb, di=di, lp=lp):
                            sync.wait_ge(vm.h, vmS4.get(di - 2, ln3_vm))
                            sync.wait_ge(*stX3f[(cb // 512) * 8 + do])
                            sync.wait_ge(ld.h, ld.n)
                            ld.inc(sync.dma_start(out=ldscr[:, lp * 512:(lp + 1) * 512],
                                                  in_=x3_dram[do * 128:(do + 1) * 128, cb: cb + 512]), 16)
                        SY(_)
                        ldn4[di] = ld.n

                        def _(v, do=do, do4=do4, di=di, lp=lp, s4_tm=s4_tm):
                            v.wait_ge(ld.h, ldn4[di])
                            v.wait_ge(tm.h, max(s4_tm, stats4n.get(di - 2, p3_tm)))
                            for _hh, _nn in ([stY4[di - 2]] if di >= 2 else ln3_sg):
                                v.wait_ge(_hh, _nn)
                            vchain(v, v.scalar_tensor_tensor(YWD[lp], bank(do4), S3C[:, do:do + 1],
                                                             ldscr[:, lp * 512:(lp + 1) * 512],
                                                             ALU.mult, ALU.add))
                            vchain(v, v.tensor_copy(ybf2[:, lp * 512:(lp + 1) * 512], YWD[lp]))
                        VE(_)
                        vmS4[di] = vm.n

                        def _(s, do=do, di=di, lp=lp):
                            s.wait_ge(vm.h, vmS4[di])
                            s.wait_ge(tm.h, stats4n.get(di - 2, p3_tm))
                            sm.inc(s.activation(sqscr[:, lp * 512:(lp + 1) * 512],
                                                ybf2[:, lp * 512:(lp + 1) * 512], AF.Square))
                        SC(_)
                        smS4[di] = sm.n

                        def _(t, do=do, di=di, lp=lp, rd=rows4_done):
                            t.wait_ge(sm.h, smS4[di])
                            if do == 0:
                                t.wait_ge(vm.h, rd)
                            t.matmul(SB[0:1, 0:512], onesb[:], ybf2[:, lp * 512:(lp + 1) * 512],
                                     start=(do == 0), stop=(do == 7))
                            ins = t.matmul(QB[0:1, 0:512], onesb[:], sqscr[:, lp * 512:(lp + 1) * 512],
                                           start=(do == 0), stop=(do == 7))
                            tm.inc(ins)
                        TE(_)
                        stats4n[di] = tm.n

                        def _(s, do=do, cb=cb, di=di, lp=lp):
                            s.wait_ge(vm.h, vmS4[di])
                            s.wait_ge(shp[lp].h, shp[lp].n)
                            shp[lp].inc(s.dma_start(out=y4_dram[do * 128:(do + 1) * 128, cb: cb + 512],
                                                    in_=YWD[lp]), 16)
                        SC(_)
                        stY4[di] = (shp[lp].h, shp[lp].n)

                def _(v, tsl=tsl):
                    v.wait_ge(tm.h, tm.n)
                    mu = rows[0:1, tsl * 512:(tsl + 1) * 512]
                    vchain(v, v.tensor_scalar(mu, SB[0:1, 0:512], 1.0 / 1024.0, None, ALU.mult))
                    vchain(v, v.tensor_scalar(rscr[0:1, 0:512], QB[0:1, 0:512], 1.0 / 1024.0, None, ALU.mult))
                    vchain(v, v.tensor_mul(rscr[0:1, 512:1024], mu, mu))
                    vchain(v, v.tensor_sub(rscr[0:1, 0:512], rscr[0:1, 0:512], rscr[0:1, 512:1024]))
                VE(_)

                def _(s):
                    s.wait_ge(vm.h, vm.n)
                    schain(s, s.activation(rscr[0:1, 512:1024], rscr[0:1, 0:512], AF.Sqrt,
                                           bias=epsc[0:1, 0:1]))
                SC(_)

                def _(v, tsl=tsl):
                    v.wait_ge(sm.h, sm.n)
                    vchain(v, v.reciprocal(rows[0:1, 2048 + tsl * 512: 2048 + (tsl + 1) * 512],
                                           rscr[0:1, 512:1024]))
                VE(_)
                rows4_done = vm.n

        vmP4 = {}
        ldnP4 = {}
        stP4 = {}
        stp4 = [st, st2]
        s4_vm_end = vm.n
        s4_sg_end = [(sh.h, sh.n), (sh2.h, sh2.n)]
        for nc4 in range(4):
            def _(t, nc4=nc4):
                t.wait_ge(vm.h, vm.n)
                t.matmul(SB[:, 0:512], onesf[:], rows[0:1, nc4 * 512:(nc4 + 1) * 512],
                         start=True, stop=True)
                ins = t.matmul(QB[:, 0:512], onesf[:],
                               rows[0:1, 2048 + nc4 * 512: 2048 + (nc4 + 1) * 512],
                               start=True, stop=True)
                tm.inc(ins)
            TE(_)
            tmUB4 = tm.n

            for et in range(8):
                gi = nc4 * 8 + et
                lp = gi % 2

                def _(sync, nc4=nc4, et=et, gi=gi, lp=lp):
                    sync.wait_ge(vm.h, vmP4.get(gi - 2, s4_vm_end))
                    sync.wait_ge(*stY4[nc4 * 8 + et])
                    sync.wait_ge(ld.h, ld.n)
                    ld.inc(sync.dma_start(out=ldscr[:, lp * 512:(lp + 1) * 512],
                                          in_=y4_dram[et * 128:(et + 1) * 128,
                                                      nc4 * 512:(nc4 + 1) * 512]), 16)
                SY(_)
                ldnP4[gi] = ld.n

                def _(v, nc4=nc4, et=et, gi=gi, lp=lp, tmUB4=tmUB4):
                    v.wait_ge(ld.h, ldnP4[gi])
                    v.wait_ge(tm.h, tmUB4)
                    if gi < 2:
                        for _hh, _nn in s4_sg_end:
                            v.wait_ge(_hh, _nn)
                    else:
                        v.wait_ge(*stP4[gi - 2])
                    vchain(v, v.tensor_sub(EVACD[lp], ldscr[:, lp * 512:(lp + 1) * 512], SB[:, 0:512]))
                    vchain(v, v.tensor_mul(YWD[lp], EVACD[lp], QB[:, 0:512]))
                VE(_)
                vmP4[gi] = vm.n

                def _(g, nc4=nc4, et=et, gi=gi, lp=lp):
                    g.wait_ge(vm.h, vmP4[gi])
                    g.wait_ge(sgp[lp].h, sgp[lp].n)
                    sgp[lp].inc(g.dma_start(out=x4T[et * 128:(et + 1) * 128,
                                                    nc4 * 512:(nc4 + 1) * 512],
                                            in_=YWD[lp]), 16)
                    stP4[gi] = (sgp[lp].h, sgp[lp].n)
                GP(_)

        def _(sync):
            sync.wait_ge(sg.h, sg.n)
            sync.wait_ge(sg2.h, sg2.n)
        SY(_)

    es.close()
    return EQ


def build_launch_D():
    nc = bass.Bass("TRN2", target_bir_lowering=False)
    io = {}
    _declare_io(nc, io, D_INPUTS + [("kf_in", [D, T], "BF16"), ("vt_in", [T, NH * 65], "BF16")],
                D_OUTPUTS)
    io["kf_ap"] = lambda rows0, kc: io["kf_in"][rows0:rows0 + 128, kc * 512:(kc + 1) * 512]
    io["vt_ap"] = lambda kc, hg: io["vt_in"].rearrange("(a p) n -> p a n", p=128)[
        :, kc * 4:(kc + 1) * 4, hg * 260:(hg + 1) * 260]
    io["kdiag_ap"] = lambda hg, ct: io["kdiag"][(2 * hg + ct) * 128:(2 * hg + ct + 1) * 128, :]
    io["vdiag_ap"] = lambda hg: io["vdiag"].rearrange("(a p) n -> p a n", p=128)[
        :, :, hg * 260:(hg + 1) * 260]
    EQ = emit_D(nc, io)
    _flush(nc, EQ)
    return nc



RG_BATCH = [[0, 1, 2, 3], [4, 5, 6, 7]]


def _inject_waits(EQ, waits):
    for q in EQ.values():
        q.calls[0:0] = [("wait_ge", (h, n), {}, None) for h, n in waits]


def build_fused():
    nc = bass.Bass("TRN2", target_bir_lowering=False)
    es = ExitStack()
    io = {}
    _declare_io(nc, io, A_INPUTS + [("td2", [128, 2], "F32"), ("tf2", [128, 2], "F32"),
                                    ("gat_idx", [128, 32], "I32")]
                + [x for x in C_INPUTS if x[0] not in ("xT", "sr_in", "wkv_in")]
                + [x for x in D_INPUTS if x[0] not in ("q_in", "kdiag", "vdiag", "x2T_in")],
                D_OUTPUTS)

    kvin = [nc.dram_tensor(f"kvin_{i}", [2048, 1024], F32, kind="Internal") for i in range(2)]
    kvag = [nc.dram_tensor(f"kvag_{i}", [4 * 2048, 1024], F32, kind="Internal") for i in range(2)]
    sr_dram = nc.dram_tensor("sr_dram", [D, TOK], F32, kind="Internal")
    x_own = nc.dram_tensor("x_own", [D, TOK], F32, kind="Internal")
    wkvo = [nc.dram_tensor(f"wkvo_{i}", [1024, 1024], F32, kind="Internal") for i in range(2)]
    wkvag = [nc.dram_tensor(f"wkvag_{i}", [4 * 1024, 1024], F32, kind="Internal") for i in range(2)]
    q_dram = nc.dram_tensor("q_dram", [D, TOK], BF16, kind="Internal")
    kqo = [nc.dram_tensor(f"kqo_{i}", [D // 2, TOK], BF16, kind="Internal") for i in range(2)]
    vto = [nc.dram_tensor(f"vto_{i}", [TOK, 520], BF16, kind="Internal") for i in range(2)]
    x2_dram = nc.dram_tensor("x2_dram", [D, TOK], F32, kind="Internal")
    kq_all = [nc.dram_tensor(f"kq_all_{i}", [4 * D // 2, TOK], BF16, kind="Internal") for i in range(2)]
    vt_all = [nc.dram_tensor(f"vt_all_{i}", [4 * TOK, 520], BF16, kind="Internal") for i in range(2)]

    cc = Sem(nc, es, "cc")
    gidx = es.enter_context(nc.sbuf_tensor("gidx", [128, 32], I32))

    def phase_sems(mark):
        return [(s.h, s.n) for s in Sem._all[mark:] if s.n > 0]

    ioA = {n: io[n] for n, _, _ in A_INPUTS}

    def kv_dst(pi, et):
        assert pi == 2
        return sr_dram.ap()[et * 128:(et + 1) * 128, :]

    def kv_sth(pi, et, half):
        j, sub = et // 2, (et % 2) * 128
        base = 512 * j + (0 if pi == 0 else 256) + sub
        return kvin[half].ap()[base:base + 128, :]
    ioA["kv_dst"] = kv_dst
    ioA["kv_sth"] = kv_sth
    markA = len(Sem._all)
    EQ_A, stA = emit_A(nc, ioA)
    gq = EQ_A["gpsimd"]
    for dst, src in ((gidx[:, :], io["gat_idx"]),
                     (x_own.ap()[:, 0:1024], io["xT"][:, 1:1025]),
                     (x_own.ap()[:, 1024:2048], io["xT"][:, 1026:2050])):
        fi = FakeIns()
        gq.calls.append(("dma_start", (), {"out": dst, "in_": src}, fi))
        fi.incs.append((cc.h, 16))
        cc.n += 16
    gq.wait_ge(stA.h, 512)
    for half in range(2):
        fi3 = FakeIns()
        gq.calls.append(("collective_compute", ("AllGather", ALU.bypass),
                         {"replica_groups": RG_BATCH, "ins": [kvin[half].ap()],
                          "outs": [kvag[half].ap()]}, fi3))
        fi3.incs.append((cc.h, 1))
        cc.n += 1
        if half == 0:
            cc_kvh = cc.n
    cc_kvt = cc.n
    wA = phase_sems(markA) + [(cc.h, cc_kvh)]

    ioB = {"td2": io["td2"], "tf2": io["tf2"]}

    def kv_load(g, dst, kv, dt2, c, half):
        g_ = 2 * c + half
        s, lc = _g2src(g_)
        col = s * 4 + kv * 2 + dt2
        if lc:
            g.wait_ge(cc.h, cc_kvt)
        return g.indirect_dma_start(dst, None, kvag[lc // 1024].ap(),
                                    IndirectOffsetOnAxis(gidx[:, col:col + 1], 0))

    def wkv_dst(dt2, c, half):
        g_ = 2 * c + half
        return wkvo[g_ // 4].ap()[256 * (g_ % 4) + dt2 * 128: 256 * (g_ % 4) + dt2 * 128 + 128, :]
    ioB["kv_load"] = kv_load
    ioB["wkv_dst"] = wkv_dst
    ccs = {}

    def wkv_ag_hook(g):
        ins = g.collective_compute("AllGather", ALU.bypass,
                                   replica_groups=RG_BATCH, ins=[wkvo[0].ap()],
                                   outs=[wkvag[0].ap()])
        cc.inc(ins, 1)
        ccs["a"] = cc.n
    ioB["wkv_ag_hook"] = wkv_ag_hook
    markB = len(Sem._all)
    EQ_B, stB = emit_B(nc, ioB)
    _inject_waits(EQ_B, wA)
    gq = EQ_B["gpsimd"]
    gq.wait_ge(stB.h, stB.n)
    fi = FakeIns()
    gq.calls.append(("collective_compute", ("AllGather", ALU.bypass),
                     {"replica_groups": RG_BATCH, "ins": [wkvo[1].ap()],
                      "outs": [wkvag[1].ap()]}, fi))
    fi.incs.append((cc.h, 1))
    cc.n += 1
    ccs["b"] = cc.n
    wB = phase_sems(markB) + [(cc.h, ccs["a"])]

    ioC = {n: io[n] for n, _, _ in C_INPUTS if n in io}
    ioC["xT"] = x_own.ap()
    ioC["sr_in"] = sr_dram.ap()
    ioC["wkv_in"] = wkvag[0].ap()

    def wkv_load(g, dst, ft, th):
        g.wait_ge(cc.h, ccs["a"] if th == 0 else ccs["b"])
        return g.indirect_dma_start(dst, None, wkvag[th].ap(),
                                    IndirectOffsetOnAxis(gidx[:, 16 + ft * 2 + th:17 + ft * 2 + th], 0))
    ioC["wkv_load"] = wkv_load
    ioC["x2T"] = x2_dram.ap()
    ioC["q_fm"] = q_dram.ap()
    ioC["k_fm"] = kqo[0].ap()
    ioC["v_tm"] = vto[0].ap()
    ioC["kq_st"] = lambda et: kqo[et // 4].ap()[(et % 4) * 128:(et % 4 + 1) * 128, :]
    ioC["vt_st"] = lambda blk, half: vto[half].ap()[blk * 128:(blk + 1) * 128, :]
    markC = len(Sem._all)
    EQ_C = emit_C(nc, ioC)
    _inject_waits(EQ_C, wB)
    gq = EQ_C["gpsimd"]
    for h, n in phase_sems(markC):
        gq.wait_ge(h, n)
    for half in (0, 1):
        for src_t, dst_t in ((kqo[half], kq_all[half]), (vto[half], vt_all[half])):
            fi = FakeIns()
            gq.calls.append(("collective_compute", ("AllGather", ALU.bypass),
                             {"replica_groups": RG_BATCH, "ins": [src_t.ap()],
                              "outs": [dst_t.ap()]}, fi))
            fi.incs.append((cc.h, 1))
            cc.n += 1
        if half == 0:
            cc_half = cc.n
    wC_nocc = phase_sems(markC)
    wC = wC_nocc + [(cc.h, cc_half)]
    cc_full = cc.n

    ioD = {n: io[n] for n, _, _ in D_INPUTS if n in io}
    ioD["q_in"] = q_dram.ap()
    ioD["kdiag"] = kqo[0].ap()
    ioD["vdiag"] = vto[0].ap()
    ioD["x2T_in"] = x2_dram.ap()
    ioD["x4T"] = io["x4T"]

    def kf_ap(rows0, kc):
        g_ = kc // 2
        s, lc0 = _g2src(g_)
        lc = lc0 + (kc % 2) * 512
        hv, r = divmod(rows0, 512)
        return kq_all[hv].ap()[512 * s + r: 512 * s + r + 128, lc:lc + 512]

    def vt_ap(kc, hg):
        g_ = kc // 2
        s, lc0 = _g2src(g_)
        r0 = 2048 * s + lc0 + (kc % 2) * 512
        return vt_all[hg // 2].ap().rearrange("(a p) n -> p a n", p=128)[
            :, r0 // 128:(r0 // 128) + 4, (hg % 2) * 260:(hg % 2 + 1) * 260]
    ioD["kf_ap"] = kf_ap
    ioD["vt_ap"] = vt_ap
    ioD["kdiag_ap"] = lambda hg, ct: kqo[(2 * hg + ct) // 4].ap()[
        ((2 * hg + ct) % 4) * 128:((2 * hg + ct) % 4 + 1) * 128, :]
    ioD["vdiag_ap"] = lambda hg: vto[hg // 2].ap().rearrange("(a p) n -> p a n", p=128)[
        :, :, (hg % 2) * 260:(hg % 2 + 1) * 260]
    ioD["agw"] = lambda sync: sync.wait_ge(cc.h, cc_full)
    EQ_D = emit_D(nc, ioD)
    _inject_waits({k: v for k, v in EQ_D.items() if k != "gpsimd"}, wC)
    _inject_waits({"gpsimd": EQ_D["gpsimd"]}, wC_nocc)

    from concourse.aot_env import allow_ici_all_to_all
    with allow_ici_all_to_all():
        _flush(nc, [EQ_A, EQ_B, EQ_C, EQ_D])
    es.close()
    return nc


def _cc(a):
    return np.ascontiguousarray(a)


def _s1w(w):
    wT = np.asarray(w, np.float32).T
    return _cc(wT.reshape(8, 128, 32, 128).transpose(1, 2, 0, 3).reshape(128, 8 * FF))


def _s23w(w):
    wT = np.asarray(w, np.float32).T
    return _cc(wT.reshape(4, 8, 128, 32, 128).transpose(2, 3, 0, 1, 4).reshape(128, 32 * FF))


def _s4w(w):
    wT = np.asarray(w, np.float32).T
    return _cc(wT.reshape(8, 4, 128, 2, 512).transpose(2, 3, 0, 1, 4).reshape(128, 8 * FF))


def fused_maps(inp):
    x = np.asarray(inp["L0_attn_x"], dtype=np.float32)
    xT = _cc(x.transpose(0, 2, 1))
    tmk = _cc(np.asarray(inp["L0_attn_time_mix_k"], np.float32).reshape(8, 128).T)
    tmv = _cc(np.asarray(inp["L0_attn_time_mix_v"], np.float32).reshape(8, 128).T)
    tmr = _cc(np.asarray(inp["L0_attn_time_mix_r"], np.float32).reshape(8, 128).T)
    WkT = _cc(np.asarray(inp["L0_attn_Wk"], np.float32).T)
    WvT = _cc(np.asarray(inp["L0_attn_Wv"], np.float32).T)
    WrT = _cc(np.asarray(inp["L0_attn_Wr"], np.float32).T)
    td = np.asarray(inp["L0_attn_time_decay"], np.float32)
    tf = np.asarray(inp["L0_attn_time_first"], np.float32)

    c32 = lambda v: _cc(np.asarray(v, np.float32).reshape(32, 128).T)
    jp = np.arange(128)
    shared = {
        "tmk": tmk, "tmv": tmv, "tmr": tmr, "WkT": WkT, "WvT": WvT, "WrT": WrT,
        "WoT": _cc(np.asarray(inp["L0_attn_Wo"], np.float32).T),
        "W1": _cc(np.asarray(inp["L0_ffn_W1"], np.float32)),
        "b1": c32(inp["L0_ffn_b1"]),
        "th": c32(inp["L0_ffn_quantum_phases"]),
        "qwr": c32(inp["L0_ffn_quantum_weights_real"]),
        "qwi": c32(inp["L0_ffn_quantum_weights_imag"]),
        "W2": _cc(np.asarray(inp["L0_ffn_W2"], np.float32)),
        "b2": _cc(np.asarray(inp["L0_ffn_b2"], np.float32).reshape(8, 128).T),
        "Wq": _cc(np.asarray(inp["L1_attn_Wq"], np.float32)),
        "Wk1": _cc(np.asarray(inp["L1_attn_Wk"], np.float32)),
        "Wv1": _cc(np.asarray(inp["L1_attn_Wv"], np.float32)),
        "ident_in": np.eye(128, dtype=np.float32),
        "Wo1": _cc(np.asarray(inp["L1_attn_Wo"], np.float32)),
        "w1T": _s1w(inp["L1_ffn_w1_w"]),
        "w2T": _s1w(inp["L1_ffn_w2_w"]),
        "h1T": _s23w(inp["L1_ffn_h1_w"]),
        "h2T": _s23w(inp["L1_ffn_h2_w"]),
        "w3T": _s4w(inp["L1_ffn_w3_w"]),
        "s1c_in": c32(inp["L1_ffn_w1_s"]), "s2c_in": c32(inp["L1_ffn_w2_s"]),
        "sh1_in": c32(inp["L1_ffn_h1_s"]), "sh2_in": c32(inp["L1_ffn_h2_s"]),
        "s3c_in": _cc(np.asarray(inp["L1_ffn_w3_s"], np.float32).reshape(8, 128).T),
        "dcol_in": np.stack([SLOPES[h] * (jp - 127) + DIAG_OFF - MS for h in range(NH)],
                            axis=1).astype(np.float32),
        "fvb_in": np.stack([-SLOPES[h] * jp - DIAG_OFF for h in range(NH)],
                           axis=1).astype(np.float32),
    }

    maps = []
    for c in range(NCORES):
        b, h0, t0 = core_chunks(c)
        j = c % 4
        m = dict(shared)
        xin = np.zeros((D, 2050), dtype=np.float32)
        if h0 > 0:
            xin[:, 0] = xT[b, :, h0 - 1]
        xin[:, 1:1025] = xT[b, :, h0:h0 + 1024]
        xin[:, 1025] = xT[b, :, t0 - 1]
        xin[:, 1026:2050] = xT[b, :, t0:t0 + 1024]
        m["xT"] = xin
        fsl = slice(256 * j, 256 * j + 256)
        m["td2"] = _cc(td[fsl].reshape(2, 128).T)
        m["tf2"] = _cc(tf[fsl].reshape(2, 128).T)
        gi = np.zeros((128, 32), dtype=np.int32)
        for s in range(4):
            for kv in range(2):
                for dt2 in range(2):
                    gi[:, s * 4 + kv * 2 + dt2] = 2048 * s + 512 * j + 256 * kv + 128 * dt2 + jp
        for ft in range(8):
            s2, dt2 = divmod(ft, 2)
            for th in range(2):
                g_ = j if th == 0 else 7 - j
                gi[:, 16 + ft * 2 + th] = 1024 * s2 + 256 * (g_ % 4) + 128 * dt2 + jp
        m["gat_idx"] = gi
        qgpos = qg_positions(c)
        bias_tbl = np.zeros((128, NSLOT * NH), dtype=np.float32)
        for qg in range(4):
            q0 = qgpos[qg]
            for slot in range(SWEEPS[qg] // 128):
                j0 = slot * 128
                for h in range(NH):
                    col = h * NSLOT + QGOFF[qg] + slot
                    if j0 < q0:
                        bias_tbl[:, col] = SLOPES[h] * (j0 + jp - q0) - MS
                    else:
                        bias_tbl[:, col] = -120.0
        m["bias_tbl"] = bias_tbl
        maps.append(m)
    return maps


def kernel(**inp):
    maps = fused_maps(inp)
    res = run_bass_kernel_spmd(build_fused(), maps, core_ids=list(range(NCORES))).results
    out = np.zeros((B, T, D), dtype=np.float32)
    for c in range(NCORES):
        b, h0, t0 = core_chunks(c)
        a = np.asarray(res[c]["x4T"], dtype=np.float32)
        out[b, h0:h0 + 1024] = a[:, 0:1024].T
        out[b, t0:t0 + 1024] = a[:, 1024:2048].T
    return out


def _kernel_multilaunch_dead(**inp):
    resA = None

    def reasm_fm(res, key, dtype=np.float32):
        out = np.zeros((B, D, T), dtype=dtype)
        for c in range(NCORES):
            b, h0, t0 = core_chunks(c)
            a = np.asarray(res[c][key])
            out[b, :, h0:h0 + 1024] = a[:, 0:1024]
            out[b, :, t0:t0 + 1024] = a[:, 1024:2048]
        return out

    kfull = reasm_fm(resA, "k_out")
    vfull = reasm_fm(resA, "v_out")

    td = np.asarray(inp["L0_attn_time_decay"], np.float32)
    tf = np.asarray(inp["L0_attn_time_first"], np.float32)
    maps = []
    for c in range(NCORES):
        sl = slice(128 * c, 128 * (c + 1))
        maps.append({"k_in": _cc(kfull[:, sl, :]), "v_in": _cc(vfull[:, sl, :]),
                     "td": _cc(td[sl].reshape(128, 1)), "tf": _cc(tf[sl].reshape(128, 1))})
    resB = run_bass_kernel_spmd(build_launch_B(), maps, core_ids=list(range(NCORES))).results
    wkv_all = np.zeros((B, D, T), dtype=np.float32)
    for c in range(NCORES):
        wkv_all[:, 128 * c:128 * (c + 1), :] = np.asarray(resB[c]["wkv_out"])

    c32 = lambda v: _cc(np.asarray(v, np.float32).reshape(32, 128).T)
    sharedC = {
        "WoT": _cc(np.asarray(inp["L0_attn_Wo"], np.float32).T),
        "W1": _cc(np.asarray(inp["L0_ffn_W1"], np.float32)),
        "b1": c32(inp["L0_ffn_b1"]),
        "th": c32(inp["L0_ffn_quantum_phases"]),
        "qwr": c32(inp["L0_ffn_quantum_weights_real"]),
        "qwi": c32(inp["L0_ffn_quantum_weights_imag"]),
        "W2": _cc(np.asarray(inp["L0_ffn_W2"], np.float32)),
        "b2": _cc(np.asarray(inp["L0_ffn_b2"], np.float32).reshape(8, 128).T),
        "Wq": _cc(np.asarray(inp["L1_attn_Wq"], np.float32)),
        "Wk1": _cc(np.asarray(inp["L1_attn_Wk"], np.float32)),
        "Wv1": _cc(np.asarray(inp["L1_attn_Wv"], np.float32)),
        "ident_in": np.eye(128, dtype=np.float32),
    }
    maps = []
    for c in range(NCORES):
        b, h0, t0 = core_chunks(c)
        m = dict(sharedC)
        m["xT"] = _cc(np.concatenate([xT[b, :, h0:h0 + 1024], xT[b, :, t0:t0 + 1024]], axis=1))
        m["sr_in"] = np.asarray(resA[c]["sr_out"])
        m["wkv_in"] = _cc(np.concatenate([wkv_all[b, :, h0:h0 + 1024], wkv_all[b, :, t0:t0 + 1024]], axis=1))
        maps.append(m)
    resC = run_bass_kernel_spmd(build_launch_C(), maps, core_ids=list(range(NCORES))).results

    kf_b = np.zeros((B, D, T), dtype=BF16NP)
    vt_b = np.zeros((B, T, NH * 65), dtype=BF16NP)
    for c in range(NCORES):
        b, h0, t0 = core_chunks(c)
        kq = np.asarray(resC[c]["k_fm"])
        kf_b[b, :, h0:h0 + 1024] = kq[:, 0:1024]
        kf_b[b, :, t0:t0 + 1024] = kq[:, 1024:2048]
        vt = np.asarray(resC[c]["v_tm"])
        vt_b[b, h0:h0 + 1024] = vt[0:1024]
        vt_b[b, t0:t0 + 1024] = vt[1024:2048]

    jp = np.arange(128)
    sharedD = {
        "Wo1": _cc(np.asarray(inp["L1_attn_Wo"], np.float32)),
        "w1T": _s1w(inp["L1_ffn_w1_w"]),
        "w2T": _s1w(inp["L1_ffn_w2_w"]),
        "h1T": _s23w(inp["L1_ffn_h1_w"]),
        "h2T": _s23w(inp["L1_ffn_h2_w"]),
        "w3T": _s4w(inp["L1_ffn_w3_w"]),
        "s1c_in": c32(inp["L1_ffn_w1_s"]), "s2c_in": c32(inp["L1_ffn_w2_s"]),
        "sh1_in": c32(inp["L1_ffn_h1_s"]), "sh2_in": c32(inp["L1_ffn_h2_s"]),
        "s3c_in": _cc(np.asarray(inp["L1_ffn_w3_s"], np.float32).reshape(8, 128).T),
        "dcol_in": np.stack([SLOPES[h] * (jp - 127) + DIAG_OFF - MS for h in range(NH)],
                            axis=1).astype(np.float32),
        "fvb_in": np.stack([-SLOPES[h] * jp - DIAG_OFF for h in range(NH)],
                           axis=1).astype(np.float32),
    }
    maps = []
    for c in range(NCORES):
        b, h0, t0 = core_chunks(c)
        m = dict(sharedD)
        m["q_in"] = np.asarray(resC[c]["q_fm"])
        m["kf_in"] = kf_b[b]
        m["vt_in"] = vt_b[b]
        m["kdiag"] = np.asarray(resC[c]["k_fm"])
        m["vdiag"] = np.asarray(resC[c]["v_tm"])
        m["x2T_in"] = np.asarray(resC[c]["x2T"])
        qgpos = qg_positions(c)
        bias_tbl = np.zeros((128, NSLOT * NH), dtype=np.float32)
        for qg in range(4):
            q0 = qgpos[qg]
            for slot in range(SWEEPS[qg] // 128):
                j0 = slot * 128
                for h in range(NH):
                    col = h * NSLOT + QGOFF[qg] + slot
                    if j0 < q0:
                        bias_tbl[:, col] = SLOPES[h] * (j0 + jp - q0) - MS
                    else:
                        bias_tbl[:, col] = -120.0
        m["bias_tbl"] = bias_tbl
        maps.append(m)
    resD = run_bass_kernel_spmd(build_launch_D(), maps, core_ids=list(range(NCORES))).results

    out = np.zeros((B, T, D), dtype=np.float32)
    for c in range(NCORES):
        b, h0, t0 = core_chunks(c)
        a = np.asarray(resD[c]["x4T"], dtype=np.float32)
        out[b, h0:h0 + 1024] = a[:, 0:1024].T
        out[b, t0:t0 + 1024] = a[:, 1024:2048].T
    return out
